# version 04
# baseline (speedup 1.0000x reference)
import numpy as np
import ml_dtypes
import concourse.bass as bass
import concourse.mybir as mybir
from concourse.bass_utils import run_bass_kernel_spmd
from concourse.ap import AP
from contextlib import ExitStack

F32 = mybir.dt.float32
BF16 = mybir.dt.bfloat16
I32 = mybir.dt.int32
AF = mybir.ActivationFunctionType
ALU = mybir.AluOpType
AX = mybir.AxisListType

DM = 4096
DFF = 8192
NH = 32
NKV = 8
DH = 128
NKC = DM // 128
EPS_LN = 1e-5
EPS_RMS = 1e-5
MAXPOLL = 3_000_000

ENG_ATTR = {"pe": "tensor", "dve": "vector", "act": "scalar",
            "pool": "gpsimd", "sp": "sync"}
ENG_SEM = {"pe": "mpe", "dve": "mdv", "act": "mac", "pool": "mpo", "sp": "msp"}
DMA_SEM = {"pool": "qg", "sp": "qs", "act": "qa"}


class Ctx:
    def __init__(self, nc, block, es):
        self.nc, self.block, self.es = nc, block, es
        self.sems = {}

    def sem(self, name):
        if name not in self.sems:
            self.sems[name] = [self.es.enter_context(self.nc.semaphore(name)), 0]
        return self.sems[name]

    def inc(self, name, instr, n=1):
        s = self.sem(name)
        instr.then_inc(s[0], n)
        s[1] += n
        return s[1]

    def val(self, name):
        return self.sem(name)[1]

    def handle(self, name):
        return self.sem(name)[0]


class Chain:

    def __init__(self, ctx, *deps):
        self.ctx = ctx
        self.prev = [d for d in deps if d is not None]

    def op(self, eng, fn, dma=False, also=()):
        ctx = self.ctx
        prev = self.prev + [a for a in also if a is not None]
        sem = DMA_SEM[eng] if dma else ENG_SEM[eng]
        ser = ctx.val(sem) if dma else None
        dec = getattr(ctx.block, ENG_ATTR[eng])
        out = {}

        def body(e):
            if ser:
                e.wait_ge(ctx.handle(sem), ser)
            for (snm, sval) in prev:
                e.wait_ge(ctx.handle(snm), sval)
            out["v"] = ctx.inc(sem, fn(e), 16 if dma else 1)
        dec(body)
        self.prev = [(sem, out["v"])]
        return self.prev[0]

    def dep(self):
        return self.prev[0]


def build(T=2048, taps=(), upto="all"):
    nT = T // 128
    nS = T // 512
    nGT = 4

    nc = bass.Bass("TRN2", target_bir_lowering=False)

    ins = {}
    for nm, shp in [("x", [T, DM]),
                    ("a_lam", [64, 2]), ("a_p", [64, 2]), ("a_q", [64, 2]),
                    ("a_vc", [64, 128]), ("a_ct", [64, 2]), ("a_bp", [64, 1]),
                    ("a_d", [1, 1]), ("a_ls", [1, 1]),
                    ("wq", [DM, DM]), ("wk", [1024, DM]), ("wv", [1024, DM]),
                    ("wo", [DM, DM]), ("wg", [DM, DM]), ("win", [NH, DM]),
                    ("dtb", [NH, 1]), ("alb", [NH, 1]), ("gnw", [DH, 1]),
                    ("w12", [DFF, DM]), ("b12", [128, 64]), ("w3", [DM, DM]),
                    ("b3", [1, DM]),
                    ("c_ones", [1, 128]), ("c_onescol", [128, 1]),
                    ("c_id32", [128, 128]), ("c_rev", [128, 128]),
                    ("c_maskneg", [128, 128]), ("c_prot", [128, 128]),
                    ("c_omr", [64, 64]), ("c_omi", [64, 64]),
                    ("c_fr", [64, 64]), ("c_fi", [64, 64]), ("c_fni", [64, 64]),
                    ("c_twr", [64, 64]), ("c_twi", [64, 64]),
                    ("c_cosfm", [128, T]), ("c_sinfm", [128, T]),
                    ("c_costm", [128, T]), ("c_sintm", [128, T]),
                    ("c_mask", [128, 1]), ("c_bias", [128, 4])]:
        ins[nm] = nc.dram_tensor(nm, shp, F32, kind="ExternalInput").ap()
    ins["c_magic"] = nc.dram_tensor("c_magic", [1, 4], I32, kind="ExternalInput").ap()
    X_in = ins["x"]
    Y_out = nc.dram_tensor("y", [T, DM], F32, kind="ExternalOutput").ap()

    WSPEC = [("w12", DFF), ("w3", DM), ("wq", DM), ("wk", 1024),
             ("wv", 1024), ("wg", DM), ("wo", DM)]
    WM = dict(WSPEC)
    wbf = {nm: nc.dram_tensor(f"{nm}_bf", [M, DM], BF16, kind="Internal").ap()
           for nm, M in WSPEC}
    wtl = {nm: nc.dram_tensor(f"{nm}_t", [DM, M], BF16, kind="Internal").ap()
           for nm, M in WSPEC}
    Zbuf = nc.dram_tensor("zbuf", [1, 4352], BF16, kind="Internal").ap()
    resid_a = nc.dram_tensor("resid_a", [T, DM], F32, kind="Internal").ap()
    resid_b = nc.dram_tensor("resid_b", [T, DM], F32, kind="Internal").ap()
    h_scr_a = nc.dram_tensor("h_scr_a", [T, DM], BF16, kind="Internal").ap()
    h_scr_b = nc.dram_tensor("h_scr_b", [T, DM], BF16, kind="Internal").ap()
    k_scr = nc.dram_tensor("k_scr", [T, 1024], BF16, kind="Internal").ap()
    v_scr = nc.dram_tensor("v_scr", [T, 1024], BF16, kind="Internal").ap()
    gate_scr = nc.dram_tensor("gate_scr", [DM, 512], BF16, kind="Internal").ap()
    g_scr = nc.dram_tensor("g_scr", [NH, T], F32, kind="Internal").ap()
    dt_scr = nc.dram_tensor("dt_scr", [NH, T], F32, kind="Internal").ap()
    grows_d = nc.dram_tensor("k_grows", [4, DM], F32, kind="Internal").ap()
    crows_d = nc.dram_tensor("k_crows", [16, DM], F32, kind="Internal").ap()
    pay = nc.dram_tensor("pay", [2 * 128, NH * 128], F32, kind="Internal",
                         addr_space="Shared").ap()
    flg = nc.dram_tensor("flg", [2, 4], I32, kind="Internal",
                         addr_space="Shared").ap()

    tap_t = {}
    TAPSPEC = {"h1": ([T, DM], BF16), "x1": ([T, DM], F32), "K": ([64, 64], F32),
               "h2": ([T, DM], BF16), "x2": ([T, DM], F32), "h3": ([T, DM], BF16),
               "x3": ([T, DM], F32), "kT": ([T, 1024], BF16),
               "vT": ([T, 1024], BF16), "G": ([NH, T], F32),
               "dt": ([NH, T], F32), "S": ([128, NH * 128], F32),
               "q": ([128, NKC * 512], BF16), "y": ([128, NKC * 512], BF16)}
    for tn in taps:
        shp, dt_ = TAPSPEC[tn]
        tap_t[tn] = nc.dram_tensor(f"tap_{tn}", shp, dt_, kind="ExternalOutput").ap()

    es = ExitStack()
    with es:
        def sbuf(name, shape, dt_):
            return es.enter_context(nc.sbuf_tensor(name, shape, dt_))

        c_ones = sbuf("s_ones", [1, 128], F32)
        c_onescol = sbuf("s_onescol", [128, 1], F32)
        c_id32 = sbuf("s_id32", [128, 128], F32)
        c_idb = sbuf("s_idb", [128, 128], BF16)
        c_revb = sbuf("s_revb", [128, 128], BF16)
        c_protb = sbuf("s_protb", [128, 128], BF16)
        c_maskneg = sbuf("s_maskneg", [128, 128], F32)
        c_mask = sbuf("s_mask", [128, 1], F32)
        cbias = sbuf("s_cbias", [128, 4], F32)
        c_magic = sbuf("s_magic", [1, 4], I32)
        b12_sb = sbuf("s_b12", [128, 64], F32)
        b3bf = sbuf("s_b3bf", [1, DM], BF16)
        c_onesb = sbuf("s_onesb", [1, 128], BF16)
        dtb_sb = sbuf("s_dtb", [NH, 1], F32)
        expa_sb = sbuf("s_expa", [NH, 1], F32)
        gnw_sb = sbuf("s_gnw", [DH, 1], F32)
        winT = sbuf("s_winT", [128, NKC * NH], BF16)
        wpan = sbuf("s_wpan", [128, 16384], BF16)
        hfm = sbuf("s_hfm", [128, NKC * 512], BF16)
        x4 = sbuf("s_x4", [128, NKC * 512], BF16)
        p0t = [sbuf(f"s_p0t{i}", [128, 4096], BF16) for i in range(2)]
        xt = [sbuf(f"s_xt{i}", [128, DM], F32) for i in range(2)]
        ep1 = sbuf("s_ep1", [128, 1024], F32)
        sil = sbuf("s_sil", [128, 1024], F32)
        rsl = [sbuf(f"s_rsl{i}", [128, 512], F32) for i in range(2)]
        hbt = sbuf("s_hbt", [128, 512], BF16)
        stat = sbuf("s_stat", [128, 16], F32)
        ytile = sbuf("s_ytile", [128, 128], F32)
        dec_sb = sbuf("s_dec", [128, 128], F32)
        eg_sb = sbuf("s_eg", [128, 128], F32)
        sdd_sb = sbuf("s_sdd", [128, 128], BF16)
        kdd_sb = sbuf("s_kdd", [128, 128], BF16)
        qex_sb = sbuf("s_qex", [128, 128], BF16)
        ysq_sb = sbuf("s_ysq", [128, 128], F32)
        gsl_sb = sbuf("s_gsl", [128, 512], F32)
        lnsq = sbuf("s_lnsq", [128, 512], F32)
        grow = sbuf("s_grow", [1, 128], F32)
        g16 = sbuf("s_g16", [16, 128], F32)
        small = sbuf("s_small", [128, 8], F32)
        sflag = sbuf("s_sflag", [1, 4], I32)
        S_all = xt[0]
        Sb_host = xt[1]

        def Sb(h):
            return Sb_host[:].bitcast(BF16)[:, h * 128:(h + 1) * 128]

        ps = [es.enter_context(nc.psum_tensor(f"ps{i}", [128, 512], F32))
              for i in range(6)]
        pbf = [es.enter_context(nc.psum_tensor(f"pbf{i}", [128, 1024], BF16))
               for i in range(2)]

        block = es.enter_context(nc.Block())
        X = Ctx(nc, block, es)
        for s_ in list(DMA_SEM.values()) + list(ENG_SEM.values()):
            X.sem(s_)
        for i in range(2):
            X.sem(f"qx{i}")
        for i in range(4):
            X.sem(f"qw{i}")
        for i in range(2):
            X.sem(f"qr{i}")
        X.sem("qhf")
        X.sem("qst")
        X.sem("qtp")
        X.sem("qcast")
        X.sem("qpl")
        for i in range(2):
            X.sem(f"qpt{i}")
            X.sem(f"qpw{i}")
        X.sem("qb3")
        X.sem("qrt")

        X.sem("qc")

        @block.sync
        def _(sy):
            for dst, key in [(c_ones, "c_ones"), (c_onescol, "c_onescol"),
                             (c_id32, "c_id32"), (c_maskneg, "c_maskneg"),
                             (c_mask, "c_mask"), (cbias, "c_bias"),
                             (b12_sb, "b12"),
                             (dtb_sb, "dtb"), (expa_sb, "alb"), (gnw_sb, "gnw")]:
                X.inc("qc", sy.dma_start(out=dst[:], in_=ins[key]), 16)
            X.inc("qc", sy.dma_start(out=c_magic[:], in_=ins["c_magic"]), 16)
        consts_v = ("qc", X.val("qc"))

        cast_done = {}

        @block.gpsimd
        def _(g):
            for nm, M in WSPEC:
                g.wait_ge(X.handle("qcast"), X.val("qcast"))
                cast_done[nm] = ("qcast",
                                 X.inc("qcast", g.dma_start(out=wbf[nm],
                                                            in_=ins[nm]), 16))

        ch = Chain(X, consts_v)
        ch.op("dve", lambda v: v.tensor_copy(c_idb[:], c_id32[:]))
        idb_v = ch.dep()
        ch.op("sp", lambda sy: sy.dma_start(out=xt[0][:, 0:128], in_=ins["c_rev"]),
              dma=True)
        ch.op("dve", lambda v: v.tensor_copy(c_revb[:], xt[0][:, 0:128]))
        ch.op("sp", lambda sy: sy.dma_start(out=xt[0][:, 0:128], in_=ins["c_prot"]),
              dma=True)
        ch.op("dve", lambda v: v.tensor_copy(c_protb[:], xt[0][:, 0:128]))
        ch.op("act", lambda s: s.activation(expa_sb[:], expa_sb[:], AF.Exp))
        ch.op("sp", lambda sy: sy.dma_start(out=xt[0][0:1, :], in_=ins["b3"]),
              dma=True)
        ch.op("dve", lambda v: v.tensor_copy(b3bf[:], xt[0][0:1, :]))
        ch.op("dve", lambda v: v.tensor_copy(c_onesb[:], c_ones[:]))
        ch.op("sp", lambda sy: sy.dma_start(out=xt[0][0:NH, :], in_=ins["win"]),
              dma=True)
        ch.op("act", lambda s: s.copy(x4[0:NH, 0:DM], xt[0][0:NH, :]))
        for kc in range(NKC):
            ch.op("pe", lambda t, kc=kc: t.transpose(
                pbf[kc % 2][0:128, 0:NH], x4[0:NH, kc * 128:(kc + 1) * 128],
                c_idb[0:NH, 0:NH]), also=[idb_v])
            ch.op("dve", lambda v, kc=kc: v.tensor_copy(
                winT[:, kc * NH:(kc + 1) * NH], pbf[kc % 2][0:128, 0:NH]))
        prep_v = ch.dep()

        wtl_done = {}
        p0t_last = [0, 0]
        p0b_jobs = []
        p0b_idx = [0]
        p0b_ring = [0]
        p0b_final = {}

        def p0b_enqueue(nm):
            M = WM[nm]
            for kc in range(NKC):
                if M > 4096:
                    p0b_jobs.append((nm, kc, 0, 4096))
                    p0b_jobs.append((nm, kc, 4096, M - 4096))
                else:
                    p0b_jobs.append((nm, kc, 0, M))

        def p0b_mark_finals():
            for nm, M in WSPEC:
                last = [j for j in p0b_jobs if j[0] == nm][-1]
                p0b_final[nm] = last

        def p0b_pump(njobs):
            end = min(p0b_idx[0] + njobs, len(p0b_jobs))
            if end <= p0b_idx[0]:
                return
            jobs = p0b_jobs[p0b_idx[0]:end]
            p0b_idx[0] = end

            @block.sync
            def _(sy):
                for job in jobs:
                    nm, kc, m0, mw = job
                    ring = p0b_ring[0]
                    p0b_ring[0] ^= 1
                    sy.wait_ge(X.handle(cast_done[nm][0]), cast_done[nm][1])
                    if p0t_last[ring]:
                        sy.wait_ge(X.handle(f"qpw{ring}"), p0t_last[ring])
                    tv = X.inc(f"qpt{ring}", sy.dma_start(
                        out=p0t[ring][:, 0:mw],
                        in_=wbf[nm][m0:m0 + mw, kc * 128:(kc + 1) * 128],
                        transpose=True), 16)
                    sy.wait_ge(X.handle(f"qpt{ring}"), tv)
                    p0t_last[ring] = X.inc(f"qpw{ring}", sy.dma_start(
                        out=wtl[nm][kc * 128:(kc + 1) * 128, m0:m0 + mw],
                        in_=p0t[ring][:, 0:mw]), 16)
                    if job == p0b_final.get(nm):
                        wtl_done[nm] = tuple(
                            ("qpw%d" % r, X.val("qpw%d" % r)) for r in (0, 1)
                            if X.val("qpw%d" % r) > 0)

        def p0b_drain(nm):
            while nm not in wtl_done:
                p0b_pump(8)

        def load_panel_half(sy, nm, m0, hq, free_dep):
            if free_dep is None:
                free_dep = ("mpe", X.val("mpe"))
            sy.wait_ge(X.handle(free_dep[0]), free_dep[1])
            for dsem, dval in wtl_done[nm]:
                sy.wait_ge(X.handle(dsem), dval)
            M = WM[nm]
            sem = f"qw{hq}"
            for cc in range(4):
                src = AP(wtl[nm].tensor, (cc * 8 * 128) * M + m0,
                         [[M, 128], [128 * M, 8], [1, 256]])
                dst = AP(wpan[:].tensor, hq * 8192 + cc * 8 * 256,
                         [[16384, 128], [256, 8], [1, 256]])
                X.inc(sem, sy.dma_start(out=dst, in_=src), 16)
            return (sem, X.val(sem))

        def wh_slice(hq, k, off=0, w=256):
            return wpan[:, hq * 8192 + k * 256 + off:hq * 8192 + k * 256 + off + w]

        def load_panel_quarter(sy, nm, m0, q, free_dep):
            if free_dep is None:
                free_dep = ("mpe", X.val("mpe"))
            sy.wait_ge(X.handle(free_dep[0]), free_dep[1])
            for dsem, dval in wtl_done[nm]:
                sy.wait_ge(X.handle(dsem), dval)
            M = WM[nm]
            sem = f"qw{q}"
            for cc in range(2):
                src = AP(wtl[nm].tensor, (cc * 16 * 128) * M + m0,
                         [[M, 128], [128 * M, 16], [1, 128]])
                dst = AP(wpan[:].tensor, q * 4096 + cc * 16 * 128,
                         [[16384, 128], [128, 16], [1, 128]])
                X.inc(sem, sy.dma_start(out=dst, in_=src), 16)
            return (sem, X.val(sem))

        def wqt_slice(q, k):
            return wpan[:, q * 4096 + k * 128:q * 4096 + (k + 1) * 128]

        def emit_hfm_load(h_src, row0, dep, free_dep=None):
            @block.sync
            def _(sy):
                for kc in range(NKC):
                    if dep is not None:
                        sy.wait_ge(X.handle(dep[0]), dep[1])
                    if kc == 0 and free_dep is not None:
                        sy.wait_ge(X.handle(free_dep[0]), free_dep[1])
                    X.inc("qhf", sy.dma_start(
                        out=hfm[:, kc * 512:(kc + 1) * 512],
                        in_=h_src[row0:row0 + 512, kc * 128:(kc + 1) * 128],
                        transpose=True), 16)
            return ("qhf", X.val("qhf"))

        def emit_ln_sweep(x_src, h_dst, dep, tapx=None, taph=None):
            prev_store = None
            xt_cons = []
            for i in range(nT):
                ring = i % 2

                @block.sync
                def _(sy, i=i, ring=ring):
                    sy.wait_ge(X.handle(dep[0]), dep[1])
                    if i >= 2:
                        sy.wait_ge(X.handle("mac"), xt_cons[i - 2])
                    X.inc(f"qx{ring}", sy.dma_start(
                        out=xt[ring][:], in_=x_src[i * 128:(i + 1) * 128, :]), 16)
                xdep = (f"qx{ring}", X.val(f"qx{ring}"))
                c = Chain(X, xdep)
                s1d = c.op("dve", lambda v, ring=ring: v.reduce_sum(
                    stat[:, 0:1], xt[ring][:], axis=AX.X))
                c2 = Chain(X, xdep)
                for sq in range(8):
                    c2.op("act", lambda s, ring=ring, sq=sq: s.activation(
                        lnsq[:], xt[ring][:, sq * 512:(sq + 1) * 512],
                        AF.Square, accum_out=stat[:, 8 + sq:9 + sq]))
                c2.op("dve", lambda v: v.reduce_sum(stat[:, 1:2], stat[:, 8:16],
                                                    axis=AX.X), also=[s1d])
                c2.op("dve", lambda v: v.tensor_scalar(stat[:, 2:3], stat[:, 0:1],
                                                       1.0 / DM, None, ALU.mult))
                c2.op("dve", lambda v: v.tensor_tensor(stat[:, 3:4], stat[:, 2:3],
                                                       stat[:, 2:3], ALU.mult))
                c2.op("dve", lambda v: v.scalar_tensor_tensor(
                    stat[:, 4:5], stat[:, 1:2], 1.0 / DM, stat[:, 3:4],
                    ALU.mult, ALU.subtract))
                c2.op("act", lambda s: s.activation(stat[:, 5:6], stat[:, 4:5],
                                                    AF.Ln, bias=cbias[:, 0:1]))
                c2.op("act", lambda s: s.activation(stat[:, 6:7], stat[:, 5:6],
                                                    AF.Exp, scale=-0.5))
                c2.op("dve", lambda v: v.scalar_tensor_tensor(
                    stat[:, 7:8], stat[:, 2:3], -1.0, stat[:, 6:7],
                    ALU.mult, ALU.mult))
                nv = None
                for sq in range(8):
                    nv = c2.op("act", lambda s, ring=ring, sq=sq: s.activation(
                        hbt[:], xt[ring][:, sq * 512:(sq + 1) * 512], AF.Identity,
                        bias=stat[:, 7:8], scale=stat[:, 6:7]),
                        also=[prev_store] if (sq == 0 and prev_store) else ())
                    prev_store = c2.op("pool", lambda g, i=i, sq=sq: g.dma_start(
                        out=h_dst[i * 128:(i + 1) * 128, sq * 512:(sq + 1) * 512],
                        in_=hbt[:]), dma=True)
                xt_cons.append(nv[1])
                if taph is not None:
                    ct = Chain(X, prev_store)
                    prev_store = ct.op("pool", lambda g, i=i: g.dma_start(
                        out=tap_t[taph][i * 128:(i + 1) * 128, :],
                        in_=h_dst[i * 128:(i + 1) * 128, :]), dma=True)
                if tapx is not None:
                    ct = Chain(X, xdep, ("mac", nv[1]))
                    ct.op("pool", lambda g, i=i, ring=ring: g.dma_start(
                        out=tap_t[tapx][i * 128:(i + 1) * 128, :],
                        in_=xt[ring][:]), dma=True)
            return prev_store

        def emit_kcomp(kv):
            ksc, toe, hbig = kv["ksc"], kv["toe"], kv["hbig"]
            c = Chain(X, consts_v, prep_v)
            for dst, key in [(ksc["lam"], "a_lam"), (ksc["p"], "a_p"),
                             (ksc["q"], "a_q"), (ksc["ct"], "a_ct"),
                             (ksc["vc"], "a_vc"), (ksc["bp"], "a_bp"),
                             (ksc["dD"], "a_d"), (ksc["ls"], "a_ls"),
                             (ksc["omr"], "c_omr"), (ksc["omi"], "c_omi"),
                             (ksc["fr"], "c_fr"), (ksc["fi"], "c_fi"),
                             (ksc["fni"], "c_fni"), (ksc["twr"], "c_twr"),
                             (ksc["twi"], "c_twi")]:
                c.op("act", lambda s, dst=dst, key=key: s.dma_start(
                    out=dst[:], in_=ins[key]), dma=True)
            c.op("pe", lambda t: t.transpose(ps[0][0:1, 0:64], ksc["bp"][:],
                                             c_id32[0:64, 0:64]))
            c.op("dve", lambda v: v.tensor_copy(grow[0:1, 0:64], ps[0][0:1, 0:64]))
            c.op("pe", lambda t: t.matmul(ps[1][0:64, 0:64], c_ones[0:1, 0:64],
                                          grow[0:1, 0:64], start=True, stop=True))
            vc_r = AP(ksc["vc"][:].tensor, 0, [[128, 64], [2, 64]])
            vc_i = AP(ksc["vc"][:].tensor, 1, [[128, 64], [2, 64]])
            c.op("dve", lambda v: v.tensor_tensor(ksc["t0"][:], vc_r,
                                                  ps[1][0:64, 0:64], ALU.mult))
            c.op("dve", lambda v: v.tensor_tensor(ksc["t1"][:], vc_i,
                                                  ps[1][0:64, 0:64], ALU.mult))
            c.op("dve", lambda v: v.reduce_sum(ksc["Bv"][:, 0:1], ksc["t0"][:],
                                               axis=AX.X))
            c.op("dve", lambda v: v.reduce_sum(ksc["Bv"][:, 1:2], ksc["t1"][:],
                                               axis=AX.X))

            def cconj(dst_r, dst_i, a, b):
                ar, ai = a[:, 0:1], a[:, 1:2]
                br, bi = b[:, 0:1], b[:, 1:2]
                c.op("dve", lambda v: v.tensor_tensor(ksc["t0"][:, 0:1], ar, br,
                                                      ALU.mult))
                c.op("dve", lambda v: v.tensor_tensor(ksc["t0"][:, 1:2], ai, bi,
                                                      ALU.mult))
                c.op("dve", lambda v: v.tensor_tensor(ksc["t1"][:, 0:1], ar, bi,
                                                      ALU.mult))
                c.op("dve", lambda v: v.tensor_tensor(ksc["t1"][:, 1:2], ai, br,
                                                      ALU.mult))
                c.op("dve", lambda v: v.tensor_add(dst_r, ksc["t0"][:, 0:1],
                                                   ksc["t0"][:, 1:2]))
                c.op("dve", lambda v: v.tensor_sub(dst_i, ksc["t1"][:, 0:1],
                                                   ksc["t1"][:, 1:2]))
            V = ksc["V"]
            cconj(V[:, 0:1], V[:, 4:5], ksc["ct"], ksc["Bv"])
            cconj(V[:, 1:2], V[:, 5:6], ksc["ct"], ksc["p"])
            cconj(V[:, 2:3], V[:, 6:7], ksc["q"], ksc["Bv"])
            cconj(V[:, 3:4], V[:, 7:8], ksc["q"], ksc["p"])
            c.op("act", lambda s: s.activation(ksc["sc"][0:1, 0:1], ksc["ls"][:],
                                               AF.Exp))
            c.op("dve", lambda v: v.reciprocal(ksc["sc"][0:1, 1:2],
                                               ksc["sc"][0:1, 0:1]))
            c.op("dve", lambda v: v.tensor_scalar(ksc["sc"][0:1, 2:3],
                                                  ksc["sc"][0:1, 1:2], 2.0, None,
                                                  ALU.mult))
            c.op("pe", lambda t: t.matmul(ps[0][0:64, 0:1], c_ones[0:1, 0:64],
                                          ksc["sc"][0:1, 2:3], start=True,
                                          stop=True))
            c.op("dve", lambda v: v.tensor_copy(ksc["rs"][:, 0:1], ps[0][0:64, 0:1]))
            c.op("dve", lambda v: v.tensor_scalar(ksc["t0"][:], ksc["omr"][:],
                                                  1.0, None, ALU.add))
            c.op("dve", lambda v: v.tensor_tensor(ksc["t1"][:], ksc["t0"][:],
                                                  ksc["t0"][:], ALU.mult))
            c.op("dve", lambda v: v.tensor_tensor(ksc["t2"][:], ksc["omi"][:],
                                                  ksc["omi"][:], ALU.mult))
            c.op("dve", lambda v: v.tensor_add(ksc["t3"][:], ksc["t1"][:],
                                               ksc["t2"][:]))
            c.op("dve", lambda v: v.reciprocal(ksc["t3"][:], ksc["t3"][:]))
            c.op("dve", lambda v: v.tensor_scalar(ksc["t1"][:], ksc["omr"][:],
                                                  -1.0, 1.0, ALU.mult, ALU.add))
            c.op("dve", lambda v: v.tensor_tensor(ksc["t1"][:], ksc["t1"][:],
                                                  ksc["t0"][:], ALU.mult))
            c.op("dve", lambda v: v.tensor_sub(ksc["t1"][:], ksc["t1"][:],
                                               ksc["t2"][:]))
            c.op("dve", lambda v: v.tensor_tensor(ksc["gr"][:], ksc["t1"][:],
                                                  ksc["t3"][:], ALU.mult))
            c.op("dve", lambda v: v.tensor_scalar_mul(ksc["gr"][:], ksc["gr"][:],
                                                      ksc["rs"][:, 0:1]))
            c.op("dve", lambda v: v.tensor_tensor(ksc["gi"][:], ksc["omi"][:],
                                                  ksc["t3"][:], ALU.mult))
            c.op("dve", lambda v: v.tensor_scalar(ksc["gi"][:], ksc["gi"][:],
                                                  2.0, None, ALU.mult))
            c.op("dve", lambda v: v.tensor_scalar_mul(ksc["gi"][:], ksc["gi"][:],
                                                      ksc["rs"][:, 0:1]))
            c.op("dve", lambda v: v.tensor_tensor(ksc["cr"][:], ksc["t0"][:],
                                                  ksc["t3"][:], ALU.mult))
            c.op("dve", lambda v: v.tensor_tensor(ksc["ci"][:], ksc["omi"][:],
                                                  ksc["t3"][:], ALU.mult))
            c.op("dve", lambda v: v.tensor_scalar(ksc["cr"][:], ksc["cr"][:],
                                                  2.0, None, ALU.mult))
            c.op("dve", lambda v: v.tensor_scalar(ksc["ci"][:], ksc["ci"][:],
                                                  2.0, None, ALU.mult))
            c.op("pool", lambda g: g.dma_start(
                out=grows_d[0:1, :].rearrange("o (p f) -> (o p) f", p=64),
                in_=ksc["gr"][:]), dma=True)
            c.op("pool", lambda g: g.dma_start(
                out=grows_d[1:2, :].rearrange("o (p f) -> (o p) f", p=64),
                in_=ksc["gi"][:]), dma=True)
            sc32 = 2.0 ** -32
            for i in range(DM // 512):
                c.op("act", lambda s, i=i: s.dma_start(
                    out=ksc["gr8"],
                    in_=grows_d[0:1, i * 512:(i + 1) * 512]), dma=True)
                c.op("act", lambda s, i=i: s.dma_start(
                    out=ksc["gi8"],
                    in_=grows_d[1:2, i * 512:(i + 1) * 512]), dma=True)
                c.op("pe", lambda t: t.matmul(ps[2][0:64, :], c_ones[0:1, 0:64],
                                              ksc["gr8"], start=True,
                                              stop=True))
                c.op("pe", lambda t: t.matmul(ps[3][0:64, :], c_ones[0:1, 0:64],
                                              ksc["gi8"], start=True,
                                              stop=True))
                c.op("dve", lambda v: v.tensor_scalar(
                    ksc["ir"][:], ps[2][0:64, :], ksc["lam"][:, 0:1], sc32,
                    ALU.subtract, ALU.mult))
                c.op("dve", lambda v: v.tensor_scalar(
                    ksc["ii"][:], ps[3][0:64, :], ksc["lam"][:, 1:2], sc32,
                    ALU.subtract, ALU.mult))
                d0 = ksc["d0"][0:64, 0:512]
                d1 = ksc["d1"][0:64, 512:1024]
                c.op("dve", lambda v, d0=d0: v.tensor_tensor(d0, ksc["ir"][:],
                                                             ksc["ir"][:], ALU.mult))
                c.op("dve", lambda v, d1=d1: v.tensor_tensor(d1, ksc["ii"][:],
                                                             ksc["ii"][:], ALU.mult))
                c.op("dve", lambda v, d0=d0, d1=d1: v.tensor_add(d0, d0, d1))
                c.op("dve", lambda v, d0=d0: v.reciprocal(d0, d0))
                c.op("dve", lambda v, d0=d0: v.tensor_tensor(ksc["ir"][:],
                                                             ksc["ir"][:], d0,
                                                             ALU.mult))
                c.op("dve", lambda v, d0=d0: v.tensor_tensor(ksc["ii"][:],
                                                             ksc["ii"][:], d0,
                                                             ALU.mult))
                c.op("dve", lambda v: v.tensor_scalar(ksc["ir"][:], ksc["ir"][:],
                                                      sc32, None, ALU.mult))
                c.op("dve", lambda v: v.tensor_scalar(ksc["ii"][:], ksc["ii"][:],
                                                      -sc32, None, ALU.mult))
                c.op("pe", lambda t: t.matmul(ps[4][0:8, :], ksc["V"][:],
                                              ksc["ir"][:], start=True, stop=True))
                c.op("dve", lambda v: v.tensor_copy(ksc["cs"][:], ps[4][0:8, :]))
                c.op("pool", lambda g, i=i: g.dma_start(
                    out=crows_d[0:8, i * 512:(i + 1) * 512], in_=ksc["cs"][:]),
                    dma=True)
                c.op("pe", lambda t: t.matmul(ps[4][0:8, :], ksc["V"][:],
                                              ksc["ii"][:], start=True, stop=True))
                c.op("dve", lambda v: v.tensor_copy(ksc["cs"][:], ps[4][0:8, :]))
                c.op("pool", lambda g, i=i: g.dma_start(
                    out=crows_d[8:16, i * 512:(i + 1) * 512], in_=ksc["cs"][:]),
                    dma=True)
            for j in range(8):
                c.op("act", lambda s, j=j: s.dma_start(
                    out=ksc["kk"][0:64, j * 64:(j + 1) * 64],
                    in_=crows_d[j:j + 1, :].rearrange("o (p f) -> (o p) f", p=64)),
                    dma=True)
                c.op("act", lambda s, j=j: s.dma_start(
                    out=ksc["kb"][0:64, 512 + j * 64:512 + (j + 1) * 64],
                    in_=crows_d[8 + j:9 + j, :].rearrange("o (p f) -> (o p) f",
                                                          p=64)), dma=True)
            kk = ksc["kk"][0:64, 0:512]
            kb = ksc["kb"][0:64, 512:1024]
            dst = ksc["kcplx"][0:64, 0:512]
            for j in range(4):
                c.op("dve", lambda v, j=j: v.tensor_sub(
                    dst[:, j * 64:(j + 1) * 64], kk[:, j * 64:(j + 1) * 64],
                    kb[:, (4 + j) * 64:(5 + j) * 64]))
                c.op("dve", lambda v, j=j: v.tensor_add(
                    dst[:, (4 + j) * 64:(5 + j) * 64], kb[:, j * 64:(j + 1) * 64],
                    kk[:, (4 + j) * 64:(5 + j) * 64]))
            k00r, k01r, k10r, k11r = (dst[:, j * 64:(j + 1) * 64] for j in range(4))
            k00i, k01i, k10i, k11i = (dst[:, (4 + j) * 64:(5 + j) * 64]
                                      for j in range(4))
            t0, t1, t2, t3, t4, t5 = (ksc[f"t{j}"][:] for j in range(6))
            c.op("dve", lambda v: v.tensor_scalar(t0, k11r, 1.0, None, ALU.add))
            c.op("dve", lambda v: v.tensor_tensor(t1, t0, t0, ALU.mult))
            c.op("dve", lambda v: v.tensor_tensor(t2, k11i, k11i, ALU.mult))
            c.op("dve", lambda v: v.tensor_add(t1, t1, t2))
            c.op("dve", lambda v: v.reciprocal(t1, t1))
            c.op("dve", lambda v: v.tensor_tensor(t2, k01r, k10r, ALU.mult))
            c.op("dve", lambda v: v.tensor_tensor(t4, k01i, k10i, ALU.mult))
            c.op("dve", lambda v: v.tensor_tensor(t3, k01r, k10i, ALU.mult))
            c.op("dve", lambda v: v.tensor_tensor(t5, k01i, k10r, ALU.mult))
            c.op("dve", lambda v: v.tensor_sub(t2, t2, t4))
            c.op("dve", lambda v: v.tensor_add(t3, t3, t5))
            c.op("dve", lambda v: v.tensor_tensor(t4, t2, t0, ALU.mult))
            c.op("dve", lambda v: v.tensor_tensor(t5, t3, k11i, ALU.mult))
            c.op("dve", lambda v: v.tensor_add(t4, t4, t5))
            c.op("dve", lambda v: v.tensor_tensor(t5, t3, t0, ALU.mult))
            c.op("dve", lambda v: v.tensor_tensor(t0, t2, k11i, ALU.mult))
            c.op("dve", lambda v: v.tensor_sub(t5, t5, t0))
            c.op("dve", lambda v: v.tensor_tensor(t4, t4, t1, ALU.mult))
            c.op("dve", lambda v: v.tensor_tensor(t5, t5, t1, ALU.mult))
            c.op("dve", lambda v: v.tensor_sub(t0, k00r, t4))
            c.op("dve", lambda v: v.tensor_sub(t1, k00i, t5))
            c.op("dve", lambda v: v.tensor_tensor(t2, ksc["cr"][:], t0, ALU.mult))
            c.op("dve", lambda v: v.tensor_tensor(t4, ksc["ci"][:], t1, ALU.mult))
            c.op("dve", lambda v: v.tensor_tensor(t3, ksc["cr"][:], t1, ALU.mult))
            c.op("dve", lambda v: v.tensor_tensor(t5, ksc["ci"][:], t0, ALU.mult))
            c.op("dve", lambda v: v.tensor_sub(t2, t2, t4))
            c.op("dve", lambda v: v.tensor_add(t3, t3, t5))

            def s1r_mm(t):
                t.matmul(ps[0][0:64, 0:64], ksc["t2"][:], ksc["fr"][:],
                         start=True, stop=False)
                return t.matmul(ps[0][0:64, 0:64], ksc["t3"][:], ksc["fni"][:],
                                start=False, stop=True)

            def s1i_mm(t):
                t.matmul(ps[1][0:64, 0:64], ksc["t2"][:], ksc["fi"][:],
                         start=True, stop=False)
                return t.matmul(ps[1][0:64, 0:64], ksc["t3"][:], ksc["fr"][:],
                                start=False, stop=True)
            c.op("pe", s1r_mm)
            c.op("pe", s1i_mm)
            c.op("dve", lambda v: v.tensor_copy(ksc["s1r"][:], ps[0][0:64, 0:64]))
            c.op("dve", lambda v: v.tensor_copy(ksc["s1i"][:], ps[1][0:64, 0:64]))
            c.op("dve", lambda v: v.tensor_tensor(ksc["br"][:], ksc["s1r"][:],
                                                  ksc["twr"][:], ALU.mult))
            c.op("dve", lambda v: v.tensor_tensor(ksc["t0"][:], ksc["s1i"][:],
                                                  ksc["twi"][:], ALU.mult))
            c.op("dve", lambda v: v.tensor_tensor(ksc["bi"][:], ksc["s1r"][:],
                                                  ksc["twi"][:], ALU.mult))
            c.op("dve", lambda v: v.tensor_tensor(ksc["t1"][:], ksc["s1i"][:],
                                                  ksc["twr"][:], ALU.mult))
            c.op("dve", lambda v: v.tensor_sub(ksc["br"][:], ksc["br"][:],
                                               ksc["t0"][:]))
            c.op("dve", lambda v: v.tensor_add(ksc["bi"][:], ksc["bi"][:],
                                               ksc["t1"][:]))

            def s2_mm(t):
                t.matmul(ps[2][0:64, 0:64], ksc["fr"][:], ksc["br"][:],
                         start=True, stop=False)
                return t.matmul(ps[2][0:64, 0:64], ksc["fni"][:], ksc["bi"][:],
                                start=False, stop=True)
            c.op("pe", s2_mm)
            c.op("act", lambda s: s.activation(ksc["Ksb"][:], ps[2][0:64, 0:64],
                                               AF.Copy, scale=1.0 / 4096.0))
            c.op("dve", lambda v: v.tensor_add(ksc["Ksb"][0:1, 0:1],
                                               ksc["Ksb"][0:1, 0:1], ksc["dD"][:]))
            if "K" in tap_t:
                c.op("pool", lambda g: g.dma_start(out=tap_t["K"],
                                                   in_=ksc["Ksb"][:]), dma=True)
            c.op("dve", lambda v: v.tensor_copy(ksc["Kbf"][:], ksc["Ksb"][:]))
            c.op("dve", lambda v: v.memset(ksc["z256"][:], 0.0))
            c.op("pool", lambda g: g.dma_start(out=Zbuf[0:1, 0:128],
                                               in_=ksc["z256"][0:1, 0:128]), dma=True)
            c.op("pool", lambda g: g.dma_start(out=Zbuf[0:1, 4224:4352],
                                               in_=ksc["z256"][0:1, 128:256]), dma=True)
            c.op("pool", lambda g: g.dma_start(
                out=Zbuf[0:1, 128:4224].rearrange("o (p f) -> (o p) f", p=64),
                in_=ksc["Kbf"][:]), dma=True)
            c.op("act", lambda s: s.dma_start(
                out=hbig[:], in_=AP(Zbuf.tensor, 0, [[1, 128], [1, 4224]])),
                dma=True)
            c.op("dve", lambda v: v.memset(toe[:, 0:3 * 128], 0.0))
            for tt in range(32):
                c.op("pe", lambda t, tt=tt: t.matmul(
                    ps[tt % 2][0:128, 0:128], c_revb[:],
                    hbig[:, 128 * tt + 1:128 * tt + 129], start=True, stop=True))
                c.op("dve", lambda v, tt=tt: v.tensor_copy(
                    toe[:, (3 + tt) * 128:(4 + tt) * 128],
                    ps[tt % 2][0:128, 0:128]))
            return c.dep()

        def emit_outproj(nm, src_w, row0, src_dep, resid_src, out_dst, with_b3,
                         extra_pump=0):
            npan = DM // 256
            ntiles = nGT
            pe_base = X.val("mpe")
            mdv0 = X.val("mdv")
            qst0 = X.val("qst")
            pan_v = {}
            rs_v = {}
            b3v = {}

            def slice_free(grp):
                return mdv0 + 2 * grp + 1

            def slice_done(grp):
                return mdv0 + 2 * (grp + 1)

            def store_done(grp):
                return qst0 + 16 * (grp + 1)

            @block.sync
            def _(sy):
                for pi in range(npan):
                    free = None
                    if pi >= 2:
                        free = ("mpe", pe_base + (pi - 1) * ntiles)
                    pan_v[pi] = load_panel_half(sy, nm, pi * 256, pi % 2, free)
                    for lt in range(ntiles):
                        k = pi * ntiles + lt
                        ring = k % 2
                        if k >= 2:
                            sy.wait_ge(X.handle("mdv"), slice_done(k - 2))
                        elif mdv0 > 0:
                            sy.wait_ge(X.handle("mdv"), mdv0)
                        gi = row0 // 128 + lt
                        X.inc(f"qr{ring}", sy.dma_start(
                            out=rsl[ring][:, 0:256],
                            in_=resid_src[gi * 128:(gi + 1) * 128,
                                          pi * 256:(pi + 1) * 256]), 16)
                        rs_v[k] = X.val(f"qr{ring}")
            if extra_pump:
                p0b_pump(extra_pump)

            @block.tensor
            def _(t):
                t.wait_ge(X.handle(src_dep[0]), src_dep[1])
                t.wait_ge(X.handle("mdv"), mdv0)
                for pi in range(npan):
                    t.wait_ge(X.handle(pan_v[pi][0]), pan_v[pi][1])
                    for lt in range(ntiles):
                        grp = pi * ntiles + lt
                        bank = grp % 5
                        if grp >= 5:
                            t.wait_ge(X.handle("mdv"), slice_free(grp - 5))
                        mm = None
                        for k in range(NKC):
                            mm = t.matmul(
                                ps[bank][:, 0:256],
                                x4[:, k * src_w + lt * 128:k * src_w + lt * 128 + 128],
                                wh_slice(pi % 2, k),
                                start=(k == 0),
                                stop=(k == NKC - 1 and not with_b3))
                        if with_b3:
                            mm = t.matmul(
                                ps[bank][:, 0:256], c_onesb[0:1, 0:128],
                                b3bf[0:1, pi * 256:(pi + 1) * 256],
                                start=False, stop=True)
                        X.inc("mpe", mm, 1)

            @block.vector
            def _(v):
                for pi in range(npan):
                    for lt in range(ntiles):
                        grp = pi * ntiles + lt
                        bank = grp % 5
                        ring = grp % 2
                        v.wait_ge(X.handle("mpe"), pe_base + grp + 1)
                        v.wait_ge(X.handle(f"qr{ring}"), rs_v[grp])
                        if grp >= 2:
                            v.wait_ge(X.handle("qst"), store_done(grp - 2))
                        elif qst0 > 0:
                            v.wait_ge(X.handle("qst"), qst0)
                        sl = ep1[:, (grp % 2) * 256:(grp % 2) * 256 + 256]
                        X.inc("mdv", v.tensor_add(sl, ps[bank][:, 0:256],
                                                  rsl[ring][:, 0:256]), 1)
                        X.inc("mdv", v.engine_nop(), 1)

            @block.gpsimd
            def _(g):
                for pi in range(npan):
                    for lt in range(ntiles):
                        grp = pi * ntiles + lt
                        gi = row0 // 128 + lt
                        g.wait_ge(X.handle("mdv"), slice_done(grp))
                        g.wait_ge(X.handle("qst"), X.val("qst"))
                        sl = ep1[:, (grp % 2) * 256:(grp % 2) * 256 + 256]
                        X.inc("qst", g.dma_start(
                            out=out_dst[gi * 128:(gi + 1) * 128,
                                        pi * 256:(pi + 1) * 256], in_=sl), 16)
            return ("qst", X.val("qst"))

        def emit_gemm1(hfm_dep):
            npair = DM // 128
            pe_base = X.val("mpe")
            mac0 = X.val("mac")
            mdv0 = X.val("mdv")
            pan_v = {}

            @block.sync
            def _(sy):
                for pi in range(npair):
                    for side in range(2):
                        q = (2 * pi + side) % 4
                        free = None
                        if pi >= 2:
                            free = ("mpe", pe_base + (pi - 1))
                        m0 = pi * 128 + side * (DFF // 2)
                        pan_v[(pi, side)] = load_panel_quarter(sy, "w12", m0, q,
                                                               free)

            @block.tensor
            def _(t):
                t.wait_ge(X.handle(hfm_dep[0]), hfm_dep[1])
                t.wait_ge(X.handle("mac"), mac0)
                t.wait_ge(X.handle("mdv"), mdv0)
                for pi in range(npair):
                    qa = (2 * pi) % 4
                    qb = (2 * pi + 1) % 4
                    t.wait_ge(X.handle(pan_v[(pi, 0)][0]), pan_v[(pi, 0)][1])
                    t.wait_ge(X.handle(pan_v[(pi, 1)][0]), pan_v[(pi, 1)][1])
                    bankA = (2 * pi) % 6
                    bankB = (2 * pi + 1) % 6
                    if pi >= 3:
                        t.wait_ge(X.handle("mac"), mac0 + pi - 2)
                        t.wait_ge(X.handle("mdv"), mdv0 + pi - 2)
                    for k in range(NKC):
                        t.matmul(ps[bankA][:], wqt_slice(qa, k),
                                 hfm[:, k * 512:(k + 1) * 512],
                                 start=(k == 0), stop=(k == NKC - 1))
                    mm = None
                    for k in range(NKC):
                        mm = t.matmul(ps[bankB][:], wqt_slice(qb, k),
                                      hfm[:, k * 512:(k + 1) * 512],
                                      start=(k == 0), stop=(k == NKC - 1))
                    X.inc("mpe", mm, 1)

            @block.scalar
            def _(s):
                for pi in range(npair):
                    bankA = (2 * pi) % 6
                    s.wait_ge(X.handle("mpe"), pe_base + pi + 1)
                    if pi >= 2:
                        s.wait_ge(X.handle("mdv"), mdv0 + pi - 1)
                    X.inc("mac", s.activation(
                        sil[:, (pi % 2) * 512:(pi % 2) * 512 + 512],
                        ps[bankA][:], AF.Silu, bias=b12_sb[:, pi:pi + 1]), 1)

            @block.vector
            def _(v):
                for pi in range(npair):
                    bankB = (2 * pi + 1) % 6
                    v.wait_ge(X.handle("mac"), mac0 + pi + 1)
                    X.inc("mdv", v.scalar_tensor_tensor(
                        x4[:, pi * 512:(pi + 1) * 512], ps[bankB][:],
                        b12_sb[:, 32 + pi:33 + pi],
                        sil[:, (pi % 2) * 512:(pi % 2) * 512 + 512],
                        ALU.add, ALU.mult), 1)
            return ("mdv", X.val("mdv"))

        def emit_p1():
            kv = {}
            with ExitStack() as p1es:
                def p1sb(name, shape, dt_):
                    return p1es.enter_context(nc.sbuf_tensor(name, shape, dt_))
                kv["toe"] = p1sb("p1_toe", [128, 35 * 128], BF16)
                kv["hbig"] = p1sb("p1_hbig", [128, 4224], BF16)
                ksc = {}
                for knm, kshape, kdt in [
                        ("lam", [64, 2], F32), ("p", [64, 2], F32),
                        ("q", [64, 2], F32), ("ct", [64, 2], F32),
                        ("vc", [64, 128], F32), ("bp", [64, 1], F32),
                        ("dD", [1, 1], F32), ("ls", [1, 1], F32),
                        ("omr", [64, 64], F32), ("omi", [64, 64], F32),
                        ("fr", [64, 64], F32), ("fi", [64, 64], F32),
                        ("fni", [64, 64], F32), ("twr", [64, 64], F32),
                        ("twi", [64, 64], F32), ("Bv", [64, 2], F32),
                        ("V", [64, 8], F32), ("sc", [1, 4], F32),
                        ("rs", [64, 1], F32),
                        ("t0", [64, 64], F32), ("t1", [64, 64], F32),
                        ("t2", [64, 64], F32), ("t3", [64, 64], F32),
                        ("t4", [64, 64], F32), ("t5", [64, 64], F32),
                        ("gr", [64, 64], F32), ("gi", [64, 64], F32),
                        ("cr", [64, 64], F32), ("ci", [64, 64], F32),
                        ("ir", [64, 512], F32), ("ii", [64, 512], F32),
                        ("cs", [8, 512], F32),
                        ("s1r", [64, 64], F32), ("s1i", [64, 64], F32),
                        ("br", [64, 64], F32), ("bi", [64, 64], F32),
                        ("Ksb", [64, 64], F32), ("Kbf", [64, 64], BF16),
                        ("z256", [1, 256], BF16)]:
                    ksc[knm] = p1sb("k_" + knm, kshape, kdt)
                ksc["gr8"] = ep1[0:1, 0:512]
                ksc["gi8"] = ep1[0:1, 512:1024]
                ksc["kcplx"] = gsl_sb
                ksc["d0"] = sil
                ksc["d1"] = sil
                ksc["kk"] = ep1
                ksc["kb"] = ep1
                kv["ksc"] = ksc
                toe_dep = emit_kcomp(kv)
                h1_dep = emit_ln_sweep(X_in, h_scr_a, prep_v,
                                       taph="h1" if "h1" in tap_t else None)
                p0b_pump(24)
                prev_pe = None
                for sl_i in range(nS):
                    hfm_dep = emit_hfm_load(h_scr_a, sl_i * 512, h1_dep,
                                            free_dep=prev_pe)
                    pe_base = X.val("mpe")
                    mdv0 = X.val("mdv")
                    qst0 = X.val("qst")
                    rs_v = {}

                    @block.sync
                    def _(sy, sl_i=sl_i, mdv0=mdv0):
                        for lt in range(nGT):
                            gi = sl_i * nGT + lt
                            for pn in range(8):
                                k = lt * 8 + pn
                                ring = k % 2
                                if mdv0 + k - 1 > 0:
                                    sy.wait_ge(X.handle("mdv"), mdv0 + k - 1)
                                X.inc(f"qr{ring}", sy.dma_start(
                                    out=rsl[ring][:],
                                    in_=X_in[gi * 128:(gi + 1) * 128,
                                             pn * 512:(pn + 1) * 512]), 16)
                                rs_v[k] = X.val(f"qr{ring}")

                    @block.tensor
                    def _(t, sl_i=sl_i, hfm_dep=hfm_dep, mdv0=mdv0):
                        t.wait_ge(X.handle(hfm_dep[0]), hfm_dep[1])
                        t.wait_ge(X.handle(toe_dep[0]), toe_dep[1])
                        t.wait_ge(X.handle("mdv"), mdv0)
                        for lt in range(nGT):
                            for pn in range(8):
                                grp = lt * 8 + pn
                                bank = grp % 5
                                if grp >= 5:
                                    t.wait_ge(X.handle("mdv"), mdv0 + grp - 4)
                                L0 = pn * 4
                                jmax = min(L0 + 4, NKC)
                                mm = None
                                for J in range(jmax):
                                    soff = (L0 - J + 3) * 128
                                    mm = t.matmul(
                                        ps[bank][:],
                                        hfm[:, J * 512 + lt * 128:
                                            J * 512 + (lt + 1) * 128],
                                        kv["toe"][:, soff:soff + 512],
                                        start=(J == 0), stop=(J == jmax - 1))
                                X.inc("mpe", mm, 1)

                    @block.vector
                    def _(v, sl_i=sl_i, mdv0=mdv0, pe_base=pe_base, qst0=qst0):
                        for lt in range(nGT):
                            for pn in range(8):
                                grp = lt * 8 + pn
                                bank = grp % 5
                                ring = grp % 2
                                v.wait_ge(X.handle("mpe"), pe_base + grp + 1)
                                v.wait_ge(X.handle(f"qr{ring}"), rs_v[grp])
                                if qst0 + 16 * (grp - 1) > 0:
                                    v.wait_ge(X.handle("qst"),
                                              qst0 + 16 * (grp - 1))
                                sl = ep1[:, (grp % 2) * 512:(grp % 2) * 512 + 512]
                                X.inc("mdv", v.tensor_add(sl, ps[bank][:],
                                                          rsl[ring][:]), 1)

                    @block.gpsimd
                    def _(g, sl_i=sl_i, mdv0=mdv0):
                        for lt in range(nGT):
                            gi = sl_i * nGT + lt
                            for pn in range(8):
                                grp = lt * 8 + pn
                                g.wait_ge(X.handle("mdv"), mdv0 + grp + 1)
                                g.wait_ge(X.handle("qst"), X.val("qst"))
                                sl = ep1[:, (grp % 2) * 512:(grp % 2) * 512 + 512]
                                X.inc("qst", g.dma_start(
                                    out=resid_a[gi * 128:(gi + 1) * 128,
                                                pn * 512:(pn + 1) * 512],
                                    in_=sl), 16)
                    prev_pe = ("mpe", X.val("mpe"))
                    p0b_pump(8)
                x1_dep = ("qst", X.val("qst"))
                if "x1" in tap_t:
                    cx = Chain(X, x1_dep)
                    cx.op("pool", lambda g: g.dma_start(out=tap_t["x1"],
                                                        in_=resid_a), dma=True)
            return x1_dep

        def emit_attnb(abt, h3_dep):
            kvring = abt["kvring"]
            cols = abt["cols"]
            gcols, dtcols, edcols, cdcols = (cols["g"], cols["dt"], cols["ed"],
                                             cols["cd"])
            ctm2, stm2, dtg = abt["ctm2"], abt["stm2"], abt["dtg"]
            prev_pe = None
            dt_deps = []
            kv_spill = None
            for sl_i in range(nS):
                row0 = sl_i * 512
                hfm_dep = emit_hfm_load(h_scr_a, row0, h3_dep, free_dep=prev_pe)
                pe_base = X.val("mpe")
                mac0 = X.val("mac")
                pan_v = {}
                kq_done = {}

                @block.sync
                def _(sy, sl_i=sl_i, pe_base=pe_base):
                    for j in range(8):
                        nm2 = "wk" if j < 4 else "wv"
                        m0 = (j % 4) * 256
                        slot = j % 2
                        free = None
                        if sl_i == 0:
                            if j >= 2:
                                free = ("mpe", pe_base + (j - 1) * 4)
                        else:
                            free = ("mpe", pe_base + max(0, (j - 1) * 4))
                        pan_v[j] = load_panel_half(sy, nm2, m0, slot, free)

                @block.tensor
                def _(t, sl_i=sl_i, hfm_dep=hfm_dep, mac0=mac0):
                    t.wait_ge(X.handle(hfm_dep[0]), hfm_dep[1])
                    t.wait_ge(X.handle(prep_v[0]), prep_v[1])
                    t.wait_ge(X.handle("mac"), mac0)
                    gidx = 0
                    for j in range(8):
                        slot = j % 2
                        t.wait_ge(X.handle(pan_v[j][0]), pan_v[j][1])
                        for lt in range(nGT):
                            bank = gidx % 6
                            if gidx >= 6:
                                t.wait_ge(X.handle("mac"), mac0 + gidx - 5)
                            mm = None
                            for k in range(NKC):
                                mm = t.matmul(
                                    ps[bank][:, 0:256],
                                    hfm[:, k * 512 + lt * 128:
                                        k * 512 + lt * 128 + 128],
                                    wh_slice(slot, k),
                                    start=(k == 0), stop=(k == NKC - 1))
                            X.inc("mpe", mm, 1)
                            gidx += 1
                    bank = gidx % 6
                    if gidx >= 6:
                        t.wait_ge(X.handle("mac"), mac0 + gidx - 5)
                    mm = None
                    for k in range(NKC):
                        mm = t.matmul(ps[bank][0:NH, :],
                                      winT[:, k * NH:(k + 1) * NH],
                                      hfm[:, k * 512:(k + 1) * 512],
                                      start=(k == 0), stop=(k == NKC - 1))
                    X.inc("mpe", mm, 1)
                dt_bank = 32 % 6

                @block.scalar
                def _(s, sl_i=sl_i, pe_base=pe_base):
                    gidx = 0
                    for j in range(8):
                        dstt = "k" if j < 4 else "v"
                        mp = j % 4
                        for lt in range(nGT):
                            bank = gidx % 6
                            s.wait_ge(X.handle("mpe"), pe_base + gidx + 1)
                            X.inc("mac", s.copy(
                                kvring[dstt][:, lt * 1024 + mp * 256:
                                             lt * 1024 + (mp + 1) * 256],
                                ps[bank][:, 0:256]), 1)
                            gidx += 1

                @block.gpsimd
                def _(g, sl_i=sl_i, mac0=mac0):
                    for lt in range(nGT):
                        gi = sl_i * nGT + lt
                        g.wait_ge(X.handle("mac"), mac0 + 3 * 4 + lt + 1)
                        g.wait_ge(X.handle("qg"), X.val("qg"))
                        X.inc("qg", g.dma_start(
                            out=ctm2[:],
                            in_=ins["c_costm"][:, gi * 128:(gi + 1) * 128]), 16)
                        X.inc("qg", g.dma_start(
                            out=stm2[:],
                            in_=ins["c_sintm"][:, gi * 128:(gi + 1) * 128]), 16)
                        g.wait_ge(X.handle("qg"), X.val("qg"))
                        for kvh in range(NKV):
                            base = lt * 1024 + kvh * 128
                            klo = kvring["k"][:, base:base + 64]
                            khi = kvring["k"][:, base + 64:base + 128]
                            X.inc("mpo", g.tensor_tensor(
                                gsl_sb[:, 0:128], kvring["k"][:, base:base + 128],
                                ctm2[:], ALU.mult), 1)
                            X.inc("mpo", g.tensor_tensor(
                                gsl_sb[:, 128:192], khi, stm2[:, 0:64],
                                ALU.mult), 1)
                            X.inc("mpo", g.tensor_tensor(
                                gsl_sb[:, 192:256], klo, stm2[:, 64:128],
                                ALU.mult), 1)
                            g.wait_ge(X.handle("mpo"), X.val("mpo"))
                            X.inc("mpo", g.tensor_sub(
                                klo, gsl_sb[:, 0:64], gsl_sb[:, 128:192]), 1)
                            X.inc("mpo", g.tensor_add(
                                khi, gsl_sb[:, 64:128], gsl_sb[:, 192:256]), 1)
                            g.wait_ge(X.handle("mpo"), X.val("mpo"))
                        g.wait_ge(X.handle("qg"), X.val("qg"))
                        X.inc("qg", g.dma_start(
                            out=k_scr[gi * 128:(gi + 1) * 128, :],
                            in_=kvring["k"][:, lt * 1024:(lt + 1) * 1024]), 16)
                    for lt in range(nGT):
                        gi = sl_i * nGT + lt
                        g.wait_ge(X.handle("mac"), mac0 + 7 * 4 + lt + 1)
                        g.wait_ge(X.handle("qg"), X.val("qg"))
                        X.inc("qg", g.dma_start(
                            out=v_scr[gi * 128:(gi + 1) * 128, :],
                            in_=kvring["v"][:, lt * 1024:(lt + 1) * 1024]), 16)
                kv_spill = ("qg", X.val("qg"))

                cdt = Chain(X, ("mpe", X.val("mpe")))
                cdt.op("act", lambda s: s.activation(
                    dtg["a"], ps[dt_bank][0:NH, :], AF.Abs,
                    bias=dtb_sb[:]))
                cdt.op("act", lambda s: s.activation(
                    dtg["a"], dtg["a"], AF.Exp, scale=-1.0))
                cdt.op("act", lambda s: s.activation(
                    dtg["a"], dtg["a"], AF.Ln,
                    bias=cbias[0:NH, 1:2]))
                cdt.op("act", lambda s: s.activation(
                    dtg["b"], ps[dt_bank][0:NH, :], AF.Relu,
                    bias=dtb_sb[:]))
                cdt.op("dve", lambda v: v.tensor_add(
                    dtg["b"], dtg["b"], dtg["a"]))
                cdt.op("pool", lambda g, row0=row0: g.dma_start(
                    out=dt_scr[:, row0:row0 + 512], in_=dtg["b"]),
                    dma=True)
                cdt.op("dve", lambda v: v.tensor_scalar(
                    dtg["a"], dtg["b"], expa_sb[:], -1.0,
                    ALU.mult, ALU.mult))
                for cc4 in range(4):
                    cdt.op("dve", lambda v, cc4=cc4: v.tensor_tensor_scan(
                        dtg["b"][0:NH, cc4 * 128:(cc4 + 1) * 128],
                        dtg["ones"][:],
                        dtg["a"][0:NH, cc4 * 128:(cc4 + 1) * 128],
                        0.0, ALU.mult, ALU.add))
                cdt.op("pool", lambda g, row0=row0: g.dma_start(
                    out=g_scr[:, row0:row0 + 512], in_=dtg["b"]),
                    dma=True)
                dt_deps.append(cdt.dep())
                prev_pe = ("mpe", X.val("mpe"))
                p0b_pump(10)
            gdt_dep = dt_deps[-1]
            for tn, src in (("G", g_scr), ("dt", dt_scr), ("kT", k_scr),
                            ("vT", v_scr)):
                if tn in tap_t:
                    cg = Chain(X, gdt_dep, kv_spill)
                    cg.op("pool", lambda g, tn=tn, src=src: g.dma_start(
                        out=tap_t[tn], in_=src), dma=True)

            col_dep = None
            for h in range(NH):
                chh = Chain(X, gdt_dep, col_dep)
                chh.op("act", lambda s, h=h: s.dma_start(
                    out=g16[:nT, :], in_=g_scr[h:h + 1, :]
                    .rearrange("o (n j) -> (o n) j", j=128)), dma=True)
                chh.op("pe", lambda t: t.transpose(
                    ps[5][0:128, 0:nT], g16[0:nT, :], c_id32[0:nT, 0:nT]))
                chh.op("dve", lambda v, h=h: v.tensor_copy(
                    gcols[:, h * 16:h * 16 + nT], ps[5][0:128, 0:nT]))
                chh.op("pe", lambda t: t.transpose(
                    ps[5][0:1, 256:256 + nT], g16[0:nT, 127:128],
                    c_id32[0:nT, 0:nT]))
                chh.op("dve", lambda v: v.tensor_copy(
                    grow[0:1, 0:nT], ps[5][0:1, 256:256 + nT]))
                chh.op("act", lambda s: s.activation(
                    grow[0:1, 32:32 + nT], grow[0:1, 0:nT], AF.Exp))
                chh.op("pe", lambda t: t.matmul(
                    ps[5][0:128, 256:256 + nT], c_ones[:], grow[0:1, 32:32 + nT],
                    start=True, stop=True))
                chh.op("dve", lambda v, h=h: v.tensor_copy(
                    cdcols[:, h * 16:h * 16 + nT], ps[5][0:128, 256:256 + nT]))
                chh.op("pe", lambda t: t.matmul(
                    ps[5][0:128, 256:256 + nT], c_ones[:], grow[0:1, 0:nT],
                    start=True, stop=True))
                chh.op("dve", lambda v, h=h: v.tensor_sub(
                    edcols[:, h * 16:h * 16 + nT], ps[5][0:128, 256:256 + nT],
                    gcols[:, h * 16:h * 16 + nT]))
                chh.op("act", lambda s, h=h: s.activation(
                    edcols[:, h * 16:h * 16 + nT],
                    edcols[:, h * 16:h * 16 + nT], AF.Exp))
                chh.op("act", lambda s, h=h: s.dma_start(
                    out=g16[:nT, :], in_=dt_scr[h:h + 1, :]
                    .rearrange("o (n j) -> (o n) j", j=128)), dma=True)
                chh.op("pe", lambda t: t.transpose(
                    ps[5][0:128, 0:nT], g16[0:nT, :], c_id32[0:nT, 0:nT]))
                chh.op("dve", lambda v, h=h: v.tensor_copy(
                    dtcols[:, h * 16:h * 16 + nT], ps[5][0:128, 0:nT]))
                chh.op("dve", lambda v, h=h: v.tensor_tensor(
                    edcols[:, h * 16:h * 16 + nT],
                    edcols[:, h * 16:h * 16 + nT],
                    dtcols[:, h * 16:h * 16 + nT], ALU.mult))
                col_dep = chh.dep()

            cch = Chain(X, col_dep, kv_spill)
            cch.op("dve", lambda v: v.memset(S_all[:], 0.0))
            for n in range(nT):
                ring = n % 2
                cch.op("act", lambda s, n=n, ring=ring: s.dma_start(
                    out=kvring["k"][:, ring * 1024:ring * 1024 + 1024],
                    in_=k_scr[n * 128:(n + 1) * 128, :]), dma=True)
                cch.op("act", lambda s, n=n, ring=ring: s.dma_start(
                    out=kvring["v"][:, ring * 1024:ring * 1024 + 1024],
                    in_=v_scr[n * 128:(n + 1) * 128, :]), dma=True)
                for h in range(NH):
                    kvh = h // 4
                    idx = h * 16 + n
                    ksl = kvring["k"][:, ring * 1024 + kvh * 128:
                                      ring * 1024 + kvh * 128 + 128]
                    vsl = kvring["v"][:, ring * 1024 + kvh * 128:
                                      ring * 1024 + kvh * 128 + 128]
                    cch.op("dve", lambda v, ksl=ksl, idx=idx: v.tensor_scalar_mul(
                        kdd_sb[:], ksl, edcols[:, idx:idx + 1]))
                    cch.op("pe", lambda t, vsl=vsl, h=h: t.matmul(
                        ps[h % 2][0:128, 0:128], kdd_sb[:], vsl,
                        start=True, stop=True))
                    cch.op("dve", lambda v, h=h, idx=idx: v.scalar_tensor_tensor(
                        S_all[:, h * 128:(h + 1) * 128],
                        S_all[:, h * 128:(h + 1) * 128],
                        cdcols[:, idx:idx + 1], ps[h % 2][0:128, 0:128],
                        ALU.mult, ALU.add))
            chain_dep = cch.dep()
            if "S" in tap_t:
                cs2 = Chain(X, chain_dep)
                cs2.op("pool", lambda g: g.dma_start(out=tap_t["S"],
                                                     in_=S_all[:]), dma=True)

            @block.gpsimd
            def _(g):
                g.wait_ge(X.handle(chain_dep[0]), chain_dep[1])
                g.wait_ge(X.handle(consts_v[0]), consts_v[1])
                g.wait_ge(X.handle("qg"), X.val("qg"))
                par = g.value_load(c_magic[0:1, 1:2])
                off = g.compute_val(par * (128 * NH * 128))
                pslot = AP(pay.tensor, off, [[NH * 128, 128], [1, NH * 128]])
                gv = X.inc("qg", g.dma_start(out=pslot, in_=S_all[:]), 16)
                g.wait_ge(X.handle("qg"), gv)
                foff = g.compute_val(par * 4)
                fslot = AP(flg.tensor, foff, [[4, 1], [1, 4]])
                gv = X.inc("qg", g.dma_start(out=fslot, in_=c_magic[:]), 16)
                g.wait_ge(X.handle("qg"), gv)
                f0 = AP(flg.tensor, 0, [[4, 1], [1, 1]])
                with g.register("creg") as creg, g.register("cnt") as cnt, \
                        g.register("itr") as itr:
                    g.reg_mov(cnt, 0)
                    g.reg_mov(itr, 0)

                    def poll():
                        g.reg_add(cnt, cnt, 16)
                        g.reg_add(itr, itr, 1)
                        g.dma_start(out=sflag[0:1, 0:1], in_=f0).then_inc(
                            X.handle("qpl"), 16)
                        g.wait_ge(X.handle("qpl"), cnt)
                        vv = g.value_load(sflag[0:1, 0:1])
                        mg = g.value_load(c_magic[0:1, 0:1])
                        g.reg_alu(creg, vv, mg, ALU.not_equal)
                        with g.If_cmp(itr, MAXPOLL, "IS_GE"):
                            g.reg_mov(creg, 0)
                    poll()
                    with g.While(creg):
                        poll()
                p0 = AP(pay.tensor, 0, [[NH * 128, 128], [1, NH * 128]])
                gv = X.inc("qg", g.dma_start(out=S_all[:], in_=p0), 16)
                g.wait_ge(X.handle("qg"), gv)
                X.inc("mpo", g.tensor_scalar_mul(S_all[:], S_all[:],
                                                 c_mask[:, 0:1]), 1)
                g.wait_ge(X.handle("mpo"), X.val("mpo"))
                X.inc("mpo", g.tensor_copy(Sb_host[:].bitcast(BF16)[:, 0:DM],
                                           S_all[:]), 1)
            return ("mpo", X.val("mpo"))

        def emit_attnb_groups(abt, sinit_dep, h3_dep):
            kvring = abt["kvring"]
            cols = abt["cols"]
            gcols, dtcols, edcols, cdcols = (cols["g"], cols["dt"], cols["ed"],
                                             cols["cd"])
            cosq, sinq = abt["cosq"], abt["sinq"]
            out_dep = None
            for grp in range(nS):
                row0 = grp * 512
                hfm_dep = emit_hfm_load(h_scr_a, row0, h3_dep,
                                        free_dep=("mpe", X.val("mpe")))

                mdv_guard = X.val("mdv")
                mac_guard = X.val("mac")
                mpo_guard = X.val("mpo")

                @block.sync
                def _(sy, row0=row0, mdv_guard=mdv_guard, mac_guard=mac_guard,
                      mpo_guard=mpo_guard):
                    sy.wait_ge(X.handle("mdv"), mdv_guard)
                    sy.wait_ge(X.handle("mac"), mac_guard)
                    sy.wait_ge(X.handle("mpo"), mpo_guard)
                    X.inc("qrt", sy.dma_start(
                        out=cosq[:], in_=ins["c_cosfm"][:, row0:row0 + 512]), 16)
                    X.inc("qrt", sy.dma_start(
                        out=sinq[:], in_=ins["c_sinfm"][:, row0:row0 + 512]), 16)
                rope_tab = ("qrt", X.val("qrt"))

                pe_base = X.val("mpe")
                mac0 = X.val("mac")
                qst0g = X.val("qst")
                pan_v = {}

                @block.sync
                def _(sy, grp=grp, pe_base=pe_base):
                    for j in range(2 * (DM // 256)):
                        pi, wi = j // 2, j % 2
                        nm = "wq" if wi == 0 else "wg"
                        hq = j % 2
                        free = None
                        if j >= 2:
                            free = ("mpe", pe_base + (j - 1) * 2)
                        pan_v[j] = load_panel_half(sy, nm, pi * 256, hq, free)

                @block.tensor
                def _(t, grp=grp, hfm_dep=hfm_dep, mac0=mac0):
                    t.wait_ge(X.handle(hfm_dep[0]), hfm_dep[1])
                    t.wait_ge(X.handle("mac"), mac0)
                    gidx = 0
                    for j in range(2 * (DM // 256)):
                        pi, wi = j // 2, j % 2
                        hq = j % 2
                        t.wait_ge(X.handle(pan_v[j][0]), pan_v[j][1])
                        for mc2 in range(2):
                            mh = pi * 2 + mc2
                            bank = gidx % 6
                            if gidx >= 6:
                                t.wait_ge(X.handle("mac"), mac0 + gidx - 5)
                            mm = None
                            for k in range(NKC):
                                mm = t.matmul(
                                    ps[bank][:],
                                    wh_slice(hq, k, mc2 * 128, 128),
                                    hfm[:, k * 512:(k + 1) * 512],
                                    start=(k == 0), stop=(k == NKC - 1))
                            X.inc("mpe", mm, 1)
                            gidx += 1

                gate_st = {}

                @block.scalar
                def _(s, grp=grp, pe_base=pe_base):
                    gidx = 0
                    for j in range(2 * (DM // 256)):
                        pi, wi = j // 2, j % 2
                        for mc2 in range(2):
                            mh = pi * 2 + mc2
                            bank = gidx % 6
                            s.wait_ge(X.handle("mpe"), pe_base + gidx + 1)
                            if wi == 0:
                                X.inc("mac", s.copy(
                                    x4[:, mh * 512:(mh + 1) * 512],
                                    ps[bank][:]), 1)
                            else:
                                if gidx >= 4:
                                    s.wait_ge(X.handle("qst"),
                                              gate_st.get(gidx - 4, 0))
                                X.inc("mac", s.copy(
                                    kvring["g2"][:, (gidx % 2) * 512:
                                                 (gidx % 2) * 512 + 512],
                                    ps[bank][:]), 1)
                            gidx += 1

                @block.gpsimd
                def _(g, grp=grp, mac0=mac0):
                    gidx = 0
                    for j in range(2 * (DM // 256)):
                        pi, wi = j // 2, j % 2
                        for mc2 in range(2):
                            mh = pi * 2 + mc2
                            if wi == 1:
                                g.wait_ge(X.handle("mac"), mac0 + gidx + 1)
                                g.wait_ge(X.handle("qst"), X.val("qst"))
                                X.inc("qst", g.dma_start(
                                    out=gate_scr[mh * 128:(mh + 1) * 128, :],
                                    in_=kvring["g2"][:, (gidx % 2) * 512:
                                                     (gidx % 2) * 512 + 512]), 16)
                                gate_st[gidx] = X.val("qst")
                            gidx += 1
                gate_done = ("qst", X.val("qst"))

                pe0r = X.val("mpe")
                mdv0r = X.val("mdv")

                @block.tensor
                def _(t, grp=grp, mac0=mac0, mdv0r=mdv0r):
                    t.wait_ge(X.handle("mac"), mac0 + 64)
                    for mh in range(NH):
                        _gq = (mh // 2) * 4 + (mh % 2)
                        if mh >= 2:
                            t.wait_ge(X.handle("mdv"), mdv0r + 3 * (mh - 1))
                        mm = t.matmul(ps[mh % 2][0:128, 0:512], c_protb[:],
                                      x4[:, mh * 512:(mh + 1) * 512],
                                      start=True, stop=True)
                        X.inc("mpe", mm, 1)

                @block.vector
                def _(v, grp=grp, pe0r=pe0r, qst0g=qst0g):
                    v.wait_ge(X.handle(rope_tab[0]), rope_tab[1])
                    v.wait_ge(X.handle("qst"), qst0g)
                    for mh in range(NH):
                        v.wait_ge(X.handle("mpe"), pe0r + mh + 1)
                        X.inc("mdv", v.tensor_tensor(
                            ep1[:, 0:512], x4[:, mh * 512:(mh + 1) * 512],
                            cosq[:], ALU.mult), 1)
                        X.inc("mdv", v.tensor_tensor(
                            ep1[:, 512:1024], ps[mh % 2][0:128, 0:512],
                            sinq[:], ALU.mult), 1)
                        X.inc("mdv", v.tensor_add(
                            x4[:, mh * 512:(mh + 1) * 512],
                            ep1[:, 0:512], ep1[:, 512:1024]), 1)
                if "q" in tap_t and grp == 0:
                    cq = Chain(X, ("mdv", X.val("mdv")))
                    cq.op("pool", lambda g: g.dma_start(
                        out=tap_t["q"], in_=x4[:, 0:NKC * 512]), dma=True)

                sc = Chain(X, ("mdv", X.val("mdv")), ("mac", X.val("mac")),
                           sinit_dep if grp == 0 else None,
                           *(wtl_done.get("wo") or ()))
                for nl in range(nGT):
                    n = grp * nGT + nl
                    ring = nl % 2
                    sc.op("act", lambda s, n=n, ring=ring: s.dma_start(
                        out=kvring["k"][:, ring * 1024:ring * 1024 + 1024],
                        in_=k_scr[n * 128:(n + 1) * 128, :]), dma=True)
                    sc.op("act", lambda s, n=n, ring=ring: s.dma_start(
                        out=kvring["v"][:, ring * 1024:ring * 1024 + 1024],
                        in_=v_scr[n * 128:(n + 1) * 128, :]), dma=True)
                    for kvh in range(NKV):
                        sc.op("pe", lambda t, ring=ring, kvh=kvh: t.transpose(
                            pbf[kvh % 2][0:128, 0:128],
                            kvring["k"][:, ring * 1024 + kvh * 128:
                                        ring * 1024 + kvh * 128 + 128],
                            c_idb[:]))
                        sc.op("dve", lambda v, kvh=kvh: v.tensor_copy(
                            kvring["kf"][:, kvh * 128:(kvh + 1) * 128],
                            pbf[kvh % 2][0:128, 0:128]))
                    for h in range(NH):
                        kvh = h // 4
                        idx = h * 16 + n
                        qsl = x4[:, h * 512 + nl * 128:h * 512 + nl * 128 + 128]
                        ksl = kvring["k"][:, ring * 1024 + kvh * 128:
                                          ring * 1024 + kvh * 128 + 128]
                        vsl = kvring["v"][:, ring * 1024 + kvh * 128:
                                          ring * 1024 + kvh * 128 + 128]
                        kfs = kvring["kf"][:, kvh * 128:(kvh + 1) * 128]
                        ssl = S_all[:, h * 128:(h + 1) * 128]
                        sbl = Sb(h)
                        sc.op("pe", lambda t, idx=idx: t.transpose(
                            ps[5][0:1, 0:128], gcols[:, idx:idx + 1], c_id32[:]))
                        sc.op("dve", lambda v: v.tensor_copy(
                            grow[0:1, 0:128], ps[5][0:1, 0:128]))
                        sc.op("pe", lambda t: t.matmul(
                            ps[3][0:128, 0:128], c_ones[:], grow[0:1, 0:128],
                            start=True, stop=True))
                        sc.op("dve", lambda v, idx=idx: v.scalar_tensor_tensor(
                            dec_sb[:], ps[3][0:128, 0:128], gcols[:, idx:idx + 1],
                            c_maskneg[:], ALU.subtract, ALU.add))
                        sc.op("act", lambda s: s.activation(
                            dec_sb[:], dec_sb[:], AF.Exp))
                        sc.op("act", lambda s: s.activation(
                            eg_sb[:], ps[3][0:128, 0:128], AF.Exp))
                        sc.op("dve", lambda v, qsl=qsl: v.tensor_tensor(
                            qex_sb[:], qsl, eg_sb[:], ALU.mult))
                        sc.op("pe", lambda t, kfs=kfs, qsl=qsl: t.matmul(
                            ps[0][0:128, 0:128], kfs, qsl, start=True, stop=True))
                        sc.op("dve", lambda v, idx=idx: v.scalar_tensor_tensor(
                            sdd_sb[:], ps[0][0:128, 0:128],
                            dtcols[:, idx:idx + 1], dec_sb[:],
                            ALU.mult, ALU.mult))

                        def ymm(t, vsl=vsl, sbl=sbl):
                            t.matmul(ps[1][0:128, 0:128], vsl, sdd_sb[:],
                                     start=True, stop=False)
                            return t.matmul(ps[1][0:128, 0:128], sbl, qex_sb[:],
                                            start=False, stop=True)
                        sc.op("pe", ymm)
                        sc.op("dve", lambda v, ksl=ksl, idx=idx:
                              v.tensor_scalar_mul(kdd_sb[:], ksl,
                                                  edcols[:, idx:idx + 1]))
                        sc.op("pe", lambda t, vsl=vsl: t.matmul(
                            ps[2][0:128, 0:128], kdd_sb[:], vsl,
                            start=True, stop=True))
                        sc.op("dve", lambda v: v.tensor_copy(
                            ytile[:], ps[1][0:128, 0:128]))
                        sc.op("dve", lambda v, ssl=ssl, idx=idx:
                              v.scalar_tensor_tensor(
                                  ssl, ssl, cdcols[:, idx:idx + 1],
                                  ps[2][0:128, 0:128], ALU.mult, ALU.add))
                        sc.op("dve", lambda v, ssl=ssl, sbl=sbl: v.tensor_copy(
                            sbl, ssl))
                        sc.op("act", lambda s: s.activation(
                            ysq_sb[:], ytile[:], AF.Square))
                        sc.op("pe", lambda t: t.matmul(
                            ps[4][0:128, 0:1], ysq_sb[:], c_onescol[:],
                            start=True, stop=True))
                        sc.op("act", lambda s: s.activation(
                            small[:, 0:1], ps[4][0:128, 0:1], AF.Ln,
                            scale=1.0 / DH, bias=cbias[:, 2:3]))
                        sc.op("act", lambda s: s.activation(
                            small[:, 1:2], small[:, 0:1], AF.Exp, scale=-0.5))
                        sc.op("pe", lambda t: t.transpose(
                            ps[5][0:1, 128:256], small[:, 1:2], c_id32[:]))
                        sc.op("dve", lambda v: v.tensor_copy(
                            grow[0:1, 0:128], ps[5][0:1, 128:256]))
                        sc.op("pe", lambda t: t.matmul(
                            ps[3][0:128, 0:128], c_ones[:], grow[0:1, 0:128],
                            start=True, stop=True))
                        sc.op("dve", lambda v, qsl=qsl: v.scalar_tensor_tensor(
                            qsl, ps[3][0:128, 0:128], gnw_sb[:, 0:1],
                            ytile[:], ALU.mult, ALU.mult))
                gp = Chain(X, sc.dep(), gate_done)
                for cblk in range(NKC):
                    rr = cblk % 2
                    gp.op("act", lambda s, cblk=cblk, rr=rr: s.dma_start(
                        out=kvring["g2"][:, rr * 512:rr * 512 + 512],
                        in_=gate_scr[cblk * 128:(cblk + 1) * 128, :]), dma=True)
                    gp.op("act", lambda s, rr=rr: s.activation(
                        gsl_sb[:], kvring["g2"][:, rr * 512:rr * 512 + 512],
                        AF.Silu))
                    gp.op("dve", lambda v, cblk=cblk: v.tensor_tensor(
                        x4[:, cblk * 512:(cblk + 1) * 512],
                        x4[:, cblk * 512:(cblk + 1) * 512], gsl_sb[:], ALU.mult))
                if "y" in tap_t and grp == 0:
                    cy = Chain(X, gp.dep())
                    cy.op("pool", lambda g: g.dma_start(
                        out=tap_t["y"], in_=x4[:, 0:NKC * 512]), dma=True)
                out_dep = emit_outproj("wo", 512, row0, gp.dep(),
                                       resid_b, resid_a, False)
            return out_dep

        for nm, _M in WSPEC:
            p0b_enqueue(nm)
        p0b_mark_finals()
        p0b_pump(40)
        x1_dep = emit_p1()
        p0b_drain("w12")
        fin_src = None
        ab_dep = None
        if upto == "p1":
            fin_src = resid_a
        else:
            h2_dep = emit_ln_sweep(resid_a, h_scr_b, x1_dep,
                                   taph="h2" if "h2" in tap_t else None)
            p0b_drain("w3")
            op_dep = None
            for sl_i in range(nS):
                hfm_dep = emit_hfm_load(h_scr_b, sl_i * 512, h2_dep,
                                        free_dep=None if sl_i == 0 else
                                        ("mpe", X.val("mpe")))
                g1 = emit_gemm1(hfm_dep)
                p0b_pump(14)
                op_dep = emit_outproj("w3", 512, sl_i * 512, g1,
                                      resid_a, resid_b, True, extra_pump=8)
            if upto == "p2":
                fin_src = resid_b
            else:
                h3_dep = emit_ln_sweep(resid_b, h_scr_a, op_dep,
                                       tapx="x2" if "x2" in tap_t else None,
                                       taph="h3" if "h3" in tap_t else None)
                p0b_drain("wk")
                p0b_drain("wv")
                with ExitStack() as abes:
                    def absb(name, shape, dt_):
                        return abes.enter_context(
                            nc.sbuf_tensor(name, shape, dt_))
                    abt = {}
                    abt["kvring"] = {"k": absb("ab_k", [128, 4096], BF16),
                                     "v": absb("ab_v", [128, 4096], BF16),
                                     "kf": absb("ab_kf", [128, 1024], BF16),
                                     "g2": absb("ab_g2", [128, 1024], BF16)}
                    abt["cols"] = {"g": absb("ab_gc", [128, NH * 16], F32),
                                   "dt": absb("ab_dc", [128, NH * 16], F32),
                                   "ed": absb("ab_ec", [128, NH * 16], F32),
                                   "cd": absb("ab_cc", [128, NH * 16], F32)}
                    abt["ctm2"] = absb("ab_ctm", [128, 128], F32)
                    abt["stm2"] = absb("ab_stm", [128, 128], F32)
                    abt["cosq"] = gsl_sb
                    abt["sinq"] = lnsq
                    abt["dtg"] = {"a": sil[0:NH, 0:512],
                                  "b": sil[0:NH, 512:1024],
                                  "ones": absb("ab_dt1", [NH, 128], F32)}
                    cinit = Chain(X, consts_v)
                    cinit.op("dve", lambda v: v.memset(abt["dtg"]["ones"][:],
                                                       1.0))
                    sinit_dep = emit_attnb(abt, h3_dep)
                    p0b_drain("wq")
                    p0b_drain("wg")
                    p0b_drain("wo")
                    if upto == "p4a":
                        fin_src = resid_b
                    else:
                        ab_dep = emit_attnb_groups(abt, sinit_dep, h3_dep)
                        if upto == "p4":
                            fin_src = resid_a
                if fin_src is None:
                    h4_dep = emit_ln_sweep(resid_a, h_scr_b, ab_dep,
                                           tapx="x3" if "x3" in tap_t else None)
                    for sl_i in range(nS):
                        hfm_dep = emit_hfm_load(h_scr_b, sl_i * 512, h4_dep,
                                                free_dep=None if sl_i == 0 else
                                                ("mpe", X.val("mpe")))
                        g1 = emit_gemm1(hfm_dep)
                        emit_outproj("w3", 512, sl_i * 512, g1,
                                     resid_a, Y_out, True)

        p0b_pump(len(p0b_jobs))

        @block.gpsimd
        def _(g):
            if fin_src is not None:
                g.wait_ge(X.handle("qst"), X.val("qst"))
                g.wait_ge(X.handle("mdv"), X.val("mdv"))
                g.wait_ge(X.handle("mac"), X.val("mac"))
                for i in range(nT):
                    g.wait_ge(X.handle("qg"), X.val("qg"))
                    X.inc("qg", g.dma_start(
                        out=Y_out[i * 128:(i + 1) * 128, :],
                        in_=fin_src[i * 128:(i + 1) * 128, :]), 16)
            g.wait_ge(X.handle("qg"), X.val("qg"))
            g.wait_ge(X.handle("qst"), X.val("qst"))
            g.wait_ge(X.handle("qtp"), X.val("qtp"))

        @block.sync
        def _(sy):
            sy.wait_ge(X.handle("qs"), X.val("qs"))
    return nc



def _host_consts(T, half, rng):
    c = {}
    c["c_ones"] = np.ones((1, 128), np.float32)
    c["c_onescol"] = np.ones((128, 1), np.float32)
    c["c_id32"] = np.eye(128, dtype=np.float32)
    c["c_rev"] = np.eye(128, dtype=np.float32)[::-1].copy()
    i_idx = np.arange(128)
    c["c_maskneg"] = np.where(i_idx[None, :] >= i_idx[:, None], 0.0,
                              -1e30).astype(np.float32)
    prot = np.zeros((128, 128), np.float32)
    for dp in range(64):
        prot[dp, dp + 64] = -1.0
    for dp in range(64, 128):
        prot[dp, dp - 64] = 1.0
    c["c_prot"] = prot.T.copy()
    k = np.arange(4096).reshape(64, 64)
    c["c_omr"] = np.cos(2 * np.pi * k / 4096).astype(np.float32)
    c["c_omi"] = np.sin(2 * np.pi * k / 4096).astype(np.float32)
    kn = np.arange(64)
    F = np.exp(2j * np.pi * kn[:, None] * kn[None, :] / 64)
    c["c_fr"] = F.real.astype(np.float32)
    c["c_fi"] = F.imag.astype(np.float32)
    c["c_fni"] = (-F.imag).astype(np.float32)
    TW = np.exp(2j * np.pi * kn[:, None] * kn[None, :] / 4096)
    c["c_twr"] = TW.real.astype(np.float32)
    c["c_twi"] = TW.imag.astype(np.float32)
    pos0 = half * T
    invf = (10000.0 ** (-np.arange(0, 128, 2) / 128.0))
    pos = pos0 + np.arange(T)
    ang = pos[:, None] * invf[None, :]
    cos_t = np.concatenate([np.cos(ang), np.cos(ang)], 1).astype(np.float32)
    sin_t = np.concatenate([np.sin(ang), np.sin(ang)], 1).astype(np.float32)
    c["c_cosfm"] = cos_t.T.copy()
    c["c_sinfm"] = sin_t.T.copy()
    nT = T // 128
    ctm = np.zeros((128, T), np.float32)
    stm = np.zeros((128, T), np.float32)
    for gi in range(nT):
        ctm[:, gi * 128:(gi + 1) * 128] = cos_t[gi * 128:(gi + 1) * 128, :]
        stm[:, gi * 128:(gi + 1) * 128] = sin_t[gi * 128:(gi + 1) * 128, :]
    c["c_costm"] = ctm
    c["c_sintm"] = stm
    c["c_mask"] = np.full((128, 1), 1.0 if half == 1 else 0.0, np.float32)
    cb = np.zeros((128, 4), np.float32)
    cb[:, 0] = EPS_LN
    cb[:, 1] = 1.0
    cb[:, 2] = EPS_RMS
    c["c_bias"] = cb
    magic = int(rng.integers(1, 2 ** 31 - 1))
    c["c_magic"] = np.array([[magic, half, 0, 0]], np.int32)
    return c


def build_inmaps(inputs, T):
    rng = np.random.default_rng(abs(int(
        np.float64(np.asarray(inputs["L0_attn_a_D"],
                              np.float32).reshape(-1)[0]) * 1e9)) % (2 ** 31)
        + 7919)
    u = np.ascontiguousarray(np.asarray(inputs["L0_attn_a_u"], np.float32))
    maps = []
    for core in range(8):
        b, half = core // 2, core % 2
        m = {"x": np.ascontiguousarray(u[b, half * T:(half + 1) * T, :])}
        m["a_lam"] = np.ascontiguousarray(inputs["L0_attn_a_Lambda_ri"],
                                          np.float32)
        m["a_p"] = np.ascontiguousarray(inputs["L0_attn_a_p_ri"], np.float32)
        m["a_q"] = np.ascontiguousarray(inputs["L0_attn_a_q_ri"], np.float32)
        m["a_vc"] = np.ascontiguousarray(
            inputs["L0_attn_a_Vc_ri"], np.float32).reshape(64, 128)
        m["a_ct"] = np.ascontiguousarray(
            inputs["L0_attn_a_Ct_ri"], np.float32).reshape(64, 2)
        m["a_bp"] = np.ascontiguousarray(inputs["L0_attn_a_Bp"], np.float32)
        m["a_d"] = np.ascontiguousarray(inputs["L0_attn_a_D"],
                                        np.float32).reshape(1, 1)
        m["a_ls"] = np.ascontiguousarray(inputs["L0_attn_a_log_step"],
                                         np.float32).reshape(1, 1)
        m["wq"] = np.ascontiguousarray(inputs["L0_attn_b_Wq"], np.float32)
        m["wk"] = np.ascontiguousarray(inputs["L0_attn_b_Wk"], np.float32)
        m["wv"] = np.ascontiguousarray(inputs["L0_attn_b_Wv"], np.float32)
        m["wo"] = np.ascontiguousarray(inputs["L0_attn_b_Wo"], np.float32)
        m["wg"] = np.ascontiguousarray(inputs["L0_attn_b_Wg"], np.float32)
        m["win"] = np.ascontiguousarray(inputs["L0_attn_b_Win"], np.float32)
        m["dtb"] = np.ascontiguousarray(inputs["L0_attn_b_dt_bias"],
                                        np.float32).reshape(NH, 1)
        m["alb"] = np.ascontiguousarray(inputs["L0_attn_b_A_log_bias"],
                                        np.float32).reshape(NH, 1)
        m["gnw"] = np.ascontiguousarray(inputs["L0_attn_b_gnorm_w"],
                                        np.float32).reshape(DH, 1)
        m["w12"] = np.ascontiguousarray(inputs["L0_ffn_w12"], np.float32)
        m["b12"] = np.ascontiguousarray(
            inputs["L0_ffn_b12"], np.float32).reshape(64, 128).T.copy()
        m["w3"] = np.ascontiguousarray(inputs["L0_ffn_w3"], np.float32)
        m["b3"] = np.ascontiguousarray(inputs["L0_ffn_b3"],
                                       np.float32).reshape(1, DM)
        m.update(_host_consts(T, half, rng))
        maps.append(m)
    return maps


def kernel(**inputs):
    T = 2048
    nc = build(T=T, upto="all")
    maps = build_inmaps(inputs, T)
    res = run_bass_kernel_spmd(nc, maps, core_ids=list(range(8))).results
    out = np.empty((4, 2 * T, DM), np.float32)
    for core in range(8):
        b, half = core // 2, core % 2
        out[b, half * T:(half + 1) * T, :] = res[core]["y"]
    return out


# version 05
# speedup vs baseline: 1.1355x; 1.1355x over previous
import numpy as np
import ml_dtypes
import concourse.bass as bass
import concourse.mybir as mybir
from concourse.bass_utils import run_bass_kernel_spmd
from concourse.ap import AP
from contextlib import ExitStack

F32 = mybir.dt.float32
BF16 = mybir.dt.bfloat16
I32 = mybir.dt.int32
AF = mybir.ActivationFunctionType
ALU = mybir.AluOpType
AX = mybir.AxisListType

DM = 4096
DFF = 8192
NH = 32
NKV = 8
DH = 128
NKC = DM // 128
EPS_LN = 1e-5
EPS_RMS = 1e-5
MAXPOLL = 3_000_000

ENG_ATTR = {"pe": "tensor", "dve": "vector", "act": "scalar",
            "pool": "gpsimd", "sp": "sync"}
ENG_SEM = {"pe": "mpe", "dve": "mdv", "act": "mac", "pool": "mpo", "sp": "msp"}
DMA_SEM = {"pool": "qg", "sp": "qs", "act": "qa"}


class Ctx:
    def __init__(self, nc, block, es):
        self.nc, self.block, self.es = nc, block, es
        self.sems = {}

    def sem(self, name):
        if name not in self.sems:
            self.sems[name] = [self.es.enter_context(self.nc.semaphore(name)), 0]
        return self.sems[name]

    def inc(self, name, instr, n=1):
        s = self.sem(name)
        instr.then_inc(s[0], n)
        s[1] += n
        return s[1]

    def val(self, name):
        return self.sem(name)[1]

    def handle(self, name):
        return self.sem(name)[0]


class Chain:

    def __init__(self, ctx, *deps):
        self.ctx = ctx
        self.prev = [d for d in deps if d is not None]

    def op(self, eng, fn, dma=False, also=()):
        ctx = self.ctx
        prev = self.prev + [a for a in also if a is not None]
        sem = DMA_SEM[eng] if dma else ENG_SEM[eng]
        ser = ctx.val(sem) if dma else None
        dec = getattr(ctx.block, ENG_ATTR[eng])
        out = {}

        def body(e):
            if ser:
                e.wait_ge(ctx.handle(sem), ser)
            for (snm, sval) in prev:
                e.wait_ge(ctx.handle(snm), sval)
            out["v"] = ctx.inc(sem, fn(e), 16 if dma else 1)
        dec(body)
        self.prev = [(sem, out["v"])]
        return self.prev[0]

    def dep(self):
        return self.prev[0]


def build(T=2048, taps=(), upto="all"):
    nT = T // 128
    nS = T // 512
    nGT = 4

    nc = bass.Bass("TRN2", target_bir_lowering=False)

    ins = {}
    for nm, shp in [("x", [T, DM]),
                    ("a_lam", [64, 2]), ("a_p", [64, 2]), ("a_q", [64, 2]),
                    ("a_vc", [64, 128]), ("a_ct", [64, 2]), ("a_bp", [64, 1]),
                    ("a_d", [1, 1]), ("a_ls", [1, 1]),
                    ("wq", [DM, DM]), ("wk", [1024, DM]), ("wv", [1024, DM]),
                    ("wo", [DM, DM]), ("wg", [DM, DM]), ("win", [NH, DM]),
                    ("dtb", [NH, 1]), ("alb", [NH, 1]), ("gnw", [DH, 1]),
                    ("w12", [DFF, DM]), ("b12", [128, 64]), ("w3", [DM, DM]),
                    ("b3", [1, DM]),
                    ("c_ones", [1, 128]), ("c_onescol", [128, 1]),
                    ("c_id32", [128, 128]), ("c_rev", [128, 128]),
                    ("c_maskneg", [128, 128]), ("c_prot", [128, 128]),
                    ("c_omr", [64, 64]), ("c_omi", [64, 64]),
                    ("c_fr", [64, 64]), ("c_fi", [64, 64]), ("c_fni", [64, 64]),
                    ("c_twr", [64, 64]), ("c_twi", [64, 64]),
                    ("c_cosfm", [128, T]), ("c_sinfm", [128, T]),
                    ("c_costm", [128, T]), ("c_sintm", [128, T]),
                    ("c_mask", [128, 1]), ("c_bias", [128, 4])]:
        ins[nm] = nc.dram_tensor(nm, shp, F32, kind="ExternalInput").ap()
    ins["c_magic"] = nc.dram_tensor("c_magic", [1, 4], I32, kind="ExternalInput").ap()
    X_in = ins["x"]
    Y_out = nc.dram_tensor("y", [T, DM], F32, kind="ExternalOutput").ap()

    WSPEC = [("w12", DFF), ("w3", DM), ("wq", DM), ("wk", 1024),
             ("wv", 1024), ("wg", DM), ("wo", DM)]
    WM = dict(WSPEC)
    wbf = {nm: nc.dram_tensor(f"{nm}_bf", [M, DM], BF16, kind="Internal").ap()
           for nm, M in WSPEC}
    wtl = {nm: nc.dram_tensor(f"{nm}_t", [DM, M], BF16, kind="Internal").ap()
           for nm, M in WSPEC}
    Zbuf = nc.dram_tensor("zbuf", [1, 4352], BF16, kind="Internal").ap()
    resid_a = nc.dram_tensor("resid_a", [T, DM], F32, kind="Internal").ap()
    resid_b = nc.dram_tensor("resid_b", [T, DM], F32, kind="Internal").ap()
    h_scr_a = nc.dram_tensor("h_scr_a", [T, DM], BF16, kind="Internal").ap()
    h_scr_b = nc.dram_tensor("h_scr_b", [T, DM], BF16, kind="Internal").ap()
    k_scr = nc.dram_tensor("k_scr", [T, 1024], BF16, kind="Internal").ap()
    v_scr = nc.dram_tensor("v_scr", [T, 1024], BF16, kind="Internal").ap()
    gate_scr = nc.dram_tensor("gate_scr", [DM, 512], BF16, kind="Internal").ap()
    g_scr = nc.dram_tensor("g_scr", [NH, T], F32, kind="Internal").ap()
    dt_scr = nc.dram_tensor("dt_scr", [NH, T], F32, kind="Internal").ap()
    grows_d = nc.dram_tensor("k_grows", [4, DM], F32, kind="Internal").ap()
    crows_d = nc.dram_tensor("k_crows", [16, DM], F32, kind="Internal").ap()
    pay = nc.dram_tensor("pay", [2 * 128, NH * 128], F32, kind="Internal",
                         addr_space="Shared").ap()
    flg = nc.dram_tensor("flg", [2, 4], I32, kind="Internal",
                         addr_space="Shared").ap()

    tap_t = {}
    TAPSPEC = {"h1": ([T, DM], BF16), "x1": ([T, DM], F32), "K": ([64, 64], F32),
               "h2": ([T, DM], BF16), "x2": ([T, DM], F32), "h3": ([T, DM], BF16),
               "x3": ([T, DM], F32), "kT": ([T, 1024], BF16),
               "vT": ([T, 1024], BF16), "G": ([NH, T], F32),
               "dt": ([NH, T], F32), "S": ([128, NH * 128], F32),
               "q": ([128, NKC * 512], BF16), "y": ([128, NKC * 512], BF16)}
    for tn in taps:
        shp, dt_ = TAPSPEC[tn]
        tap_t[tn] = nc.dram_tensor(f"tap_{tn}", shp, dt_, kind="ExternalOutput").ap()

    es = ExitStack()
    with es:
        def sbuf(name, shape, dt_):
            return es.enter_context(nc.sbuf_tensor(name, shape, dt_))

        c_ones = sbuf("s_ones", [1, 128], F32)
        c_onescol = sbuf("s_onescol", [128, 1], F32)
        c_id32 = sbuf("s_id32", [128, 128], F32)
        c_idb = sbuf("s_idb", [128, 128], BF16)
        c_revb = sbuf("s_revb", [128, 128], BF16)
        c_protb = sbuf("s_protb", [128, 128], BF16)
        c_maskneg = sbuf("s_maskneg", [128, 128], F32)
        c_mask = sbuf("s_mask", [128, 1], F32)
        cbias = sbuf("s_cbias", [128, 4], F32)
        c_magic = sbuf("s_magic", [1, 4], I32)
        b12_sb = sbuf("s_b12", [128, 64], F32)
        b3bf = sbuf("s_b3bf", [1, DM], BF16)
        c_onesb = sbuf("s_onesb", [1, 128], BF16)
        dtb_sb = sbuf("s_dtb", [NH, 1], F32)
        expa_sb = sbuf("s_expa", [NH, 1], F32)
        gnw_sb = sbuf("s_gnw", [DH, 1], F32)
        winT = sbuf("s_winT", [128, NKC * NH], BF16)
        wpan = sbuf("s_wpan", [128, 16384], BF16)
        hfm = sbuf("s_hfm", [128, NKC * 512], BF16)
        x4 = sbuf("s_x4", [128, NKC * 512], BF16)
        p0t = [sbuf(f"s_p0t{i}", [128, 4096], BF16) for i in range(2)]
        xt = [sbuf(f"s_xt{i}", [128, DM], F32) for i in range(2)]
        ep1 = sbuf("s_ep1", [128, 1024], F32)
        sil = sbuf("s_sil", [128, 1024], F32)
        rsl = [sbuf(f"s_rsl{i}", [128, 512], F32) for i in range(2)]
        hbt = sbuf("s_hbt", [128, 512], BF16)
        stat = sbuf("s_stat", [128, 16], F32)
        ytile = sbuf("s_ytile", [128, 128], F32)
        dec_sb = sbuf("s_dec", [128, 128], F32)
        eg_sb = sbuf("s_eg", [128, 128], F32)
        sdd_sb = sbuf("s_sdd", [128, 128], BF16)
        kdd_sb = sbuf("s_kdd", [128, 128], BF16)
        qex_sb = sbuf("s_qex", [128, 128], BF16)
        ysq_sb = sbuf("s_ysq", [128, 128], F32)
        gsl_sb = sbuf("s_gsl", [128, 512], F32)
        lnsq = sbuf("s_lnsq", [128, 512], F32)
        grow = sbuf("s_grow", [1, 128], F32)
        g16 = sbuf("s_g16", [16, 128], F32)
        small = sbuf("s_small", [128, 8], F32)
        sflag = sbuf("s_sflag", [1, 4], I32)
        S_all = xt[0]
        Sb_host = xt[1]

        def Sb(h):
            return Sb_host[:].bitcast(BF16)[:, h * 128:(h + 1) * 128]

        ps = [es.enter_context(nc.psum_tensor(f"ps{i}", [128, 512], F32))
              for i in range(6)]
        pbf = [es.enter_context(nc.psum_tensor(f"pbf{i}", [128, 1024], BF16))
               for i in range(2)]

        block = es.enter_context(nc.Block())
        X = Ctx(nc, block, es)
        for s_ in list(DMA_SEM.values()) + list(ENG_SEM.values()):
            X.sem(s_)
        for i in range(2):
            X.sem(f"qx{i}")
        for i in range(4):
            X.sem(f"qw{i}")
        for i in range(2):
            X.sem(f"qr{i}")
        X.sem("qhf")
        X.sem("qst")
        X.sem("qtp")
        X.sem("qcast")
        X.sem("qpl")
        for i in range(2):
            X.sem(f"qpt{i}")
            X.sem(f"qpw{i}")
        X.sem("qb3")
        X.sem("qrt")

        X.sem("qc")

        @block.sync
        def _(sy):
            for dst, key in [(c_ones, "c_ones"), (c_onescol, "c_onescol"),
                             (c_id32, "c_id32"), (c_maskneg, "c_maskneg"),
                             (c_mask, "c_mask"), (cbias, "c_bias"),
                             (b12_sb, "b12"),
                             (dtb_sb, "dtb"), (expa_sb, "alb"), (gnw_sb, "gnw")]:
                X.inc("qc", sy.dma_start(out=dst[:], in_=ins[key]), 16)
            X.inc("qc", sy.dma_start(out=c_magic[:], in_=ins["c_magic"]), 16)
        consts_v = ("qc", X.val("qc"))

        cast_done = {}

        @block.gpsimd
        def _(g):
            for nm, M in WSPEC:
                g.wait_ge(X.handle("qcast"), X.val("qcast"))
                cast_done[nm] = ("qcast",
                                 X.inc("qcast", g.dma_start(out=wbf[nm],
                                                            in_=ins[nm]), 16))

        ch = Chain(X, consts_v)
        ch.op("dve", lambda v: v.tensor_copy(c_idb[:], c_id32[:]))
        idb_v = ch.dep()
        ch.op("sp", lambda sy: sy.dma_start(out=xt[0][:, 0:128], in_=ins["c_rev"]),
              dma=True)
        ch.op("dve", lambda v: v.tensor_copy(c_revb[:], xt[0][:, 0:128]))
        ch.op("sp", lambda sy: sy.dma_start(out=xt[0][:, 0:128], in_=ins["c_prot"]),
              dma=True)
        ch.op("dve", lambda v: v.tensor_copy(c_protb[:], xt[0][:, 0:128]))
        ch.op("act", lambda s: s.activation(expa_sb[:], expa_sb[:], AF.Exp))
        ch.op("sp", lambda sy: sy.dma_start(out=xt[0][0:1, :], in_=ins["b3"]),
              dma=True)
        ch.op("dve", lambda v: v.tensor_copy(b3bf[:], xt[0][0:1, :]))
        ch.op("dve", lambda v: v.tensor_copy(c_onesb[:], c_ones[:]))
        ch.op("sp", lambda sy: sy.dma_start(out=xt[0][0:NH, :], in_=ins["win"]),
              dma=True)
        ch.op("act", lambda s: s.copy(x4[0:NH, 0:DM], xt[0][0:NH, :]))
        for kc in range(NKC):
            ch.op("pe", lambda t, kc=kc: t.transpose(
                pbf[kc % 2][0:128, 0:NH], x4[0:NH, kc * 128:(kc + 1) * 128],
                c_idb[0:NH, 0:NH]), also=[idb_v])
            ch.op("dve", lambda v, kc=kc: v.tensor_copy(
                winT[:, kc * NH:(kc + 1) * NH], pbf[kc % 2][0:128, 0:NH]))
        prep_v = ch.dep()

        wtl_done = {}
        p0t_last = [0, 0]
        p0b_jobs = []
        p0b_idx = [0]
        p0b_ring = [0]
        p0b_final = {}

        def p0b_enqueue(nm):
            M = WM[nm]
            for kc in range(NKC):
                if M > 4096:
                    p0b_jobs.append((nm, kc, 0, 4096))
                    p0b_jobs.append((nm, kc, 4096, M - 4096))
                else:
                    p0b_jobs.append((nm, kc, 0, M))

        def p0b_mark_finals():
            for nm, M in WSPEC:
                last = [j for j in p0b_jobs if j[0] == nm][-1]
                p0b_final[nm] = last

        def p0b_pump(njobs):
            end = min(p0b_idx[0] + njobs, len(p0b_jobs))
            if end <= p0b_idx[0]:
                return
            jobs = p0b_jobs[p0b_idx[0]:end]
            p0b_idx[0] = end

            @block.sync
            def _(sy):
                for job in jobs:
                    nm, kc, m0, mw = job
                    ring = p0b_ring[0]
                    p0b_ring[0] ^= 1
                    sy.wait_ge(X.handle(cast_done[nm][0]), cast_done[nm][1])
                    if p0t_last[ring]:
                        sy.wait_ge(X.handle(f"qpw{ring}"), p0t_last[ring])
                    tv = X.inc(f"qpt{ring}", sy.dma_start(
                        out=p0t[ring][:, 0:mw],
                        in_=wbf[nm][m0:m0 + mw, kc * 128:(kc + 1) * 128],
                        transpose=True), 16)
                    sy.wait_ge(X.handle(f"qpt{ring}"), tv)
                    p0t_last[ring] = X.inc(f"qpw{ring}", sy.dma_start(
                        out=wtl[nm][kc * 128:(kc + 1) * 128, m0:m0 + mw],
                        in_=p0t[ring][:, 0:mw]), 16)
                    if job == p0b_final.get(nm):
                        wtl_done[nm] = tuple(
                            ("qpw%d" % r, X.val("qpw%d" % r)) for r in (0, 1)
                            if X.val("qpw%d" % r) > 0)

        def p0b_drain(nm):
            while nm not in wtl_done:
                p0b_pump(8)

        def load_panel_half(sy, nm, m0, hq, free_dep):
            if free_dep is None:
                free_dep = ("mpe", X.val("mpe"))
            sy.wait_ge(X.handle(free_dep[0]), free_dep[1])
            for dsem, dval in wtl_done[nm]:
                sy.wait_ge(X.handle(dsem), dval)
            M = WM[nm]
            sem = f"qw{hq}"
            for cc in range(4):
                src = AP(wtl[nm].tensor, (cc * 8 * 128) * M + m0,
                         [[M, 128], [128 * M, 8], [1, 256]])
                dst = AP(wpan[:].tensor, hq * 8192 + cc * 8 * 256,
                         [[16384, 128], [256, 8], [1, 256]])
                X.inc(sem, sy.dma_start(out=dst, in_=src), 16)
            return (sem, X.val(sem))

        def wh_slice(hq, k, off=0, w=256):
            return wpan[:, hq * 8192 + k * 256 + off:hq * 8192 + k * 256 + off + w]

        def load_panel_quarter(sy, nm, m0, q, free_dep):
            if free_dep is None:
                free_dep = ("mpe", X.val("mpe"))
            sy.wait_ge(X.handle(free_dep[0]), free_dep[1])
            for dsem, dval in wtl_done[nm]:
                sy.wait_ge(X.handle(dsem), dval)
            M = WM[nm]
            sem = f"qw{q}"
            for cc in range(2):
                src = AP(wtl[nm].tensor, (cc * 16 * 128) * M + m0,
                         [[M, 128], [128 * M, 16], [1, 128]])
                dst = AP(wpan[:].tensor, q * 4096 + cc * 16 * 128,
                         [[16384, 128], [128, 16], [1, 128]])
                X.inc(sem, sy.dma_start(out=dst, in_=src), 16)
            return (sem, X.val(sem))

        def wqt_slice(q, k):
            return wpan[:, q * 4096 + k * 128:q * 4096 + (k + 1) * 128]

        def emit_hfm_load(h_src, row0, dep, free_dep=None):
            @block.sync
            def _(sy):
                for kc in range(NKC):
                    if dep is not None:
                        sy.wait_ge(X.handle(dep[0]), dep[1])
                    if kc == 0 and free_dep is not None:
                        sy.wait_ge(X.handle(free_dep[0]), free_dep[1])
                    X.inc("qhf", sy.dma_start(
                        out=hfm[:, kc * 512:(kc + 1) * 512],
                        in_=h_src[row0:row0 + 512, kc * 128:(kc + 1) * 128],
                        transpose=True), 16)
            return ("qhf", X.val("qhf"))

        def emit_ln_sweep(x_src, h_dst, dep, tapx=None, taph=None):
            prev_store = None
            xt_cons = []
            for i in range(nT):
                ring = i % 2

                @block.sync
                def _(sy, i=i, ring=ring):
                    sy.wait_ge(X.handle(dep[0]), dep[1])
                    if i >= 2:
                        sy.wait_ge(X.handle("mac"), xt_cons[i - 2])
                    X.inc(f"qx{ring}", sy.dma_start(
                        out=xt[ring][:], in_=x_src[i * 128:(i + 1) * 128, :]), 16)
                xdep = (f"qx{ring}", X.val(f"qx{ring}"))
                c = Chain(X, xdep)
                s1d = c.op("dve", lambda v, ring=ring: v.reduce_sum(
                    stat[:, 0:1], xt[ring][:], axis=AX.X))
                c2 = Chain(X, xdep)
                for sq in range(8):
                    c2.op("act", lambda s, ring=ring, sq=sq: s.activation(
                        lnsq[:], xt[ring][:, sq * 512:(sq + 1) * 512],
                        AF.Square, accum_out=stat[:, 8 + sq:9 + sq]))
                c2.op("dve", lambda v: v.reduce_sum(stat[:, 1:2], stat[:, 8:16],
                                                    axis=AX.X), also=[s1d])
                c2.op("dve", lambda v: v.tensor_scalar(stat[:, 2:3], stat[:, 0:1],
                                                       1.0 / DM, None, ALU.mult))
                c2.op("dve", lambda v: v.tensor_tensor(stat[:, 3:4], stat[:, 2:3],
                                                       stat[:, 2:3], ALU.mult))
                c2.op("dve", lambda v: v.scalar_tensor_tensor(
                    stat[:, 4:5], stat[:, 1:2], 1.0 / DM, stat[:, 3:4],
                    ALU.mult, ALU.subtract))
                c2.op("act", lambda s: s.activation(stat[:, 5:6], stat[:, 4:5],
                                                    AF.Ln, bias=cbias[:, 0:1]))
                c2.op("act", lambda s: s.activation(stat[:, 6:7], stat[:, 5:6],
                                                    AF.Exp, scale=-0.5))
                c2.op("dve", lambda v: v.scalar_tensor_tensor(
                    stat[:, 7:8], stat[:, 2:3], -1.0, stat[:, 6:7],
                    ALU.mult, ALU.mult))
                nv = None
                for sq in range(8):
                    nv = c2.op("act", lambda s, ring=ring, sq=sq: s.activation(
                        hbt[:], xt[ring][:, sq * 512:(sq + 1) * 512], AF.Identity,
                        bias=stat[:, 7:8], scale=stat[:, 6:7]),
                        also=[prev_store] if (sq == 0 and prev_store) else ())
                    prev_store = c2.op("pool", lambda g, i=i, sq=sq: g.dma_start(
                        out=h_dst[i * 128:(i + 1) * 128, sq * 512:(sq + 1) * 512],
                        in_=hbt[:]), dma=True)
                xt_cons.append(nv[1])
                if taph is not None:
                    ct = Chain(X, prev_store)
                    prev_store = ct.op("pool", lambda g, i=i: g.dma_start(
                        out=tap_t[taph][i * 128:(i + 1) * 128, :],
                        in_=h_dst[i * 128:(i + 1) * 128, :]), dma=True)
                if tapx is not None:
                    ct = Chain(X, xdep, ("mac", nv[1]))
                    ct.op("pool", lambda g, i=i, ring=ring: g.dma_start(
                        out=tap_t[tapx][i * 128:(i + 1) * 128, :],
                        in_=xt[ring][:]), dma=True)
            return prev_store

        def emit_kcomp(kv):
            ksc, toe, hbig = kv["ksc"], kv["toe"], kv["hbig"]
            c = Chain(X, consts_v, prep_v)
            for dst, key in [(ksc["lam"], "a_lam"), (ksc["p"], "a_p"),
                             (ksc["q"], "a_q"), (ksc["ct"], "a_ct"),
                             (ksc["vc"], "a_vc"), (ksc["bp"], "a_bp"),
                             (ksc["dD"], "a_d"), (ksc["ls"], "a_ls"),
                             (ksc["omr"], "c_omr"), (ksc["omi"], "c_omi"),
                             (ksc["fr"], "c_fr"), (ksc["fi"], "c_fi"),
                             (ksc["fni"], "c_fni"), (ksc["twr"], "c_twr"),
                             (ksc["twi"], "c_twi")]:
                c.op("act", lambda s, dst=dst, key=key: s.dma_start(
                    out=dst[:], in_=ins[key]), dma=True)
            c.op("pe", lambda t: t.transpose(ps[0][0:1, 0:64], ksc["bp"][:],
                                             c_id32[0:64, 0:64]))
            c.op("dve", lambda v: v.tensor_copy(grow[0:1, 0:64], ps[0][0:1, 0:64]))
            c.op("pe", lambda t: t.matmul(ps[1][0:64, 0:64], c_ones[0:1, 0:64],
                                          grow[0:1, 0:64], start=True, stop=True))
            vc_r = AP(ksc["vc"][:].tensor, 0, [[128, 64], [2, 64]])
            vc_i = AP(ksc["vc"][:].tensor, 1, [[128, 64], [2, 64]])
            c.op("dve", lambda v: v.tensor_tensor(ksc["t0"][:], vc_r,
                                                  ps[1][0:64, 0:64], ALU.mult))
            c.op("dve", lambda v: v.tensor_tensor(ksc["t1"][:], vc_i,
                                                  ps[1][0:64, 0:64], ALU.mult))
            c.op("dve", lambda v: v.reduce_sum(ksc["Bv"][:, 0:1], ksc["t0"][:],
                                               axis=AX.X))
            c.op("dve", lambda v: v.reduce_sum(ksc["Bv"][:, 1:2], ksc["t1"][:],
                                               axis=AX.X))

            def cconj(dst_r, dst_i, a, b):
                ar, ai = a[:, 0:1], a[:, 1:2]
                br, bi = b[:, 0:1], b[:, 1:2]
                c.op("dve", lambda v: v.tensor_tensor(ksc["t0"][:, 0:1], ar, br,
                                                      ALU.mult))
                c.op("dve", lambda v: v.tensor_tensor(ksc["t0"][:, 1:2], ai, bi,
                                                      ALU.mult))
                c.op("dve", lambda v: v.tensor_tensor(ksc["t1"][:, 0:1], ar, bi,
                                                      ALU.mult))
                c.op("dve", lambda v: v.tensor_tensor(ksc["t1"][:, 1:2], ai, br,
                                                      ALU.mult))
                c.op("dve", lambda v: v.tensor_add(dst_r, ksc["t0"][:, 0:1],
                                                   ksc["t0"][:, 1:2]))
                c.op("dve", lambda v: v.tensor_sub(dst_i, ksc["t1"][:, 0:1],
                                                   ksc["t1"][:, 1:2]))
            V = ksc["V"]
            cconj(V[:, 0:1], V[:, 4:5], ksc["ct"], ksc["Bv"])
            cconj(V[:, 1:2], V[:, 5:6], ksc["ct"], ksc["p"])
            cconj(V[:, 2:3], V[:, 6:7], ksc["q"], ksc["Bv"])
            cconj(V[:, 3:4], V[:, 7:8], ksc["q"], ksc["p"])
            c.op("act", lambda s: s.activation(ksc["sc"][0:1, 0:1], ksc["ls"][:],
                                               AF.Exp))
            c.op("dve", lambda v: v.reciprocal(ksc["sc"][0:1, 1:2],
                                               ksc["sc"][0:1, 0:1]))
            c.op("dve", lambda v: v.tensor_scalar(ksc["sc"][0:1, 2:3],
                                                  ksc["sc"][0:1, 1:2], 2.0, None,
                                                  ALU.mult))
            c.op("pe", lambda t: t.matmul(ps[0][0:64, 0:1], c_ones[0:1, 0:64],
                                          ksc["sc"][0:1, 2:3], start=True,
                                          stop=True))
            c.op("dve", lambda v: v.tensor_copy(ksc["rs"][:, 0:1], ps[0][0:64, 0:1]))
            c.op("dve", lambda v: v.tensor_scalar(ksc["t0"][:], ksc["omr"][:],
                                                  1.0, None, ALU.add))
            c.op("dve", lambda v: v.tensor_tensor(ksc["t1"][:], ksc["t0"][:],
                                                  ksc["t0"][:], ALU.mult))
            c.op("dve", lambda v: v.tensor_tensor(ksc["t2"][:], ksc["omi"][:],
                                                  ksc["omi"][:], ALU.mult))
            c.op("dve", lambda v: v.tensor_add(ksc["t3"][:], ksc["t1"][:],
                                               ksc["t2"][:]))
            c.op("dve", lambda v: v.reciprocal(ksc["t3"][:], ksc["t3"][:]))
            c.op("dve", lambda v: v.tensor_scalar(ksc["t1"][:], ksc["omr"][:],
                                                  -1.0, 1.0, ALU.mult, ALU.add))
            c.op("dve", lambda v: v.tensor_tensor(ksc["t1"][:], ksc["t1"][:],
                                                  ksc["t0"][:], ALU.mult))
            c.op("dve", lambda v: v.tensor_sub(ksc["t1"][:], ksc["t1"][:],
                                               ksc["t2"][:]))
            c.op("dve", lambda v: v.tensor_tensor(ksc["gr"][:], ksc["t1"][:],
                                                  ksc["t3"][:], ALU.mult))
            c.op("dve", lambda v: v.tensor_scalar_mul(ksc["gr"][:], ksc["gr"][:],
                                                      ksc["rs"][:, 0:1]))
            c.op("dve", lambda v: v.tensor_tensor(ksc["gi"][:], ksc["omi"][:],
                                                  ksc["t3"][:], ALU.mult))
            c.op("dve", lambda v: v.tensor_scalar(ksc["gi"][:], ksc["gi"][:],
                                                  2.0, None, ALU.mult))
            c.op("dve", lambda v: v.tensor_scalar_mul(ksc["gi"][:], ksc["gi"][:],
                                                      ksc["rs"][:, 0:1]))
            c.op("dve", lambda v: v.tensor_tensor(ksc["cr"][:], ksc["t0"][:],
                                                  ksc["t3"][:], ALU.mult))
            c.op("dve", lambda v: v.tensor_tensor(ksc["ci"][:], ksc["omi"][:],
                                                  ksc["t3"][:], ALU.mult))
            c.op("dve", lambda v: v.tensor_scalar(ksc["cr"][:], ksc["cr"][:],
                                                  2.0, None, ALU.mult))
            c.op("dve", lambda v: v.tensor_scalar(ksc["ci"][:], ksc["ci"][:],
                                                  2.0, None, ALU.mult))
            c.op("pool", lambda g: g.dma_start(
                out=grows_d[0:1, :].rearrange("o (p f) -> (o p) f", p=64),
                in_=ksc["gr"][:]), dma=True)
            c.op("pool", lambda g: g.dma_start(
                out=grows_d[1:2, :].rearrange("o (p f) -> (o p) f", p=64),
                in_=ksc["gi"][:]), dma=True)
            sc32 = 2.0 ** -32
            for i in range(DM // 512):
                c.op("act", lambda s, i=i: s.dma_start(
                    out=ksc["gr8"],
                    in_=grows_d[0:1, i * 512:(i + 1) * 512]), dma=True)
                c.op("act", lambda s, i=i: s.dma_start(
                    out=ksc["gi8"],
                    in_=grows_d[1:2, i * 512:(i + 1) * 512]), dma=True)
                c.op("pe", lambda t: t.matmul(ps[2][0:64, :], c_ones[0:1, 0:64],
                                              ksc["gr8"], start=True,
                                              stop=True))
                c.op("pe", lambda t: t.matmul(ps[3][0:64, :], c_ones[0:1, 0:64],
                                              ksc["gi8"], start=True,
                                              stop=True))
                c.op("dve", lambda v: v.tensor_scalar(
                    ksc["ir"][:], ps[2][0:64, :], ksc["lam"][:, 0:1], sc32,
                    ALU.subtract, ALU.mult))
                c.op("dve", lambda v: v.tensor_scalar(
                    ksc["ii"][:], ps[3][0:64, :], ksc["lam"][:, 1:2], sc32,
                    ALU.subtract, ALU.mult))
                d0 = ksc["d0"][0:64, 0:512]
                d1 = ksc["d1"][0:64, 512:1024]
                c.op("dve", lambda v, d0=d0: v.tensor_tensor(d0, ksc["ir"][:],
                                                             ksc["ir"][:], ALU.mult))
                c.op("dve", lambda v, d1=d1: v.tensor_tensor(d1, ksc["ii"][:],
                                                             ksc["ii"][:], ALU.mult))
                c.op("dve", lambda v, d0=d0, d1=d1: v.tensor_add(d0, d0, d1))
                c.op("dve", lambda v, d0=d0: v.reciprocal(d0, d0))
                c.op("dve", lambda v, d0=d0: v.tensor_tensor(ksc["ir"][:],
                                                             ksc["ir"][:], d0,
                                                             ALU.mult))
                c.op("dve", lambda v, d0=d0: v.tensor_tensor(ksc["ii"][:],
                                                             ksc["ii"][:], d0,
                                                             ALU.mult))
                c.op("dve", lambda v: v.tensor_scalar(ksc["ir"][:], ksc["ir"][:],
                                                      sc32, None, ALU.mult))
                c.op("dve", lambda v: v.tensor_scalar(ksc["ii"][:], ksc["ii"][:],
                                                      -sc32, None, ALU.mult))
                c.op("pe", lambda t: t.matmul(ps[4][0:8, :], ksc["V"][:],
                                              ksc["ir"][:], start=True, stop=True))
                c.op("dve", lambda v: v.tensor_copy(ksc["cs"][:], ps[4][0:8, :]))
                c.op("pool", lambda g, i=i: g.dma_start(
                    out=crows_d[0:8, i * 512:(i + 1) * 512], in_=ksc["cs"][:]),
                    dma=True)
                c.op("pe", lambda t: t.matmul(ps[4][0:8, :], ksc["V"][:],
                                              ksc["ii"][:], start=True, stop=True))
                c.op("dve", lambda v: v.tensor_copy(ksc["cs"][:], ps[4][0:8, :]))
                c.op("pool", lambda g, i=i: g.dma_start(
                    out=crows_d[8:16, i * 512:(i + 1) * 512], in_=ksc["cs"][:]),
                    dma=True)
            for j in range(8):
                c.op("act", lambda s, j=j: s.dma_start(
                    out=ksc["kk"][0:64, j * 64:(j + 1) * 64],
                    in_=crows_d[j:j + 1, :].rearrange("o (p f) -> (o p) f", p=64)),
                    dma=True)
                c.op("act", lambda s, j=j: s.dma_start(
                    out=ksc["kb"][0:64, 512 + j * 64:512 + (j + 1) * 64],
                    in_=crows_d[8 + j:9 + j, :].rearrange("o (p f) -> (o p) f",
                                                          p=64)), dma=True)
            kk = ksc["kk"][0:64, 0:512]
            kb = ksc["kb"][0:64, 512:1024]
            dst = ksc["kcplx"][0:64, 0:512]
            for j in range(4):
                c.op("dve", lambda v, j=j: v.tensor_sub(
                    dst[:, j * 64:(j + 1) * 64], kk[:, j * 64:(j + 1) * 64],
                    kb[:, (4 + j) * 64:(5 + j) * 64]))
                c.op("dve", lambda v, j=j: v.tensor_add(
                    dst[:, (4 + j) * 64:(5 + j) * 64], kb[:, j * 64:(j + 1) * 64],
                    kk[:, (4 + j) * 64:(5 + j) * 64]))
            k00r, k01r, k10r, k11r = (dst[:, j * 64:(j + 1) * 64] for j in range(4))
            k00i, k01i, k10i, k11i = (dst[:, (4 + j) * 64:(5 + j) * 64]
                                      for j in range(4))
            t0, t1, t2, t3, t4, t5 = (ksc[f"t{j}"][:] for j in range(6))
            c.op("dve", lambda v: v.tensor_scalar(t0, k11r, 1.0, None, ALU.add))
            c.op("dve", lambda v: v.tensor_tensor(t1, t0, t0, ALU.mult))
            c.op("dve", lambda v: v.tensor_tensor(t2, k11i, k11i, ALU.mult))
            c.op("dve", lambda v: v.tensor_add(t1, t1, t2))
            c.op("dve", lambda v: v.reciprocal(t1, t1))
            c.op("dve", lambda v: v.tensor_tensor(t2, k01r, k10r, ALU.mult))
            c.op("dve", lambda v: v.tensor_tensor(t4, k01i, k10i, ALU.mult))
            c.op("dve", lambda v: v.tensor_tensor(t3, k01r, k10i, ALU.mult))
            c.op("dve", lambda v: v.tensor_tensor(t5, k01i, k10r, ALU.mult))
            c.op("dve", lambda v: v.tensor_sub(t2, t2, t4))
            c.op("dve", lambda v: v.tensor_add(t3, t3, t5))
            c.op("dve", lambda v: v.tensor_tensor(t4, t2, t0, ALU.mult))
            c.op("dve", lambda v: v.tensor_tensor(t5, t3, k11i, ALU.mult))
            c.op("dve", lambda v: v.tensor_add(t4, t4, t5))
            c.op("dve", lambda v: v.tensor_tensor(t5, t3, t0, ALU.mult))
            c.op("dve", lambda v: v.tensor_tensor(t0, t2, k11i, ALU.mult))
            c.op("dve", lambda v: v.tensor_sub(t5, t5, t0))
            c.op("dve", lambda v: v.tensor_tensor(t4, t4, t1, ALU.mult))
            c.op("dve", lambda v: v.tensor_tensor(t5, t5, t1, ALU.mult))
            c.op("dve", lambda v: v.tensor_sub(t0, k00r, t4))
            c.op("dve", lambda v: v.tensor_sub(t1, k00i, t5))
            c.op("dve", lambda v: v.tensor_tensor(t2, ksc["cr"][:], t0, ALU.mult))
            c.op("dve", lambda v: v.tensor_tensor(t4, ksc["ci"][:], t1, ALU.mult))
            c.op("dve", lambda v: v.tensor_tensor(t3, ksc["cr"][:], t1, ALU.mult))
            c.op("dve", lambda v: v.tensor_tensor(t5, ksc["ci"][:], t0, ALU.mult))
            c.op("dve", lambda v: v.tensor_sub(t2, t2, t4))
            c.op("dve", lambda v: v.tensor_add(t3, t3, t5))

            def s1r_mm(t):
                t.matmul(ps[0][0:64, 0:64], ksc["t2"][:], ksc["fr"][:],
                         start=True, stop=False)
                return t.matmul(ps[0][0:64, 0:64], ksc["t3"][:], ksc["fni"][:],
                                start=False, stop=True)

            def s1i_mm(t):
                t.matmul(ps[1][0:64, 0:64], ksc["t2"][:], ksc["fi"][:],
                         start=True, stop=False)
                return t.matmul(ps[1][0:64, 0:64], ksc["t3"][:], ksc["fr"][:],
                                start=False, stop=True)
            c.op("pe", s1r_mm)
            c.op("pe", s1i_mm)
            c.op("dve", lambda v: v.tensor_copy(ksc["s1r"][:], ps[0][0:64, 0:64]))
            c.op("dve", lambda v: v.tensor_copy(ksc["s1i"][:], ps[1][0:64, 0:64]))
            c.op("dve", lambda v: v.tensor_tensor(ksc["br"][:], ksc["s1r"][:],
                                                  ksc["twr"][:], ALU.mult))
            c.op("dve", lambda v: v.tensor_tensor(ksc["t0"][:], ksc["s1i"][:],
                                                  ksc["twi"][:], ALU.mult))
            c.op("dve", lambda v: v.tensor_tensor(ksc["bi"][:], ksc["s1r"][:],
                                                  ksc["twi"][:], ALU.mult))
            c.op("dve", lambda v: v.tensor_tensor(ksc["t1"][:], ksc["s1i"][:],
                                                  ksc["twr"][:], ALU.mult))
            c.op("dve", lambda v: v.tensor_sub(ksc["br"][:], ksc["br"][:],
                                               ksc["t0"][:]))
            c.op("dve", lambda v: v.tensor_add(ksc["bi"][:], ksc["bi"][:],
                                               ksc["t1"][:]))

            def s2_mm(t):
                t.matmul(ps[2][0:64, 0:64], ksc["fr"][:], ksc["br"][:],
                         start=True, stop=False)
                return t.matmul(ps[2][0:64, 0:64], ksc["fni"][:], ksc["bi"][:],
                                start=False, stop=True)
            c.op("pe", s2_mm)
            c.op("act", lambda s: s.activation(ksc["Ksb"][:], ps[2][0:64, 0:64],
                                               AF.Copy, scale=1.0 / 4096.0))
            c.op("dve", lambda v: v.tensor_add(ksc["Ksb"][0:1, 0:1],
                                               ksc["Ksb"][0:1, 0:1], ksc["dD"][:]))
            if "K" in tap_t:
                c.op("pool", lambda g: g.dma_start(out=tap_t["K"],
                                                   in_=ksc["Ksb"][:]), dma=True)
            c.op("dve", lambda v: v.tensor_copy(ksc["Kbf"][:], ksc["Ksb"][:]))
            c.op("dve", lambda v: v.memset(ksc["z256"][:], 0.0))
            c.op("pool", lambda g: g.dma_start(out=Zbuf[0:1, 0:128],
                                               in_=ksc["z256"][0:1, 0:128]), dma=True)
            c.op("pool", lambda g: g.dma_start(out=Zbuf[0:1, 4224:4352],
                                               in_=ksc["z256"][0:1, 128:256]), dma=True)
            c.op("pool", lambda g: g.dma_start(
                out=Zbuf[0:1, 128:4224].rearrange("o (p f) -> (o p) f", p=64),
                in_=ksc["Kbf"][:]), dma=True)
            c.op("act", lambda s: s.dma_start(
                out=hbig[:], in_=AP(Zbuf.tensor, 0, [[1, 128], [1, 4224]])),
                dma=True)
            c.op("dve", lambda v: v.memset(toe[:, 0:3 * 128], 0.0))
            for tt in range(32):
                c.op("pe", lambda t, tt=tt: t.matmul(
                    ps[tt % 2][0:128, 0:128], c_revb[:],
                    hbig[:, 128 * tt + 1:128 * tt + 129], start=True, stop=True))
                c.op("dve", lambda v, tt=tt: v.tensor_copy(
                    toe[:, (3 + tt) * 128:(4 + tt) * 128],
                    ps[tt % 2][0:128, 0:128]))
            return c.dep()

        def emit_outproj(nm, src_w, row0, src_dep, resid_src, out_dst, with_b3,
                         extra_pump=0):
            npan = DM // 256
            ntiles = nGT
            pe_base = X.val("mpe")
            mdv0 = X.val("mdv")
            qst0 = X.val("qst")
            pan_v = {}
            rs_v = {}
            b3v = {}

            def slice_free(grp):
                return mdv0 + 2 * grp + 1

            def slice_done(grp):
                return mdv0 + 2 * (grp + 1)

            def store_done(grp):
                return qst0 + 16 * (grp + 1)

            @block.sync
            def _(sy):
                for pi in range(npan):
                    free = None
                    if pi >= 2:
                        free = ("mpe", pe_base + (pi - 1) * ntiles)
                    pan_v[pi] = load_panel_half(sy, nm, pi * 256, pi % 2, free)
                    for lt in range(ntiles):
                        k = pi * ntiles + lt
                        ring = k % 2
                        if k >= 2:
                            sy.wait_ge(X.handle("mdv"), slice_done(k - 2))
                        elif mdv0 > 0:
                            sy.wait_ge(X.handle("mdv"), mdv0)
                        gi = row0 // 128 + lt
                        X.inc(f"qr{ring}", sy.dma_start(
                            out=rsl[ring][:, 0:256],
                            in_=resid_src[gi * 128:(gi + 1) * 128,
                                          pi * 256:(pi + 1) * 256]), 16)
                        rs_v[k] = X.val(f"qr{ring}")
            if extra_pump:
                p0b_pump(extra_pump)

            @block.tensor
            def _(t):
                t.wait_ge(X.handle(src_dep[0]), src_dep[1])
                t.wait_ge(X.handle("mdv"), mdv0)
                for pi in range(npan):
                    t.wait_ge(X.handle(pan_v[pi][0]), pan_v[pi][1])
                    for lt in range(ntiles):
                        grp = pi * ntiles + lt
                        bank = grp % 5
                        if grp >= 5:
                            t.wait_ge(X.handle("mdv"), slice_free(grp - 5))
                        mm = None
                        for k in range(NKC):
                            mm = t.matmul(
                                ps[bank][:, 0:256],
                                x4[:, k * src_w + lt * 128:k * src_w + lt * 128 + 128],
                                wh_slice(pi % 2, k),
                                start=(k == 0),
                                stop=(k == NKC - 1 and not with_b3))
                        if with_b3:
                            mm = t.matmul(
                                ps[bank][:, 0:256], c_onesb[0:1, 0:128],
                                b3bf[0:1, pi * 256:(pi + 1) * 256],
                                start=False, stop=True)
                        X.inc("mpe", mm, 1)

            @block.vector
            def _(v):
                for pi in range(npan):
                    for lt in range(ntiles):
                        grp = pi * ntiles + lt
                        bank = grp % 5
                        ring = grp % 2
                        v.wait_ge(X.handle("mpe"), pe_base + grp + 1)
                        v.wait_ge(X.handle(f"qr{ring}"), rs_v[grp])
                        if grp >= 2:
                            v.wait_ge(X.handle("qst"), store_done(grp - 2))
                        elif qst0 > 0:
                            v.wait_ge(X.handle("qst"), qst0)
                        sl = ep1[:, (grp % 2) * 256:(grp % 2) * 256 + 256]
                        X.inc("mdv", v.tensor_add(sl, ps[bank][:, 0:256],
                                                  rsl[ring][:, 0:256]), 1)
                        X.inc("mdv", v.engine_nop(), 1)

            @block.gpsimd
            def _(g):
                for pi in range(npan):
                    for lt in range(ntiles):
                        grp = pi * ntiles + lt
                        gi = row0 // 128 + lt
                        g.wait_ge(X.handle("mdv"), slice_done(grp))
                        g.wait_ge(X.handle("qst"), X.val("qst"))
                        sl = ep1[:, (grp % 2) * 256:(grp % 2) * 256 + 256]
                        X.inc("qst", g.dma_start(
                            out=out_dst[gi * 128:(gi + 1) * 128,
                                        pi * 256:(pi + 1) * 256], in_=sl), 16)
            return ("qst", X.val("qst"))

        def emit_gemm1(hfm_dep):
            npair = DM // 128
            pe_base = X.val("mpe")
            mac0 = X.val("mac")
            mdv0 = X.val("mdv")
            pan_v = {}

            @block.sync
            def _(sy):
                for pi in range(npair):
                    for side in range(2):
                        q = (2 * pi + side) % 4
                        free = None
                        if pi >= 2:
                            free = ("mpe", pe_base + (pi - 1))
                        m0 = pi * 128 + side * (DFF // 2)
                        pan_v[(pi, side)] = load_panel_quarter(sy, "w12", m0, q,
                                                               free)

            @block.tensor
            def _(t):
                t.wait_ge(X.handle(hfm_dep[0]), hfm_dep[1])
                t.wait_ge(X.handle("mac"), mac0)
                t.wait_ge(X.handle("mdv"), mdv0)
                for pi in range(npair):
                    qa = (2 * pi) % 4
                    qb = (2 * pi + 1) % 4
                    t.wait_ge(X.handle(pan_v[(pi, 0)][0]), pan_v[(pi, 0)][1])
                    t.wait_ge(X.handle(pan_v[(pi, 1)][0]), pan_v[(pi, 1)][1])
                    bankA = (2 * pi) % 6
                    bankB = (2 * pi + 1) % 6
                    if pi >= 3:
                        t.wait_ge(X.handle("mac"), mac0 + pi - 2)
                        t.wait_ge(X.handle("mdv"), mdv0 + pi - 2)
                    for k in range(NKC):
                        t.matmul(ps[bankA][:], wqt_slice(qa, k),
                                 hfm[:, k * 512:(k + 1) * 512],
                                 start=(k == 0), stop=(k == NKC - 1))
                    mm = None
                    for k in range(NKC):
                        mm = t.matmul(ps[bankB][:], wqt_slice(qb, k),
                                      hfm[:, k * 512:(k + 1) * 512],
                                      start=(k == 0), stop=(k == NKC - 1))
                    X.inc("mpe", mm, 1)

            @block.scalar
            def _(s):
                for pi in range(npair):
                    bankA = (2 * pi) % 6
                    s.wait_ge(X.handle("mpe"), pe_base + pi + 1)
                    if pi >= 2:
                        s.wait_ge(X.handle("mdv"), mdv0 + pi - 1)
                    X.inc("mac", s.activation(
                        sil[:, (pi % 2) * 512:(pi % 2) * 512 + 512],
                        ps[bankA][:], AF.Silu, bias=b12_sb[:, pi:pi + 1]), 1)

            @block.vector
            def _(v):
                for pi in range(npair):
                    bankB = (2 * pi + 1) % 6
                    v.wait_ge(X.handle("mac"), mac0 + pi + 1)
                    X.inc("mdv", v.scalar_tensor_tensor(
                        x4[:, pi * 512:(pi + 1) * 512], ps[bankB][:],
                        b12_sb[:, 32 + pi:33 + pi],
                        sil[:, (pi % 2) * 512:(pi % 2) * 512 + 512],
                        ALU.add, ALU.mult), 1)
            return ("mdv", X.val("mdv"))

        def emit_p1():
            kv = {}
            with ExitStack() as p1es:
                def p1sb(name, shape, dt_):
                    return p1es.enter_context(nc.sbuf_tensor(name, shape, dt_))
                kv["toe"] = p1sb("p1_toe", [128, 35 * 128], BF16)
                kv["hbig"] = p1sb("p1_hbig", [128, 4224], BF16)
                ksc = {}
                for knm, kshape, kdt in [
                        ("lam", [64, 2], F32), ("p", [64, 2], F32),
                        ("q", [64, 2], F32), ("ct", [64, 2], F32),
                        ("vc", [64, 128], F32), ("bp", [64, 1], F32),
                        ("dD", [1, 1], F32), ("ls", [1, 1], F32),
                        ("omr", [64, 64], F32), ("omi", [64, 64], F32),
                        ("fr", [64, 64], F32), ("fi", [64, 64], F32),
                        ("fni", [64, 64], F32), ("twr", [64, 64], F32),
                        ("twi", [64, 64], F32), ("Bv", [64, 2], F32),
                        ("V", [64, 8], F32), ("sc", [1, 4], F32),
                        ("rs", [64, 1], F32),
                        ("t0", [64, 64], F32), ("t1", [64, 64], F32),
                        ("t2", [64, 64], F32), ("t3", [64, 64], F32),
                        ("t4", [64, 64], F32), ("t5", [64, 64], F32),
                        ("gr", [64, 64], F32), ("gi", [64, 64], F32),
                        ("cr", [64, 64], F32), ("ci", [64, 64], F32),
                        ("ir", [64, 512], F32), ("ii", [64, 512], F32),
                        ("cs", [8, 512], F32),
                        ("s1r", [64, 64], F32), ("s1i", [64, 64], F32),
                        ("br", [64, 64], F32), ("bi", [64, 64], F32),
                        ("Ksb", [64, 64], F32), ("Kbf", [64, 64], BF16),
                        ("z256", [1, 256], BF16)]:
                    ksc[knm] = p1sb("k_" + knm, kshape, kdt)
                ksc["gr8"] = ep1[0:1, 0:512]
                ksc["gi8"] = ep1[0:1, 512:1024]
                ksc["kcplx"] = gsl_sb
                ksc["d0"] = sil
                ksc["d1"] = sil
                ksc["kk"] = ep1
                ksc["kb"] = ep1
                kv["ksc"] = ksc
                toe_dep = emit_kcomp(kv)
                h1_dep = emit_ln_sweep(X_in, h_scr_a, prep_v,
                                       taph="h1" if "h1" in tap_t else None)
                p0b_pump(24)
                prev_pe = None
                for sl_i in range(nS):
                    hfm_dep = emit_hfm_load(h_scr_a, sl_i * 512, h1_dep,
                                            free_dep=prev_pe)
                    pe_base = X.val("mpe")
                    mdv0 = X.val("mdv")
                    qst0 = X.val("qst")
                    rs_v = {}

                    @block.sync
                    def _(sy, sl_i=sl_i, mdv0=mdv0):
                        for lt in range(nGT):
                            gi = sl_i * nGT + lt
                            for pn in range(8):
                                k = lt * 8 + pn
                                ring = k % 2
                                if mdv0 + k - 1 > 0:
                                    sy.wait_ge(X.handle("mdv"), mdv0 + k - 1)
                                X.inc(f"qr{ring}", sy.dma_start(
                                    out=rsl[ring][:],
                                    in_=X_in[gi * 128:(gi + 1) * 128,
                                             pn * 512:(pn + 1) * 512]), 16)
                                rs_v[k] = X.val(f"qr{ring}")

                    @block.tensor
                    def _(t, sl_i=sl_i, hfm_dep=hfm_dep, mdv0=mdv0):
                        t.wait_ge(X.handle(hfm_dep[0]), hfm_dep[1])
                        t.wait_ge(X.handle(toe_dep[0]), toe_dep[1])
                        t.wait_ge(X.handle("mdv"), mdv0)
                        for lt in range(nGT):
                            for pn in range(8):
                                grp = lt * 8 + pn
                                bank = grp % 5
                                if grp >= 5:
                                    t.wait_ge(X.handle("mdv"), mdv0 + grp - 4)
                                L0 = pn * 4
                                jmax = min(L0 + 4, NKC)
                                mm = None
                                for J in range(jmax):
                                    soff = (L0 - J + 3) * 128
                                    mm = t.matmul(
                                        ps[bank][:],
                                        hfm[:, J * 512 + lt * 128:
                                            J * 512 + (lt + 1) * 128],
                                        kv["toe"][:, soff:soff + 512],
                                        start=(J == 0), stop=(J == jmax - 1))
                                X.inc("mpe", mm, 1)

                    @block.vector
                    def _(v, sl_i=sl_i, mdv0=mdv0, pe_base=pe_base, qst0=qst0):
                        for lt in range(nGT):
                            for pn in range(8):
                                grp = lt * 8 + pn
                                bank = grp % 5
                                ring = grp % 2
                                v.wait_ge(X.handle("mpe"), pe_base + grp + 1)
                                v.wait_ge(X.handle(f"qr{ring}"), rs_v[grp])
                                if qst0 + 16 * (grp - 1) > 0:
                                    v.wait_ge(X.handle("qst"),
                                              qst0 + 16 * (grp - 1))
                                sl = ep1[:, (grp % 2) * 512:(grp % 2) * 512 + 512]
                                X.inc("mdv", v.tensor_add(sl, ps[bank][:],
                                                          rsl[ring][:]), 1)

                    @block.gpsimd
                    def _(g, sl_i=sl_i, mdv0=mdv0):
                        for lt in range(nGT):
                            gi = sl_i * nGT + lt
                            for pn in range(8):
                                grp = lt * 8 + pn
                                g.wait_ge(X.handle("mdv"), mdv0 + grp + 1)
                                g.wait_ge(X.handle("qst"), X.val("qst"))
                                sl = ep1[:, (grp % 2) * 512:(grp % 2) * 512 + 512]
                                X.inc("qst", g.dma_start(
                                    out=resid_a[gi * 128:(gi + 1) * 128,
                                                pn * 512:(pn + 1) * 512],
                                    in_=sl), 16)
                    prev_pe = ("mpe", X.val("mpe"))
                    p0b_pump(8)
                x1_dep = ("qst", X.val("qst"))
                if "x1" in tap_t:
                    cx = Chain(X, x1_dep)
                    cx.op("pool", lambda g: g.dma_start(out=tap_t["x1"],
                                                        in_=resid_a), dma=True)
            return x1_dep

        def emit_attnb(abt, h3_dep):
            kvring = abt["kvring"]
            cols = abt["cols"]
            gcols, dtcols, edcols, cdcols = (cols["g"], cols["dt"], cols["ed"],
                                             cols["cd"])
            ctm2, stm2, dtg = abt["ctm2"], abt["stm2"], abt["dtg"]
            prev_pe = None
            dt_deps = []
            kv_spill = None
            for sl_i in range(nS):
                row0 = sl_i * 512
                prev_rope = ("mpo", X.val("mpo")) if sl_i else None
                prev_spill = ("qg", X.val("qg")) if sl_i else None
                hfm_dep = emit_hfm_load(h_scr_a, row0, h3_dep, free_dep=prev_pe)
                pe_base = X.val("mpe")
                mac0 = X.val("mac")
                pan_v = {}
                kq_done = {}

                @block.sync
                def _(sy, sl_i=sl_i, pe_base=pe_base):
                    for j in range(8):
                        nm2 = "wk" if j < 4 else "wv"
                        m0 = (j % 4) * 256
                        slot = j % 2
                        free = None
                        if sl_i == 0:
                            if j >= 2:
                                free = ("mpe", pe_base + (j - 1) * 4)
                        else:
                            free = ("mpe", pe_base + max(0, (j - 1) * 4))
                        pan_v[j] = load_panel_half(sy, nm2, m0, slot, free)

                @block.tensor
                def _(t, sl_i=sl_i, hfm_dep=hfm_dep, mac0=mac0):
                    t.wait_ge(X.handle(hfm_dep[0]), hfm_dep[1])
                    t.wait_ge(X.handle(prep_v[0]), prep_v[1])
                    t.wait_ge(X.handle("mac"), mac0)
                    gidx = 0
                    for j in range(8):
                        slot = j % 2
                        t.wait_ge(X.handle(pan_v[j][0]), pan_v[j][1])
                        for lt in range(nGT):
                            bank = gidx % 6
                            if gidx >= 6:
                                t.wait_ge(X.handle("mac"), mac0 + gidx - 5)
                            mm = None
                            for k in range(NKC):
                                mm = t.matmul(
                                    ps[bank][:, 0:256],
                                    hfm[:, k * 512 + lt * 128:
                                        k * 512 + lt * 128 + 128],
                                    wh_slice(slot, k),
                                    start=(k == 0), stop=(k == NKC - 1))
                            X.inc("mpe", mm, 1)
                            gidx += 1
                    bank = gidx % 6
                    if gidx >= 6:
                        t.wait_ge(X.handle("mac"), mac0 + gidx - 5)
                    mm = None
                    for k in range(NKC):
                        mm = t.matmul(ps[bank][0:NH, :],
                                      winT[:, k * NH:(k + 1) * NH],
                                      hfm[:, k * 512:(k + 1) * 512],
                                      start=(k == 0), stop=(k == NKC - 1))
                    X.inc("mpe", mm, 1)
                dt_bank = 32 % 6

                @block.scalar
                def _(s, sl_i=sl_i, pe_base=pe_base, prev_rope=prev_rope,
                      prev_spill=prev_spill):
                    if prev_rope:
                        s.wait_ge(X.handle(prev_rope[0]), prev_rope[1])
                    if prev_spill:
                        s.wait_ge(X.handle(prev_spill[0]), prev_spill[1])
                    gidx = 0
                    for j in range(8):
                        dstt = "k" if j < 4 else "v"
                        mp = j % 4
                        for lt in range(nGT):
                            bank = gidx % 6
                            s.wait_ge(X.handle("mpe"), pe_base + gidx + 1)
                            X.inc("mac", s.copy(
                                kvring[dstt][:, lt * 1024 + mp * 256:
                                             lt * 1024 + (mp + 1) * 256],
                                ps[bank][:, 0:256]), 1)
                            gidx += 1

                @block.gpsimd
                def _(g, sl_i=sl_i, mac0=mac0):
                    for lt in range(nGT):
                        gi = sl_i * nGT + lt
                        g.wait_ge(X.handle("mac"), mac0 + 3 * 4 + lt + 1)
                        g.wait_ge(X.handle("qg"), X.val("qg"))
                        X.inc("qg", g.dma_start(
                            out=ctm2[:],
                            in_=ins["c_costm"][:, gi * 128:(gi + 1) * 128]), 16)
                        X.inc("qg", g.dma_start(
                            out=stm2[:],
                            in_=ins["c_sintm"][:, gi * 128:(gi + 1) * 128]), 16)
                        g.wait_ge(X.handle("qg"), X.val("qg"))
                        for kvh in range(NKV):
                            base = lt * 1024 + kvh * 128
                            klo = kvring["k"][:, base:base + 64]
                            khi = kvring["k"][:, base + 64:base + 128]
                            X.inc("mpo", g.tensor_tensor(
                                gsl_sb[:, 0:128], kvring["k"][:, base:base + 128],
                                ctm2[:], ALU.mult), 1)
                            X.inc("mpo", g.tensor_tensor(
                                gsl_sb[:, 128:192], khi, stm2[:, 0:64],
                                ALU.mult), 1)
                            X.inc("mpo", g.tensor_tensor(
                                gsl_sb[:, 192:256], klo, stm2[:, 64:128],
                                ALU.mult), 1)
                            g.wait_ge(X.handle("mpo"), X.val("mpo"))
                            X.inc("mpo", g.tensor_sub(
                                klo, gsl_sb[:, 0:64], gsl_sb[:, 128:192]), 1)
                            X.inc("mpo", g.tensor_add(
                                khi, gsl_sb[:, 64:128], gsl_sb[:, 192:256]), 1)
                            g.wait_ge(X.handle("mpo"), X.val("mpo"))
                        g.wait_ge(X.handle("qg"), X.val("qg"))
                        X.inc("qg", g.dma_start(
                            out=k_scr[gi * 128:(gi + 1) * 128, :],
                            in_=kvring["k"][:, lt * 1024:(lt + 1) * 1024]), 16)
                    for lt in range(nGT):
                        gi = sl_i * nGT + lt
                        g.wait_ge(X.handle("mac"), mac0 + 7 * 4 + lt + 1)
                        g.wait_ge(X.handle("qg"), X.val("qg"))
                        X.inc("qg", g.dma_start(
                            out=v_scr[gi * 128:(gi + 1) * 128, :],
                            in_=kvring["v"][:, lt * 1024:(lt + 1) * 1024]), 16)
                kv_spill = ("qg", X.val("qg"))

                cdt = Chain(X, ("mpe", X.val("mpe")),
                            dt_deps[-1] if dt_deps else None)
                cdt.op("act", lambda s: s.activation(
                    dtg["a"], ps[dt_bank][0:NH, :], AF.Abs,
                    bias=dtb_sb[:]))
                cdt.op("act", lambda s: s.activation(
                    dtg["a"], dtg["a"], AF.Exp, scale=-1.0))
                cdt.op("act", lambda s: s.activation(
                    dtg["a"], dtg["a"], AF.Ln,
                    bias=cbias[0:NH, 1:2]))
                cdt.op("act", lambda s: s.activation(
                    dtg["b"], ps[dt_bank][0:NH, :], AF.Relu,
                    bias=dtb_sb[:]))
                cdt.op("dve", lambda v: v.tensor_add(
                    dtg["b"], dtg["b"], dtg["a"]))
                cdt.op("pool", lambda g, row0=row0: g.dma_start(
                    out=dt_scr[:, row0:row0 + 512], in_=dtg["b"]),
                    dma=True)
                cdt.op("dve", lambda v: v.tensor_scalar(
                    dtg["a"], dtg["b"], expa_sb[:], -1.0,
                    ALU.mult, ALU.mult))
                for cc4 in range(4):
                    cdt.op("dve", lambda v, cc4=cc4: v.tensor_tensor_scan(
                        dtg["b"][0:NH, cc4 * 128:(cc4 + 1) * 128],
                        dtg["ones"][:],
                        dtg["a"][0:NH, cc4 * 128:(cc4 + 1) * 128],
                        0.0, ALU.mult, ALU.add))
                cdt.op("pool", lambda g, row0=row0: g.dma_start(
                    out=g_scr[:, row0:row0 + 512], in_=dtg["b"]),
                    dma=True)
                dt_deps.append(cdt.dep())
                prev_pe = ("mpe", X.val("mpe"))
                p0b_pump(10)
            gdt_dep = dt_deps[-1]
            for tn, src in (("G", g_scr), ("dt", dt_scr), ("kT", k_scr),
                            ("vT", v_scr)):
                if tn in tap_t:
                    cg = Chain(X, gdt_dep, kv_spill)
                    cg.op("pool", lambda g, tn=tn, src=src: g.dma_start(
                        out=tap_t[tn], in_=src), dma=True)

            col_dep = None
            for h in range(NH):
                chh = Chain(X, gdt_dep, col_dep)
                chh.op("act", lambda s, h=h: s.dma_start(
                    out=g16[:nT, :], in_=g_scr[h:h + 1, :]
                    .rearrange("o (n j) -> (o n) j", j=128)), dma=True)
                chh.op("pe", lambda t: t.transpose(
                    ps[5][0:128, 0:nT], g16[0:nT, :], c_id32[0:nT, 0:nT]))
                chh.op("dve", lambda v, h=h: v.tensor_copy(
                    gcols[:, h * 16:h * 16 + nT], ps[5][0:128, 0:nT]))
                chh.op("pe", lambda t: t.transpose(
                    ps[5][0:1, 256:256 + nT], g16[0:nT, 127:128],
                    c_id32[0:nT, 0:nT]))
                chh.op("dve", lambda v: v.tensor_copy(
                    grow[0:1, 0:nT], ps[5][0:1, 256:256 + nT]))
                chh.op("act", lambda s: s.activation(
                    grow[0:1, 32:32 + nT], grow[0:1, 0:nT], AF.Exp))
                chh.op("pe", lambda t: t.matmul(
                    ps[5][0:128, 256:256 + nT], c_ones[:], grow[0:1, 32:32 + nT],
                    start=True, stop=True))
                chh.op("dve", lambda v, h=h: v.tensor_copy(
                    cdcols[:, h * 16:h * 16 + nT], ps[5][0:128, 256:256 + nT]))
                chh.op("pe", lambda t: t.matmul(
                    ps[5][0:128, 256:256 + nT], c_ones[:], grow[0:1, 0:nT],
                    start=True, stop=True))
                chh.op("dve", lambda v, h=h: v.tensor_sub(
                    edcols[:, h * 16:h * 16 + nT], ps[5][0:128, 256:256 + nT],
                    gcols[:, h * 16:h * 16 + nT]))
                chh.op("act", lambda s, h=h: s.activation(
                    edcols[:, h * 16:h * 16 + nT],
                    edcols[:, h * 16:h * 16 + nT], AF.Exp))
                chh.op("act", lambda s, h=h: s.dma_start(
                    out=g16[:nT, :], in_=dt_scr[h:h + 1, :]
                    .rearrange("o (n j) -> (o n) j", j=128)), dma=True)
                chh.op("pe", lambda t: t.transpose(
                    ps[5][0:128, 0:nT], g16[0:nT, :], c_id32[0:nT, 0:nT]))
                chh.op("dve", lambda v, h=h: v.tensor_copy(
                    dtcols[:, h * 16:h * 16 + nT], ps[5][0:128, 0:nT]))
                chh.op("dve", lambda v, h=h: v.tensor_tensor(
                    edcols[:, h * 16:h * 16 + nT],
                    edcols[:, h * 16:h * 16 + nT],
                    dtcols[:, h * 16:h * 16 + nT], ALU.mult))
                col_dep = chh.dep()

            cch = Chain(X, col_dep, kv_spill)
            cch.op("dve", lambda v: v.memset(S_all[:], 0.0))
            for n in range(nT):
                ring = n % 2
                cch.op("act", lambda s, n=n, ring=ring: s.dma_start(
                    out=kvring["k"][:, ring * 1024:ring * 1024 + 1024],
                    in_=k_scr[n * 128:(n + 1) * 128, :]), dma=True)
                cch.op("act", lambda s, n=n, ring=ring: s.dma_start(
                    out=kvring["v"][:, ring * 1024:ring * 1024 + 1024],
                    in_=v_scr[n * 128:(n + 1) * 128, :]), dma=True)
                for h in range(NH):
                    kvh = h // 4
                    idx = h * 16 + n
                    ksl = kvring["k"][:, ring * 1024 + kvh * 128:
                                      ring * 1024 + kvh * 128 + 128]
                    vsl = kvring["v"][:, ring * 1024 + kvh * 128:
                                      ring * 1024 + kvh * 128 + 128]
                    cch.op("dve", lambda v, ksl=ksl, idx=idx: v.tensor_scalar_mul(
                        kdd_sb[:], ksl, edcols[:, idx:idx + 1]))
                    cch.op("pe", lambda t, vsl=vsl, h=h: t.matmul(
                        ps[h % 2][0:128, 0:128], kdd_sb[:], vsl,
                        start=True, stop=True))
                    cch.op("dve", lambda v, h=h, idx=idx: v.scalar_tensor_tensor(
                        S_all[:, h * 128:(h + 1) * 128],
                        S_all[:, h * 128:(h + 1) * 128],
                        cdcols[:, idx:idx + 1], ps[h % 2][0:128, 0:128],
                        ALU.mult, ALU.add))
            chain_dep = cch.dep()
            if "S" in tap_t:
                cs2 = Chain(X, chain_dep)
                cs2.op("pool", lambda g: g.dma_start(out=tap_t["S"],
                                                     in_=S_all[:]), dma=True)

            @block.gpsimd
            def _(g):
                g.wait_ge(X.handle(chain_dep[0]), chain_dep[1])
                g.wait_ge(X.handle(consts_v[0]), consts_v[1])
                g.wait_ge(X.handle("qg"), X.val("qg"))
                par = g.value_load(c_magic[0:1, 1:2])
                off = g.compute_val(par * (128 * NH * 128))
                pslot = AP(pay.tensor, off, [[NH * 128, 128], [1, NH * 128]])
                gv = X.inc("qg", g.dma_start(out=pslot, in_=S_all[:]), 16)
                g.wait_ge(X.handle("qg"), gv)
                foff = g.compute_val(par * 4)
                fslot = AP(flg.tensor, foff, [[4, 1], [1, 4]])
                gv = X.inc("qg", g.dma_start(out=fslot, in_=c_magic[:]), 16)
                g.wait_ge(X.handle("qg"), gv)
                f0 = AP(flg.tensor, 0, [[4, 1], [1, 1]])
                with g.register("creg") as creg, g.register("cnt") as cnt, \
                        g.register("itr") as itr:
                    g.reg_mov(cnt, 0)
                    g.reg_mov(itr, 0)

                    def poll():
                        g.reg_add(cnt, cnt, 16)
                        g.reg_add(itr, itr, 1)
                        g.dma_start(out=sflag[0:1, 0:1], in_=f0).then_inc(
                            X.handle("qpl"), 16)
                        g.wait_ge(X.handle("qpl"), cnt)
                        vv = g.value_load(sflag[0:1, 0:1])
                        mg = g.value_load(c_magic[0:1, 0:1])
                        g.reg_alu(creg, vv, mg, ALU.not_equal)
                        with g.If_cmp(itr, MAXPOLL, "IS_GE"):
                            g.reg_mov(creg, 0)
                    poll()
                    with g.While(creg):
                        poll()
                p0 = AP(pay.tensor, 0, [[NH * 128, 128], [1, NH * 128]])
                gv = X.inc("qg", g.dma_start(out=S_all[:], in_=p0), 16)
                g.wait_ge(X.handle("qg"), gv)
                X.inc("mpo", g.tensor_scalar_mul(S_all[:], S_all[:],
                                                 c_mask[:, 0:1]), 1)
                g.wait_ge(X.handle("mpo"), X.val("mpo"))
                X.inc("mpo", g.tensor_copy(Sb_host[:].bitcast(BF16)[:, 0:DM],
                                           S_all[:]), 1)
            return ("mpo", X.val("mpo"))

        def emit_attnb_groups(abt, sinit_dep, h3_dep):
            kvring = abt["kvring"]
            cols = abt["cols"]
            gcols, dtcols, edcols, cdcols = (cols["g"], cols["dt"], cols["ed"],
                                             cols["cd"])
            cosq, sinq = abt["cosq"], abt["sinq"]
            out_dep = None
            for grp in range(nS):
                row0 = grp * 512
                hfm_dep = emit_hfm_load(h_scr_a, row0, h3_dep,
                                        free_dep=("mpe", X.val("mpe")))

                mdv_guard = X.val("mdv")
                mac_guard = X.val("mac")
                mpo_guard = X.val("mpo")

                @block.sync
                def _(sy, row0=row0, mdv_guard=mdv_guard, mac_guard=mac_guard,
                      mpo_guard=mpo_guard):
                    sy.wait_ge(X.handle("mdv"), mdv_guard)
                    sy.wait_ge(X.handle("mac"), mac_guard)
                    sy.wait_ge(X.handle("mpo"), mpo_guard)
                    X.inc("qrt", sy.dma_start(
                        out=cosq[:], in_=ins["c_cosfm"][:, row0:row0 + 512]), 16)
                    X.inc("qrt", sy.dma_start(
                        out=sinq[:], in_=ins["c_sinfm"][:, row0:row0 + 512]), 16)
                rope_tab = ("qrt", X.val("qrt"))

                pe_base = X.val("mpe")
                mac0 = X.val("mac")
                qst0g = X.val("qst")
                pan_v = {}

                @block.sync
                def _(sy, grp=grp, pe_base=pe_base):
                    for j in range(2 * (DM // 256)):
                        pi, wi = j // 2, j % 2
                        nm = "wq" if wi == 0 else "wg"
                        hq = j % 2
                        free = None
                        if j >= 2:
                            free = ("mpe", pe_base + (j - 1) * 2)
                        pan_v[j] = load_panel_half(sy, nm, pi * 256, hq, free)

                @block.tensor
                def _(t, grp=grp, hfm_dep=hfm_dep, mac0=mac0):
                    t.wait_ge(X.handle(hfm_dep[0]), hfm_dep[1])
                    t.wait_ge(X.handle("mac"), mac0)
                    gidx = 0
                    for j in range(2 * (DM // 256)):
                        pi, wi = j // 2, j % 2
                        hq = j % 2
                        t.wait_ge(X.handle(pan_v[j][0]), pan_v[j][1])
                        for mc2 in range(2):
                            mh = pi * 2 + mc2
                            bank = gidx % 6
                            if gidx >= 6:
                                t.wait_ge(X.handle("mac"), mac0 + gidx - 5)
                            mm = None
                            for k in range(NKC):
                                mm = t.matmul(
                                    ps[bank][:],
                                    wh_slice(hq, k, mc2 * 128, 128),
                                    hfm[:, k * 512:(k + 1) * 512],
                                    start=(k == 0), stop=(k == NKC - 1))
                            X.inc("mpe", mm, 1)
                            gidx += 1

                gate_st = {}

                @block.scalar
                def _(s, grp=grp, pe_base=pe_base):
                    gidx = 0
                    for j in range(2 * (DM // 256)):
                        pi, wi = j // 2, j % 2
                        for mc2 in range(2):
                            mh = pi * 2 + mc2
                            bank = gidx % 6
                            s.wait_ge(X.handle("mpe"), pe_base + gidx + 1)
                            if wi == 0:
                                X.inc("mac", s.copy(
                                    x4[:, mh * 512:(mh + 1) * 512],
                                    ps[bank][:]), 1)
                            else:
                                if gidx >= 4:
                                    s.wait_ge(X.handle("qst"),
                                              gate_st.get(gidx - 4, 0))
                                X.inc("mac", s.copy(
                                    kvring["g2"][:, (gidx % 2) * 512:
                                                 (gidx % 2) * 512 + 512],
                                    ps[bank][:]), 1)
                            gidx += 1

                @block.gpsimd
                def _(g, grp=grp, mac0=mac0):
                    gidx = 0
                    for j in range(2 * (DM // 256)):
                        pi, wi = j // 2, j % 2
                        for mc2 in range(2):
                            mh = pi * 2 + mc2
                            if wi == 1:
                                g.wait_ge(X.handle("mac"), mac0 + gidx + 1)
                                g.wait_ge(X.handle("qst"), X.val("qst"))
                                X.inc("qst", g.dma_start(
                                    out=gate_scr[mh * 128:(mh + 1) * 128, :],
                                    in_=kvring["g2"][:, (gidx % 2) * 512:
                                                     (gidx % 2) * 512 + 512]), 16)
                                gate_st[gidx] = X.val("qst")
                            gidx += 1
                gate_done = ("qst", X.val("qst"))

                pe0r = X.val("mpe")
                mdv0r = X.val("mdv")

                @block.tensor
                def _(t, grp=grp, mac0=mac0, mdv0r=mdv0r):
                    t.wait_ge(X.handle("mac"), mac0 + 64)
                    for mh in range(NH):
                        _gq = (mh // 2) * 4 + (mh % 2)
                        if mh >= 2:
                            t.wait_ge(X.handle("mdv"), mdv0r + 3 * (mh - 1))
                        mm = t.matmul(ps[mh % 2][0:128, 0:512], c_protb[:],
                                      x4[:, mh * 512:(mh + 1) * 512],
                                      start=True, stop=True)
                        X.inc("mpe", mm, 1)

                @block.vector
                def _(v, grp=grp, pe0r=pe0r, qst0g=qst0g):
                    v.wait_ge(X.handle(rope_tab[0]), rope_tab[1])
                    v.wait_ge(X.handle("qst"), qst0g)
                    for mh in range(NH):
                        v.wait_ge(X.handle("mpe"), pe0r + mh + 1)
                        X.inc("mdv", v.tensor_tensor(
                            ep1[:, 0:512], x4[:, mh * 512:(mh + 1) * 512],
                            cosq[:], ALU.mult), 1)
                        X.inc("mdv", v.tensor_tensor(
                            ep1[:, 512:1024], ps[mh % 2][0:128, 0:512],
                            sinq[:], ALU.mult), 1)
                        X.inc("mdv", v.tensor_add(
                            x4[:, mh * 512:(mh + 1) * 512],
                            ep1[:, 0:512], ep1[:, 512:1024]), 1)
                if "q" in tap_t and grp == 0:
                    cq = Chain(X, ("mdv", X.val("mdv")))
                    cq.op("pool", lambda g: g.dma_start(
                        out=tap_t["q"], in_=x4[:, 0:NKC * 512]), dma=True)

                sc = Chain(X, ("mdv", X.val("mdv")), ("mac", X.val("mac")),
                           sinit_dep if grp == 0 else None,
                           *(wtl_done.get("wo") or ()))
                for nl in range(nGT):
                    n = grp * nGT + nl
                    ring = nl % 2
                    sc.op("act", lambda s, n=n, ring=ring: s.dma_start(
                        out=kvring["k"][:, ring * 1024:ring * 1024 + 1024],
                        in_=k_scr[n * 128:(n + 1) * 128, :]), dma=True)
                    sc.op("act", lambda s, n=n, ring=ring: s.dma_start(
                        out=kvring["v"][:, ring * 1024:ring * 1024 + 1024],
                        in_=v_scr[n * 128:(n + 1) * 128, :]), dma=True)
                    for kvh in range(NKV):
                        sc.op("pe", lambda t, ring=ring, kvh=kvh: t.transpose(
                            pbf[kvh % 2][0:128, 0:128],
                            kvring["k"][:, ring * 1024 + kvh * 128:
                                        ring * 1024 + kvh * 128 + 128],
                            c_idb[:]))
                        sc.op("dve", lambda v, kvh=kvh: v.tensor_copy(
                            kvring["kf"][:, kvh * 128:(kvh + 1) * 128],
                            pbf[kvh % 2][0:128, 0:128]))
                    for h in range(NH):
                        kvh = h // 4
                        idx = h * 16 + n
                        qsl = x4[:, h * 512 + nl * 128:h * 512 + nl * 128 + 128]
                        ksl = kvring["k"][:, ring * 1024 + kvh * 128:
                                          ring * 1024 + kvh * 128 + 128]
                        vsl = kvring["v"][:, ring * 1024 + kvh * 128:
                                          ring * 1024 + kvh * 128 + 128]
                        kfs = kvring["kf"][:, kvh * 128:(kvh + 1) * 128]
                        ssl = S_all[:, h * 128:(h + 1) * 128]
                        sbl = Sb(h)
                        sc.op("pe", lambda t, idx=idx: t.transpose(
                            ps[5][0:1, 0:128], gcols[:, idx:idx + 1], c_id32[:]))
                        sc.op("dve", lambda v: v.tensor_copy(
                            grow[0:1, 0:128], ps[5][0:1, 0:128]))
                        sc.op("pe", lambda t: t.matmul(
                            ps[3][0:128, 0:128], c_ones[:], grow[0:1, 0:128],
                            start=True, stop=True))
                        sc.op("dve", lambda v, idx=idx: v.scalar_tensor_tensor(
                            dec_sb[:], ps[3][0:128, 0:128], gcols[:, idx:idx + 1],
                            c_maskneg[:], ALU.subtract, ALU.add))
                        sc.op("act", lambda s: s.activation(
                            dec_sb[:], dec_sb[:], AF.Exp))
                        sc.op("act", lambda s: s.activation(
                            eg_sb[:], ps[3][0:128, 0:128], AF.Exp))
                        sc.op("dve", lambda v, qsl=qsl: v.tensor_tensor(
                            qex_sb[:], qsl, eg_sb[:], ALU.mult))
                        sc.op("pe", lambda t, kfs=kfs, qsl=qsl: t.matmul(
                            ps[0][0:128, 0:128], kfs, qsl, start=True, stop=True))
                        sc.op("dve", lambda v, idx=idx: v.scalar_tensor_tensor(
                            sdd_sb[:], ps[0][0:128, 0:128],
                            dtcols[:, idx:idx + 1], dec_sb[:],
                            ALU.mult, ALU.mult))

                        def ymm(t, vsl=vsl, sbl=sbl):
                            t.matmul(ps[1][0:128, 0:128], vsl, sdd_sb[:],
                                     start=True, stop=False)
                            return t.matmul(ps[1][0:128, 0:128], sbl, qex_sb[:],
                                            start=False, stop=True)
                        sc.op("pe", ymm)
                        sc.op("dve", lambda v, ksl=ksl, idx=idx:
                              v.tensor_scalar_mul(kdd_sb[:], ksl,
                                                  edcols[:, idx:idx + 1]))
                        sc.op("pe", lambda t, vsl=vsl: t.matmul(
                            ps[2][0:128, 0:128], kdd_sb[:], vsl,
                            start=True, stop=True))
                        sc.op("dve", lambda v: v.tensor_copy(
                            ytile[:], ps[1][0:128, 0:128]))
                        sc.op("dve", lambda v, ssl=ssl, idx=idx:
                              v.scalar_tensor_tensor(
                                  ssl, ssl, cdcols[:, idx:idx + 1],
                                  ps[2][0:128, 0:128], ALU.mult, ALU.add))
                        sc.op("dve", lambda v, ssl=ssl, sbl=sbl: v.tensor_copy(
                            sbl, ssl))
                        sc.op("act", lambda s: s.activation(
                            ysq_sb[:], ytile[:], AF.Square))
                        sc.op("pe", lambda t: t.matmul(
                            ps[4][0:128, 0:1], ysq_sb[:], c_onescol[:],
                            start=True, stop=True))
                        sc.op("act", lambda s: s.activation(
                            small[:, 0:1], ps[4][0:128, 0:1], AF.Ln,
                            scale=1.0 / DH, bias=cbias[:, 2:3]))
                        sc.op("act", lambda s: s.activation(
                            small[:, 1:2], small[:, 0:1], AF.Exp, scale=-0.5))
                        sc.op("pe", lambda t: t.transpose(
                            ps[5][0:1, 128:256], small[:, 1:2], c_id32[:]))
                        sc.op("dve", lambda v: v.tensor_copy(
                            grow[0:1, 0:128], ps[5][0:1, 128:256]))
                        sc.op("pe", lambda t: t.matmul(
                            ps[3][0:128, 0:128], c_ones[:], grow[0:1, 0:128],
                            start=True, stop=True))
                        sc.op("dve", lambda v, qsl=qsl: v.scalar_tensor_tensor(
                            qsl, ps[3][0:128, 0:128], gnw_sb[:, 0:1],
                            ytile[:], ALU.mult, ALU.mult))
                gp = Chain(X, sc.dep(), gate_done)
                for cblk in range(NKC):
                    rr = cblk % 2
                    gp.op("act", lambda s, cblk=cblk, rr=rr: s.dma_start(
                        out=kvring["g2"][:, rr * 512:rr * 512 + 512],
                        in_=gate_scr[cblk * 128:(cblk + 1) * 128, :]), dma=True)
                    gp.op("act", lambda s, rr=rr: s.activation(
                        gsl_sb[:], kvring["g2"][:, rr * 512:rr * 512 + 512],
                        AF.Silu))
                    gp.op("dve", lambda v, cblk=cblk: v.tensor_tensor(
                        x4[:, cblk * 512:(cblk + 1) * 512],
                        x4[:, cblk * 512:(cblk + 1) * 512], gsl_sb[:], ALU.mult))
                if "y" in tap_t and grp == 0:
                    cy = Chain(X, gp.dep())
                    cy.op("pool", lambda g: g.dma_start(
                        out=tap_t["y"], in_=x4[:, 0:NKC * 512]), dma=True)
                out_dep = emit_outproj("wo", 512, row0, gp.dep(),
                                       resid_b, resid_a, False)
            return out_dep

        for nm, _M in WSPEC:
            p0b_enqueue(nm)
        p0b_mark_finals()
        p0b_pump(40)
        x1_dep = emit_p1()
        p0b_drain("w12")
        fin_src = None
        ab_dep = None
        if upto == "p1":
            fin_src = resid_a
        else:
            h2_dep = emit_ln_sweep(resid_a, h_scr_b, x1_dep,
                                   taph="h2" if "h2" in tap_t else None)
            p0b_drain("w3")
            op_dep = None
            for sl_i in range(nS):
                hfm_dep = emit_hfm_load(h_scr_b, sl_i * 512, h2_dep,
                                        free_dep=None if sl_i == 0 else
                                        ("mpe", X.val("mpe")))
                g1 = emit_gemm1(hfm_dep)
                p0b_pump(14)
                op_dep = emit_outproj("w3", 512, sl_i * 512, g1,
                                      resid_a, resid_b, True, extra_pump=8)
            if upto == "p2":
                fin_src = resid_b
            else:
                h3_dep = emit_ln_sweep(resid_b, h_scr_a, op_dep,
                                       tapx="x2" if "x2" in tap_t else None,
                                       taph="h3" if "h3" in tap_t else None)
                p0b_drain("wk")
                p0b_drain("wv")
                with ExitStack() as abes:
                    def absb(name, shape, dt_):
                        return abes.enter_context(
                            nc.sbuf_tensor(name, shape, dt_))
                    abt = {}
                    abt["kvring"] = {"k": absb("ab_k", [128, 4096], BF16),
                                     "v": absb("ab_v", [128, 4096], BF16),
                                     "kf": absb("ab_kf", [128, 1024], BF16),
                                     "g2": absb("ab_g2", [128, 1024], BF16)}
                    abt["cols"] = {"g": absb("ab_gc", [128, NH * 16], F32),
                                   "dt": absb("ab_dc", [128, NH * 16], F32),
                                   "ed": absb("ab_ec", [128, NH * 16], F32),
                                   "cd": absb("ab_cc", [128, NH * 16], F32)}
                    abt["ctm2"] = absb("ab_ctm", [128, 128], F32)
                    abt["stm2"] = absb("ab_stm", [128, 128], F32)
                    abt["cosq"] = gsl_sb
                    abt["sinq"] = lnsq
                    abt["dtg"] = {"a": sil[0:NH, 0:512],
                                  "b": sil[0:NH, 512:1024],
                                  "ones": absb("ab_dt1", [NH, 128], F32)}
                    cinit = Chain(X, consts_v)
                    cinit.op("dve", lambda v: v.memset(abt["dtg"]["ones"][:],
                                                       1.0))
                    sinit_dep = emit_attnb(abt, h3_dep)
                    p0b_drain("wq")
                    p0b_drain("wg")
                    p0b_drain("wo")
                    if upto == "p4a":
                        fin_src = resid_b
                    else:
                        ab_dep = emit_attnb_groups(abt, sinit_dep, h3_dep)
                        if upto == "p4":
                            fin_src = resid_a
                if fin_src is None:
                    h4_dep = emit_ln_sweep(resid_a, h_scr_b, ab_dep,
                                           tapx="x3" if "x3" in tap_t else None)
                    for sl_i in range(nS):
                        hfm_dep = emit_hfm_load(h_scr_b, sl_i * 512, h4_dep,
                                                free_dep=None if sl_i == 0 else
                                                ("mpe", X.val("mpe")))
                        g1 = emit_gemm1(hfm_dep)
                        emit_outproj("w3", 512, sl_i * 512, g1,
                                     resid_a, Y_out, True)

        p0b_pump(len(p0b_jobs))

        @block.gpsimd
        def _(g):
            if fin_src is not None:
                g.wait_ge(X.handle("qst"), X.val("qst"))
                g.wait_ge(X.handle("mdv"), X.val("mdv"))
                g.wait_ge(X.handle("mac"), X.val("mac"))
                for i in range(nT):
                    g.wait_ge(X.handle("qg"), X.val("qg"))
                    X.inc("qg", g.dma_start(
                        out=Y_out[i * 128:(i + 1) * 128, :],
                        in_=fin_src[i * 128:(i + 1) * 128, :]), 16)
            g.wait_ge(X.handle("qg"), X.val("qg"))
            g.wait_ge(X.handle("qst"), X.val("qst"))
            g.wait_ge(X.handle("qtp"), X.val("qtp"))

        @block.sync
        def _(sy):
            sy.wait_ge(X.handle("qs"), X.val("qs"))
    return nc



def _host_consts(T, half, rng):
    c = {}
    c["c_ones"] = np.ones((1, 128), np.float32)
    c["c_onescol"] = np.ones((128, 1), np.float32)
    c["c_id32"] = np.eye(128, dtype=np.float32)
    c["c_rev"] = np.eye(128, dtype=np.float32)[::-1].copy()
    i_idx = np.arange(128)
    c["c_maskneg"] = np.where(i_idx[None, :] >= i_idx[:, None], 0.0,
                              -1e30).astype(np.float32)
    prot = np.zeros((128, 128), np.float32)
    for dp in range(64):
        prot[dp, dp + 64] = -1.0
    for dp in range(64, 128):
        prot[dp, dp - 64] = 1.0
    c["c_prot"] = prot.T.copy()
    k = np.arange(4096).reshape(64, 64)
    c["c_omr"] = np.cos(2 * np.pi * k / 4096).astype(np.float32)
    c["c_omi"] = np.sin(2 * np.pi * k / 4096).astype(np.float32)
    kn = np.arange(64)
    F = np.exp(2j * np.pi * kn[:, None] * kn[None, :] / 64)
    c["c_fr"] = F.real.astype(np.float32)
    c["c_fi"] = F.imag.astype(np.float32)
    c["c_fni"] = (-F.imag).astype(np.float32)
    TW = np.exp(2j * np.pi * kn[:, None] * kn[None, :] / 4096)
    c["c_twr"] = TW.real.astype(np.float32)
    c["c_twi"] = TW.imag.astype(np.float32)
    pos0 = half * T
    invf = (10000.0 ** (-np.arange(0, 128, 2) / 128.0))
    pos = pos0 + np.arange(T)
    ang = pos[:, None] * invf[None, :]
    cos_t = np.concatenate([np.cos(ang), np.cos(ang)], 1).astype(np.float32)
    sin_t = np.concatenate([np.sin(ang), np.sin(ang)], 1).astype(np.float32)
    c["c_cosfm"] = cos_t.T.copy()
    c["c_sinfm"] = sin_t.T.copy()
    nT = T // 128
    ctm = np.zeros((128, T), np.float32)
    stm = np.zeros((128, T), np.float32)
    for gi in range(nT):
        ctm[:, gi * 128:(gi + 1) * 128] = cos_t[gi * 128:(gi + 1) * 128, :]
        stm[:, gi * 128:(gi + 1) * 128] = sin_t[gi * 128:(gi + 1) * 128, :]
    c["c_costm"] = ctm
    c["c_sintm"] = stm
    c["c_mask"] = np.full((128, 1), 1.0 if half == 1 else 0.0, np.float32)
    cb = np.zeros((128, 4), np.float32)
    cb[:, 0] = EPS_LN
    cb[:, 1] = 1.0
    cb[:, 2] = EPS_RMS
    c["c_bias"] = cb
    magic = int(rng.integers(1, 2 ** 31 - 1))
    c["c_magic"] = np.array([[magic, half, 0, 0]], np.int32)
    return c


def build_inmaps(inputs, T):
    rng = np.random.default_rng(abs(int(
        np.float64(np.asarray(inputs["L0_attn_a_D"],
                              np.float32).reshape(-1)[0]) * 1e9)) % (2 ** 31)
        + 7919)
    u = np.ascontiguousarray(np.asarray(inputs["L0_attn_a_u"], np.float32))
    maps = []
    for core in range(8):
        b, half = core // 2, core % 2
        m = {"x": np.ascontiguousarray(u[b, half * T:(half + 1) * T, :])}
        m["a_lam"] = np.ascontiguousarray(inputs["L0_attn_a_Lambda_ri"],
                                          np.float32)
        m["a_p"] = np.ascontiguousarray(inputs["L0_attn_a_p_ri"], np.float32)
        m["a_q"] = np.ascontiguousarray(inputs["L0_attn_a_q_ri"], np.float32)
        m["a_vc"] = np.ascontiguousarray(
            inputs["L0_attn_a_Vc_ri"], np.float32).reshape(64, 128)
        m["a_ct"] = np.ascontiguousarray(
            inputs["L0_attn_a_Ct_ri"], np.float32).reshape(64, 2)
        m["a_bp"] = np.ascontiguousarray(inputs["L0_attn_a_Bp"], np.float32)
        m["a_d"] = np.ascontiguousarray(inputs["L0_attn_a_D"],
                                        np.float32).reshape(1, 1)
        m["a_ls"] = np.ascontiguousarray(inputs["L0_attn_a_log_step"],
                                         np.float32).reshape(1, 1)
        m["wq"] = np.ascontiguousarray(inputs["L0_attn_b_Wq"], np.float32)
        m["wk"] = np.ascontiguousarray(inputs["L0_attn_b_Wk"], np.float32)
        m["wv"] = np.ascontiguousarray(inputs["L0_attn_b_Wv"], np.float32)
        m["wo"] = np.ascontiguousarray(inputs["L0_attn_b_Wo"], np.float32)
        m["wg"] = np.ascontiguousarray(inputs["L0_attn_b_Wg"], np.float32)
        m["win"] = np.ascontiguousarray(inputs["L0_attn_b_Win"], np.float32)
        m["dtb"] = np.ascontiguousarray(inputs["L0_attn_b_dt_bias"],
                                        np.float32).reshape(NH, 1)
        m["alb"] = np.ascontiguousarray(inputs["L0_attn_b_A_log_bias"],
                                        np.float32).reshape(NH, 1)
        m["gnw"] = np.ascontiguousarray(inputs["L0_attn_b_gnorm_w"],
                                        np.float32).reshape(DH, 1)
        m["w12"] = np.ascontiguousarray(inputs["L0_ffn_w12"], np.float32)
        m["b12"] = np.ascontiguousarray(
            inputs["L0_ffn_b12"], np.float32).reshape(64, 128).T.copy()
        m["w3"] = np.ascontiguousarray(inputs["L0_ffn_w3"], np.float32)
        m["b3"] = np.ascontiguousarray(inputs["L0_ffn_b3"],
                                       np.float32).reshape(1, DM)
        m.update(_host_consts(T, half, rng))
        maps.append(m)
    return maps


def kernel(**inputs):
    T = 2048
    nc = build(T=T, upto="all")
    maps = build_inmaps(inputs, T)
    res = run_bass_kernel_spmd(nc, maps, core_ids=list(range(8))).results
    out = np.empty((4, 2 * T, DM), np.float32)
    for core in range(8):
        b, half = core // 2, core % 2
        out[b, half * T:(half + 1) * T, :] = res[core]["y"]
    return out


# version 09
# speedup vs baseline: 1.3877x; 1.2222x over previous
import numpy as np
import ml_dtypes
import concourse.bass as bass
import concourse.mybir as mybir
from concourse.bass_utils import run_bass_kernel_spmd
from concourse.ap import AP
from contextlib import ExitStack

F32 = mybir.dt.float32
BF16 = mybir.dt.bfloat16
I32 = mybir.dt.int32
AF = mybir.ActivationFunctionType
ALU = mybir.AluOpType
AX = mybir.AxisListType

DM = 4096
DFF = 8192
NH = 32
NKV = 8
DH = 128
NKC = DM // 128
EPS_LN = 1e-5
EPS_RMS = 1e-5
MAXPOLL = 3_000_000

ENG_ATTR = {"pe": "tensor", "dve": "vector", "act": "scalar",
            "pool": "gpsimd", "sp": "sync"}
PHASE_MARKS = []
ENG_SEM = {"pe": "mpe", "dve": "mdv", "act": "mac", "pool": "mpo", "sp": "msp"}
DMA_SEM = {"pool": "qg", "sp": "qs", "act": "qa"}


class Ctx:
    def __init__(self, nc, block, es):
        self.nc, self.block, self.es = nc, block, es
        self.sems = {}

    def sem(self, name):
        if name not in self.sems:
            self.sems[name] = [self.es.enter_context(self.nc.semaphore(name)), 0]
        return self.sems[name]

    def inc(self, name, instr, n=1):
        s = self.sem(name)
        instr.then_inc(s[0], n)
        s[1] += n
        return s[1]

    def val(self, name):
        return self.sem(name)[1]

    def handle(self, name):
        return self.sem(name)[0]


class Chain:

    def __init__(self, ctx, *deps):
        self.ctx = ctx
        self.prev = [d for d in deps if d is not None]

    def op(self, eng, fn, dma=False, also=()):
        ctx = self.ctx
        prev = self.prev + [a for a in also if a is not None]
        sem = DMA_SEM[eng] if dma else ENG_SEM[eng]
        ser = ctx.val(sem) if dma else None
        dec = getattr(ctx.block, ENG_ATTR[eng])
        out = {}

        def body(e):
            if ser:
                e.wait_ge(ctx.handle(sem), ser)
            for (snm, sval) in prev:
                e.wait_ge(ctx.handle(snm), sval)
            out["v"] = ctx.inc(sem, fn(e), 16 if dma else 1)
        dec(body)
        self.prev = [(sem, out["v"])]
        return self.prev[0]

    def dep(self):
        return self.prev[0]


class Plan:

    def __init__(self, ctx, serial=False):
        self.ctx = ctx
        self.serial = serial
        self.last = None
        self.ops = {e: [] for e in ENG_SEM}
        self.n = {e: ctx.val(ENG_SEM[e]) for e in ENG_SEM}
        self.q = {e: ctx.val(DMA_SEM[e]) for e in DMA_SEM}

    def add(self, eng, fn, deps=()):
        if self.serial:
            deps = tuple(deps) + (self.last,)
        self.n[eng] += 1
        self.ops[eng].append((fn, tuple(d for d in deps if d), False))
        self.last = (ENG_SEM[eng], self.n[eng])
        return self.last

    def dma(self, eng, fn, deps=()):
        if self.serial:
            deps = tuple(deps) + (self.last,)
        self.q[eng] += 16
        self.ops[eng].append((fn, tuple(d for d in deps if d), True))
        self.last = (DMA_SEM[eng], self.q[eng])
        return self.last

    def emit(self):
        ctx = self.ctx
        for eng, ops in self.ops.items():
            if not ops:
                continue
            dec = getattr(ctx.block, ENG_ATTR[eng])

            def body(e, eng=eng, ops=ops):
                seen = {}
                for fn, deps, isdma in ops:
                    if isdma:
                        qs = DMA_SEM[eng]
                        deps = deps + ((qs, ctx.val(qs)),)
                    for snm, sval in deps:
                        if seen.get(snm, -1) >= sval:
                            continue
                        seen[snm] = sval
                        e.wait_ge(ctx.handle(snm), sval)
                    ctx.inc(DMA_SEM[eng] if isdma else ENG_SEM[eng],
                            fn(e), 16 if isdma else 1)
            dec(body)
        for e in ENG_SEM:
            assert self.ctx.val(ENG_SEM[e]) == self.n[e], (e, "plan drift")


def build(T=2048, taps=(), upto="all"):
    nT = T // 128
    nS = T // 512
    nGT = 4

    nc = bass.Bass("TRN2", target_bir_lowering=False)

    ins = {}
    for nm, shp in [("x", [T, DM]),
                    ("a_lam", [64, 2]), ("a_p", [64, 2]), ("a_q", [64, 2]),
                    ("a_vc", [64, 128]), ("a_ct", [64, 2]), ("a_bp", [64, 1]),
                    ("a_d", [1, 1]), ("a_ls", [1, 1]),
                    ("wq", [DM, DM]), ("wk", [1024, DM]), ("wv", [1024, DM]),
                    ("wo", [DM, DM]), ("wg", [DM, DM]), ("win", [NH, DM]),
                    ("dtb", [NH, 1]), ("alb", [NH, 1]), ("gnw", [DH, 1]),
                    ("w12", [DFF, DM]), ("b12", [128, 64]), ("w3", [DM, DM]),
                    ("b3", [1, DM]),
                    ("c_ones", [1, 128]), ("c_onescol", [128, 1]),
                    ("c_id32", [128, 128]), ("c_rev", [128, 128]),
                    ("c_maskneg", [128, 128]), ("c_prot", [128, 128]),
                    ("c_omr", [64, 64]), ("c_omi", [64, 64]),
                    ("c_fr", [64, 64]), ("c_fi", [64, 64]), ("c_fni", [64, 64]),
                    ("c_twr", [64, 64]), ("c_twi", [64, 64]),
                    ("c_cosfm", [128, T]), ("c_sinfm", [128, T]),
                    ("c_costm", [128, T]), ("c_sintm", [128, T]),
                    ("c_mask", [128, 1]), ("c_bias", [128, 4])]:
        ins[nm] = nc.dram_tensor(nm, shp, F32, kind="ExternalInput").ap()
    ins["c_magic"] = nc.dram_tensor("c_magic", [1, 4], I32, kind="ExternalInput").ap()
    X_in = ins["x"]
    Y_out = nc.dram_tensor("y", [T, DM], F32, kind="ExternalOutput").ap()

    WSPEC = [("w12", DFF), ("w3", DM), ("wq", DM), ("wk", 1024),
             ("wv", 1024), ("wg", DM), ("wo", DM)]
    WM = dict(WSPEC)
    wbf = {nm: nc.dram_tensor(f"{nm}_bf", [M, DM], BF16, kind="Internal").ap()
           for nm, M in WSPEC}
    wtl = {nm: nc.dram_tensor(f"{nm}_t", [DM, M], BF16, kind="Internal").ap()
           for nm, M in WSPEC}
    Zbuf = nc.dram_tensor("zbuf", [1, 4352], BF16, kind="Internal").ap()
    resid_a = nc.dram_tensor("resid_a", [T, DM], F32, kind="Internal").ap()
    resid_b = nc.dram_tensor("resid_b", [T, DM], F32, kind="Internal").ap()
    h_scr_a = nc.dram_tensor("h_scr_a", [T, DM], BF16, kind="Internal").ap()
    h_scr_b = nc.dram_tensor("h_scr_b", [T, DM], BF16, kind="Internal").ap()
    k_scr = nc.dram_tensor("k_scr", [T, 1024], BF16, kind="Internal").ap()
    v_scr = nc.dram_tensor("v_scr", [T, 1024], BF16, kind="Internal").ap()
    gate_scr = nc.dram_tensor("gate_scr", [DM, 512], BF16, kind="Internal").ap()
    g_scr = nc.dram_tensor("g_scr", [NH, T], F32, kind="Internal").ap()
    dt_scr = nc.dram_tensor("dt_scr", [NH, T], F32, kind="Internal").ap()
    grows_d = nc.dram_tensor("k_grows", [4, DM], F32, kind="Internal").ap()
    crows_d = nc.dram_tensor("k_crows", [16, DM], F32, kind="Internal").ap()
    pay = nc.dram_tensor("pay", [2 * 128, NH * 128], F32, kind="Internal",
                         addr_space="Shared").ap()
    flg = nc.dram_tensor("flg", [2, 4], I32, kind="Internal",
                         addr_space="Shared").ap()

    tap_t = {}
    TAPSPEC = {"h1": ([T, DM], BF16), "x1": ([T, DM], F32), "K": ([64, 64], F32),
               "h2": ([T, DM], BF16), "x2": ([T, DM], F32), "h3": ([T, DM], BF16),
               "x3": ([T, DM], F32), "kT": ([T, 1024], BF16),
               "vT": ([T, 1024], BF16), "G": ([NH, T], F32),
               "dt": ([NH, T], F32), "S": ([128, NH * 128], F32),
               "q": ([128, NKC * 512], BF16), "y": ([128, NKC * 512], BF16)}
    for tn in taps:
        shp, dt_ = TAPSPEC[tn]
        tap_t[tn] = nc.dram_tensor(f"tap_{tn}", shp, dt_, kind="ExternalOutput").ap()

    es = ExitStack()
    with es:
        def sbuf(name, shape, dt_):
            return es.enter_context(nc.sbuf_tensor(name, shape, dt_))

        c_ones = sbuf("s_ones", [1, 128], F32)
        c_onescol = sbuf("s_onescol", [128, 1], F32)
        c_id32 = sbuf("s_id32", [128, 128], F32)
        c_idb = sbuf("s_idb", [128, 128], BF16)
        c_revb = sbuf("s_revb", [128, 128], BF16)
        c_protb = sbuf("s_protb", [128, 128], BF16)
        c_maskneg = sbuf("s_maskneg", [128, 128], F32)
        c_mask = sbuf("s_mask", [128, 1], F32)
        cbias = sbuf("s_cbias", [128, 4], F32)
        c_magic = sbuf("s_magic", [1, 4], I32)
        b12_sb = sbuf("s_b12", [128, 64], F32)
        b3bf = sbuf("s_b3bf", [1, DM], BF16)
        c_onesb = sbuf("s_onesb", [1, 128], BF16)
        dtb_sb = sbuf("s_dtb", [NH, 1], F32)
        expa_sb = sbuf("s_expa", [NH, 1], F32)
        gnw_sb = sbuf("s_gnw", [DH, 1], F32)
        winT = sbuf("s_winT", [128, NKC * NH], BF16)
        wpan = sbuf("s_wpan", [128, 16384], BF16)
        hfm = sbuf("s_hfm", [128, NKC * 512], BF16)
        x4 = sbuf("s_x4", [128, NKC * 512], BF16)
        p0t = [sbuf(f"s_p0t{i}", [128, 4096], BF16) for i in range(2)]
        xt = [sbuf(f"s_xt{i}", [128, DM], F32) for i in range(2)]
        ep1 = sbuf("s_ep1", [128, 1024], F32)
        sil = sbuf("s_sil", [128, 1024], F32)
        rsl = [sbuf(f"s_rsl{i}", [128, 512], F32) for i in range(2)]
        hbt = sbuf("s_hbt", [128, 512], BF16)
        stat = sbuf("s_stat", [128, 16], F32)
        ytile = sbuf("s_ytile", [128, 128], F32)
        dec_sb = sbuf("s_dec", [128, 128], F32)
        eg_sb = sbuf("s_eg", [128, 128], F32)
        sdd_sb = sbuf("s_sdd", [128, 128], BF16)
        kdd_sb = sbuf("s_kdd", [128, 128], BF16)
        qex_sb = sbuf("s_qex", [128, 128], BF16)
        ysq_sb = sbuf("s_ysq", [128, 128], F32)
        gsl_sb = sbuf("s_gsl", [128, 512], F32)
        lnsq = sbuf("s_lnsq", [128, 512], F32)
        grow = sbuf("s_grow", [1, 128], F32)
        g16 = sbuf("s_g16", [16, 128], F32)
        small = sbuf("s_small", [128, 8], F32)
        sflag = sbuf("s_sflag", [1, 4], I32)
        S_all = xt[0]
        Sb_host = xt[1]

        def Sb(h):
            return Sb_host[:].bitcast(BF16)[:, h * 128:(h + 1) * 128]

        ps = [es.enter_context(nc.psum_tensor(f"ps{i}", [128, 512], F32))
              for i in range(6)]
        pbf = [es.enter_context(nc.psum_tensor(f"pbf{i}", [128, 1024], BF16))
               for i in range(2)]

        block = es.enter_context(nc.Block())
        X = Ctx(nc, block, es)
        for s_ in list(DMA_SEM.values()) + list(ENG_SEM.values()):
            X.sem(s_)
        for i in range(2):
            X.sem(f"qx{i}")
        for i in range(4):
            X.sem(f"qw{i}")
        for i in range(2):
            X.sem(f"qr{i}")
        X.sem("qhf")
        X.sem("qst")
        X.sem("qtp")
        X.sem("qcast")
        X.sem("qpl")
        for i in range(2):
            X.sem(f"qpt{i}")
            X.sem(f"qpw{i}")
        X.sem("qb3")
        X.sem("qrt")

        X.sem("qc")

        @block.sync
        def _(sy):
            for dst, key in [(c_ones, "c_ones"), (c_onescol, "c_onescol"),
                             (c_id32, "c_id32"), (c_maskneg, "c_maskneg"),
                             (c_mask, "c_mask"), (cbias, "c_bias"),
                             (b12_sb, "b12"),
                             (dtb_sb, "dtb"), (expa_sb, "alb"), (gnw_sb, "gnw")]:
                X.inc("qc", sy.dma_start(out=dst[:], in_=ins[key]), 16)
            X.inc("qc", sy.dma_start(out=c_magic[:], in_=ins["c_magic"]), 16)
        consts_v = ("qc", X.val("qc"))

        cast_done = {}

        @block.gpsimd
        def _(g):
            for nm, M in WSPEC:
                g.wait_ge(X.handle("qcast"), X.val("qcast"))
                cast_done[nm] = ("qcast",
                                 X.inc("qcast", g.dma_start(out=wbf[nm],
                                                            in_=ins[nm]), 16))

        ch = Chain(X, consts_v)
        ch.op("dve", lambda v: v.tensor_copy(c_idb[:], c_id32[:]))
        idb_v = ch.dep()
        ch.op("sp", lambda sy: sy.dma_start(out=xt[0][:, 0:128], in_=ins["c_rev"]),
              dma=True)
        ch.op("dve", lambda v: v.tensor_copy(c_revb[:], xt[0][:, 0:128]))
        ch.op("sp", lambda sy: sy.dma_start(out=xt[0][:, 0:128], in_=ins["c_prot"]),
              dma=True)
        ch.op("dve", lambda v: v.tensor_copy(c_protb[:], xt[0][:, 0:128]))
        ch.op("act", lambda s: s.activation(expa_sb[:], expa_sb[:], AF.Exp))
        ch.op("sp", lambda sy: sy.dma_start(out=xt[0][0:1, :], in_=ins["b3"]),
              dma=True)
        ch.op("dve", lambda v: v.tensor_copy(b3bf[:], xt[0][0:1, :]))
        ch.op("dve", lambda v: v.tensor_copy(c_onesb[:], c_ones[:]))
        ch.op("sp", lambda sy: sy.dma_start(out=xt[0][0:NH, :], in_=ins["win"]),
              dma=True)
        ch.op("act", lambda s: s.copy(x4[0:NH, 0:DM], xt[0][0:NH, :]))
        for kc in range(NKC):
            ch.op("pe", lambda t, kc=kc: t.transpose(
                pbf[kc % 2][0:128, 0:NH], x4[0:NH, kc * 128:(kc + 1) * 128],
                c_idb[0:NH, 0:NH]), also=[idb_v])
            ch.op("dve", lambda v, kc=kc: v.tensor_copy(
                winT[:, kc * NH:(kc + 1) * NH], pbf[kc % 2][0:128, 0:NH]))
        prep_v = ch.dep()

        wtl_done = {}
        p0t_last = [0, 0]
        p0b_jobs = []
        p0b_idx = [0]
        p0b_ring = [0]
        p0b_final = {}

        def p0b_enqueue(nm):
            M = WM[nm]
            for kc in range(NKC):
                if M > 4096:
                    p0b_jobs.append((nm, kc, 0, 4096))
                    p0b_jobs.append((nm, kc, 4096, M - 4096))
                else:
                    p0b_jobs.append((nm, kc, 0, M))

        def p0b_mark_finals():
            for nm, M in WSPEC:
                last = [j for j in p0b_jobs if j[0] == nm][-1]
                p0b_final[nm] = last

        def p0b_pump(njobs):
            end = min(p0b_idx[0] + njobs, len(p0b_jobs))
            if end <= p0b_idx[0]:
                return
            jobs = p0b_jobs[p0b_idx[0]:end]
            p0b_idx[0] = end

            @block.sync
            def _(sy):
                for job in jobs:
                    nm, kc, m0, mw = job
                    ring = p0b_ring[0]
                    p0b_ring[0] ^= 1
                    sy.wait_ge(X.handle(cast_done[nm][0]), cast_done[nm][1])
                    if p0t_last[ring]:
                        sy.wait_ge(X.handle(f"qpw{ring}"), p0t_last[ring])
                    tv = X.inc(f"qpt{ring}", sy.dma_start(
                        out=p0t[ring][:, 0:mw],
                        in_=wbf[nm][m0:m0 + mw, kc * 128:(kc + 1) * 128],
                        transpose=True), 16)
                    sy.wait_ge(X.handle(f"qpt{ring}"), tv)
                    p0t_last[ring] = X.inc(f"qpw{ring}", sy.dma_start(
                        out=wtl[nm][kc * 128:(kc + 1) * 128, m0:m0 + mw],
                        in_=p0t[ring][:, 0:mw]), 16)
                    if job == p0b_final.get(nm):
                        wtl_done[nm] = tuple(
                            ("qpw%d" % r, X.val("qpw%d" % r)) for r in (0, 1)
                            if X.val("qpw%d" % r) > 0)

        def p0b_drain(nm):
            while nm not in wtl_done:
                p0b_pump(8)

        def load_panel_half(sy, nm, m0, hq, free_dep):
            if free_dep is None:
                free_dep = ("mpe", X.val("mpe"))
            sy.wait_ge(X.handle(free_dep[0]), free_dep[1])
            for dsem, dval in wtl_done[nm]:
                sy.wait_ge(X.handle(dsem), dval)
            M = WM[nm]
            sem = f"qw{hq}"
            for cc in range(4):
                src = AP(wtl[nm].tensor, (cc * 8 * 128) * M + m0,
                         [[M, 128], [128 * M, 8], [1, 256]])
                dst = AP(wpan[:].tensor, hq * 8192 + cc * 8 * 256,
                         [[16384, 128], [256, 8], [1, 256]])
                X.inc(sem, sy.dma_start(out=dst, in_=src), 16)
            return (sem, X.val(sem))

        def wh_slice(hq, k, off=0, w=256):
            return wpan[:, hq * 8192 + k * 256 + off:hq * 8192 + k * 256 + off + w]

        def load_panel_quarter(sy, nm, m0, q, free_dep):
            if free_dep is None:
                free_dep = ("mpe", X.val("mpe"))
            sy.wait_ge(X.handle(free_dep[0]), free_dep[1])
            for dsem, dval in wtl_done[nm]:
                sy.wait_ge(X.handle(dsem), dval)
            M = WM[nm]
            sem = f"qw{q}"
            for cc in range(2):
                src = AP(wtl[nm].tensor, (cc * 16 * 128) * M + m0,
                         [[M, 128], [128 * M, 16], [1, 128]])
                dst = AP(wpan[:].tensor, q * 4096 + cc * 16 * 128,
                         [[16384, 128], [128, 16], [1, 128]])
                X.inc(sem, sy.dma_start(out=dst, in_=src), 16)
            return (sem, X.val(sem))

        def wqt_slice(q, k):
            return wpan[:, q * 4096 + k * 128:q * 4096 + (k + 1) * 128]

        def emit_hfm_load(h_src, row0, dep, free_dep=None):
            @block.sync
            def _(sy):
                for kc in range(NKC):
                    if dep is not None:
                        sy.wait_ge(X.handle(dep[0]), dep[1])
                    if kc == 0 and free_dep is not None:
                        sy.wait_ge(X.handle(free_dep[0]), free_dep[1])
                    X.inc("qhf", sy.dma_start(
                        out=hfm[:, kc * 512:(kc + 1) * 512],
                        in_=h_src[row0:row0 + 512, kc * 128:(kc + 1) * 128],
                        transpose=True), 16)
            return ("qhf", X.val("qhf"))

        def emit_ln_sweep(x_src, h_dst, dep, tapx=None, taph=None):
            prev_store = None
            xt_cons = []
            for i in range(nT):
                ring = i % 2

                @block.sync
                def _(sy, i=i, ring=ring):
                    sy.wait_ge(X.handle(dep[0]), dep[1])
                    if i >= 2:
                        sy.wait_ge(X.handle("mac"), xt_cons[i - 2])
                    X.inc(f"qx{ring}", sy.dma_start(
                        out=xt[ring][:], in_=x_src[i * 128:(i + 1) * 128, :]), 16)
                xdep = (f"qx{ring}", X.val(f"qx{ring}"))
                c = Chain(X, xdep)
                s1d = c.op("dve", lambda v, ring=ring: v.reduce_sum(
                    stat[:, 0:1], xt[ring][:], axis=AX.X))
                c2 = Chain(X, xdep)
                for sq in range(8):
                    c2.op("act", lambda s, ring=ring, sq=sq: s.activation(
                        lnsq[:], xt[ring][:, sq * 512:(sq + 1) * 512],
                        AF.Square, accum_out=stat[:, 8 + sq:9 + sq]))
                c2.op("dve", lambda v: v.reduce_sum(stat[:, 1:2], stat[:, 8:16],
                                                    axis=AX.X), also=[s1d])
                c2.op("dve", lambda v: v.tensor_scalar(stat[:, 2:3], stat[:, 0:1],
                                                       1.0 / DM, None, ALU.mult))
                c2.op("dve", lambda v: v.tensor_tensor(stat[:, 3:4], stat[:, 2:3],
                                                       stat[:, 2:3], ALU.mult))
                c2.op("dve", lambda v: v.scalar_tensor_tensor(
                    stat[:, 4:5], stat[:, 1:2], 1.0 / DM, stat[:, 3:4],
                    ALU.mult, ALU.subtract))
                c2.op("act", lambda s: s.activation(stat[:, 5:6], stat[:, 4:5],
                                                    AF.Ln, bias=cbias[:, 0:1]))
                c2.op("act", lambda s: s.activation(stat[:, 6:7], stat[:, 5:6],
                                                    AF.Exp, scale=-0.5))
                c2.op("dve", lambda v: v.scalar_tensor_tensor(
                    stat[:, 7:8], stat[:, 2:3], -1.0, stat[:, 6:7],
                    ALU.mult, ALU.mult))
                nv = None
                for sq in range(8):
                    nv = c2.op("act", lambda s, ring=ring, sq=sq: s.activation(
                        hbt[:], xt[ring][:, sq * 512:(sq + 1) * 512], AF.Identity,
                        bias=stat[:, 7:8], scale=stat[:, 6:7]),
                        also=[prev_store] if (sq == 0 and prev_store) else ())
                    prev_store = c2.op("pool", lambda g, i=i, sq=sq: g.dma_start(
                        out=h_dst[i * 128:(i + 1) * 128, sq * 512:(sq + 1) * 512],
                        in_=hbt[:]), dma=True)
                xt_cons.append(nv[1])
                if taph is not None:
                    ct = Chain(X, prev_store)
                    prev_store = ct.op("pool", lambda g, i=i: g.dma_start(
                        out=tap_t[taph][i * 128:(i + 1) * 128, :],
                        in_=h_dst[i * 128:(i + 1) * 128, :]), dma=True)
                if tapx is not None:
                    ct = Chain(X, xdep, ("mac", nv[1]))
                    ct.op("pool", lambda g, i=i, ring=ring: g.dma_start(
                        out=tap_t[tapx][i * 128:(i + 1) * 128, :],
                        in_=xt[ring][:]), dma=True)
            return prev_store

        def emit_kcomp(kv):
            ksc, toe, hbig = kv["ksc"], kv["toe"], kv["hbig"]
            c = Chain(X, consts_v, prep_v)
            for dst, key in [(ksc["lam"], "a_lam"), (ksc["p"], "a_p"),
                             (ksc["q"], "a_q"), (ksc["ct"], "a_ct"),
                             (ksc["vc"], "a_vc"), (ksc["bp"], "a_bp"),
                             (ksc["dD"], "a_d"), (ksc["ls"], "a_ls"),
                             (ksc["omr"], "c_omr"), (ksc["omi"], "c_omi"),
                             (ksc["fr"], "c_fr"), (ksc["fi"], "c_fi"),
                             (ksc["fni"], "c_fni"), (ksc["twr"], "c_twr"),
                             (ksc["twi"], "c_twi")]:
                c.op("act", lambda s, dst=dst, key=key: s.dma_start(
                    out=dst[:], in_=ins[key]), dma=True)
            c.op("pe", lambda t: t.transpose(ps[0][0:1, 0:64], ksc["bp"][:],
                                             c_id32[0:64, 0:64]))
            c.op("dve", lambda v: v.tensor_copy(grow[0:1, 0:64], ps[0][0:1, 0:64]))
            c.op("pe", lambda t: t.matmul(ps[1][0:64, 0:64], c_ones[0:1, 0:64],
                                          grow[0:1, 0:64], start=True, stop=True))
            vc_r = AP(ksc["vc"][:].tensor, 0, [[128, 64], [2, 64]])
            vc_i = AP(ksc["vc"][:].tensor, 1, [[128, 64], [2, 64]])
            c.op("dve", lambda v: v.tensor_tensor(ksc["t0"][:], vc_r,
                                                  ps[1][0:64, 0:64], ALU.mult))
            c.op("dve", lambda v: v.tensor_tensor(ksc["t1"][:], vc_i,
                                                  ps[1][0:64, 0:64], ALU.mult))
            c.op("dve", lambda v: v.reduce_sum(ksc["Bv"][:, 0:1], ksc["t0"][:],
                                               axis=AX.X))
            c.op("dve", lambda v: v.reduce_sum(ksc["Bv"][:, 1:2], ksc["t1"][:],
                                               axis=AX.X))

            def cconj(dst_r, dst_i, a, b):
                ar, ai = a[:, 0:1], a[:, 1:2]
                br, bi = b[:, 0:1], b[:, 1:2]
                c.op("dve", lambda v: v.tensor_tensor(ksc["t0"][:, 0:1], ar, br,
                                                      ALU.mult))
                c.op("dve", lambda v: v.tensor_tensor(ksc["t0"][:, 1:2], ai, bi,
                                                      ALU.mult))
                c.op("dve", lambda v: v.tensor_tensor(ksc["t1"][:, 0:1], ar, bi,
                                                      ALU.mult))
                c.op("dve", lambda v: v.tensor_tensor(ksc["t1"][:, 1:2], ai, br,
                                                      ALU.mult))
                c.op("dve", lambda v: v.tensor_add(dst_r, ksc["t0"][:, 0:1],
                                                   ksc["t0"][:, 1:2]))
                c.op("dve", lambda v: v.tensor_sub(dst_i, ksc["t1"][:, 0:1],
                                                   ksc["t1"][:, 1:2]))
            V = ksc["V"]
            cconj(V[:, 0:1], V[:, 4:5], ksc["ct"], ksc["Bv"])
            cconj(V[:, 1:2], V[:, 5:6], ksc["ct"], ksc["p"])
            cconj(V[:, 2:3], V[:, 6:7], ksc["q"], ksc["Bv"])
            cconj(V[:, 3:4], V[:, 7:8], ksc["q"], ksc["p"])
            c.op("act", lambda s: s.activation(ksc["sc"][0:1, 0:1], ksc["ls"][:],
                                               AF.Exp))
            c.op("dve", lambda v: v.reciprocal(ksc["sc"][0:1, 1:2],
                                               ksc["sc"][0:1, 0:1]))
            c.op("dve", lambda v: v.tensor_scalar(ksc["sc"][0:1, 2:3],
                                                  ksc["sc"][0:1, 1:2], 2.0, None,
                                                  ALU.mult))
            c.op("pe", lambda t: t.matmul(ps[0][0:64, 0:1], c_ones[0:1, 0:64],
                                          ksc["sc"][0:1, 2:3], start=True,
                                          stop=True))
            c.op("dve", lambda v: v.tensor_copy(ksc["rs"][:, 0:1], ps[0][0:64, 0:1]))
            c.op("dve", lambda v: v.tensor_scalar(ksc["t0"][:], ksc["omr"][:],
                                                  1.0, None, ALU.add))
            c.op("dve", lambda v: v.tensor_tensor(ksc["t1"][:], ksc["t0"][:],
                                                  ksc["t0"][:], ALU.mult))
            c.op("dve", lambda v: v.tensor_tensor(ksc["t2"][:], ksc["omi"][:],
                                                  ksc["omi"][:], ALU.mult))
            c.op("dve", lambda v: v.tensor_add(ksc["t3"][:], ksc["t1"][:],
                                               ksc["t2"][:]))
            c.op("dve", lambda v: v.reciprocal(ksc["t3"][:], ksc["t3"][:]))
            c.op("dve", lambda v: v.tensor_scalar(ksc["t1"][:], ksc["omr"][:],
                                                  -1.0, 1.0, ALU.mult, ALU.add))
            c.op("dve", lambda v: v.tensor_tensor(ksc["t1"][:], ksc["t1"][:],
                                                  ksc["t0"][:], ALU.mult))
            c.op("dve", lambda v: v.tensor_sub(ksc["t1"][:], ksc["t1"][:],
                                               ksc["t2"][:]))
            c.op("dve", lambda v: v.tensor_tensor(ksc["gr"][:], ksc["t1"][:],
                                                  ksc["t3"][:], ALU.mult))
            c.op("dve", lambda v: v.tensor_scalar_mul(ksc["gr"][:], ksc["gr"][:],
                                                      ksc["rs"][:, 0:1]))
            c.op("dve", lambda v: v.tensor_tensor(ksc["gi"][:], ksc["omi"][:],
                                                  ksc["t3"][:], ALU.mult))
            c.op("dve", lambda v: v.tensor_scalar(ksc["gi"][:], ksc["gi"][:],
                                                  2.0, None, ALU.mult))
            c.op("dve", lambda v: v.tensor_scalar_mul(ksc["gi"][:], ksc["gi"][:],
                                                      ksc["rs"][:, 0:1]))
            c.op("dve", lambda v: v.tensor_tensor(ksc["cr"][:], ksc["t0"][:],
                                                  ksc["t3"][:], ALU.mult))
            c.op("dve", lambda v: v.tensor_tensor(ksc["ci"][:], ksc["omi"][:],
                                                  ksc["t3"][:], ALU.mult))
            c.op("dve", lambda v: v.tensor_scalar(ksc["cr"][:], ksc["cr"][:],
                                                  2.0, None, ALU.mult))
            c.op("dve", lambda v: v.tensor_scalar(ksc["ci"][:], ksc["ci"][:],
                                                  2.0, None, ALU.mult))
            c.op("pool", lambda g: g.dma_start(
                out=grows_d[0:1, :].rearrange("o (p f) -> (o p) f", p=64),
                in_=ksc["gr"][:]), dma=True)
            c.op("pool", lambda g: g.dma_start(
                out=grows_d[1:2, :].rearrange("o (p f) -> (o p) f", p=64),
                in_=ksc["gi"][:]), dma=True)
            sc32 = 2.0 ** -32
            for i in range(DM // 512):
                c.op("act", lambda s, i=i: s.dma_start(
                    out=ksc["gr8"],
                    in_=grows_d[0:1, i * 512:(i + 1) * 512]), dma=True)
                c.op("act", lambda s, i=i: s.dma_start(
                    out=ksc["gi8"],
                    in_=grows_d[1:2, i * 512:(i + 1) * 512]), dma=True)
                c.op("pe", lambda t: t.matmul(ps[2][0:64, :], c_ones[0:1, 0:64],
                                              ksc["gr8"], start=True,
                                              stop=True))
                c.op("pe", lambda t: t.matmul(ps[3][0:64, :], c_ones[0:1, 0:64],
                                              ksc["gi8"], start=True,
                                              stop=True))
                c.op("dve", lambda v: v.tensor_scalar(
                    ksc["ir"][:], ps[2][0:64, :], ksc["lam"][:, 0:1], sc32,
                    ALU.subtract, ALU.mult))
                c.op("dve", lambda v: v.tensor_scalar(
                    ksc["ii"][:], ps[3][0:64, :], ksc["lam"][:, 1:2], sc32,
                    ALU.subtract, ALU.mult))
                d0 = ksc["d0"][0:64, 0:512]
                d1 = ksc["d1"][0:64, 512:1024]
                c.op("dve", lambda v, d0=d0: v.tensor_tensor(d0, ksc["ir"][:],
                                                             ksc["ir"][:], ALU.mult))
                c.op("dve", lambda v, d1=d1: v.tensor_tensor(d1, ksc["ii"][:],
                                                             ksc["ii"][:], ALU.mult))
                c.op("dve", lambda v, d0=d0, d1=d1: v.tensor_add(d0, d0, d1))
                c.op("dve", lambda v, d0=d0: v.reciprocal(d0, d0))
                c.op("dve", lambda v, d0=d0: v.tensor_tensor(ksc["ir"][:],
                                                             ksc["ir"][:], d0,
                                                             ALU.mult))
                c.op("dve", lambda v, d0=d0: v.tensor_tensor(ksc["ii"][:],
                                                             ksc["ii"][:], d0,
                                                             ALU.mult))
                c.op("dve", lambda v: v.tensor_scalar(ksc["ir"][:], ksc["ir"][:],
                                                      sc32, None, ALU.mult))
                c.op("dve", lambda v: v.tensor_scalar(ksc["ii"][:], ksc["ii"][:],
                                                      -sc32, None, ALU.mult))
                c.op("pe", lambda t: t.matmul(ps[4][0:8, :], ksc["V"][:],
                                              ksc["ir"][:], start=True, stop=True))
                c.op("dve", lambda v: v.tensor_copy(ksc["cs"][:], ps[4][0:8, :]))
                c.op("pool", lambda g, i=i: g.dma_start(
                    out=crows_d[0:8, i * 512:(i + 1) * 512], in_=ksc["cs"][:]),
                    dma=True)
                c.op("pe", lambda t: t.matmul(ps[4][0:8, :], ksc["V"][:],
                                              ksc["ii"][:], start=True, stop=True))
                c.op("dve", lambda v: v.tensor_copy(ksc["cs"][:], ps[4][0:8, :]))
                c.op("pool", lambda g, i=i: g.dma_start(
                    out=crows_d[8:16, i * 512:(i + 1) * 512], in_=ksc["cs"][:]),
                    dma=True)
            for j in range(8):
                c.op("act", lambda s, j=j: s.dma_start(
                    out=ksc["kk"][0:64, j * 64:(j + 1) * 64],
                    in_=crows_d[j:j + 1, :].rearrange("o (p f) -> (o p) f", p=64)),
                    dma=True)
                c.op("act", lambda s, j=j: s.dma_start(
                    out=ksc["kb"][0:64, 512 + j * 64:512 + (j + 1) * 64],
                    in_=crows_d[8 + j:9 + j, :].rearrange("o (p f) -> (o p) f",
                                                          p=64)), dma=True)
            kk = ksc["kk"][0:64, 0:512]
            kb = ksc["kb"][0:64, 512:1024]
            dst = ksc["kcplx"][0:64, 0:512]
            for j in range(4):
                c.op("dve", lambda v, j=j: v.tensor_sub(
                    dst[:, j * 64:(j + 1) * 64], kk[:, j * 64:(j + 1) * 64],
                    kb[:, (4 + j) * 64:(5 + j) * 64]))
                c.op("dve", lambda v, j=j: v.tensor_add(
                    dst[:, (4 + j) * 64:(5 + j) * 64], kb[:, j * 64:(j + 1) * 64],
                    kk[:, (4 + j) * 64:(5 + j) * 64]))
            k00r, k01r, k10r, k11r = (dst[:, j * 64:(j + 1) * 64] for j in range(4))
            k00i, k01i, k10i, k11i = (dst[:, (4 + j) * 64:(5 + j) * 64]
                                      for j in range(4))
            t0, t1, t2, t3, t4, t5 = (ksc[f"t{j}"][:] for j in range(6))
            c.op("dve", lambda v: v.tensor_scalar(t0, k11r, 1.0, None, ALU.add))
            c.op("dve", lambda v: v.tensor_tensor(t1, t0, t0, ALU.mult))
            c.op("dve", lambda v: v.tensor_tensor(t2, k11i, k11i, ALU.mult))
            c.op("dve", lambda v: v.tensor_add(t1, t1, t2))
            c.op("dve", lambda v: v.reciprocal(t1, t1))
            c.op("dve", lambda v: v.tensor_tensor(t2, k01r, k10r, ALU.mult))
            c.op("dve", lambda v: v.tensor_tensor(t4, k01i, k10i, ALU.mult))
            c.op("dve", lambda v: v.tensor_tensor(t3, k01r, k10i, ALU.mult))
            c.op("dve", lambda v: v.tensor_tensor(t5, k01i, k10r, ALU.mult))
            c.op("dve", lambda v: v.tensor_sub(t2, t2, t4))
            c.op("dve", lambda v: v.tensor_add(t3, t3, t5))
            c.op("dve", lambda v: v.tensor_tensor(t4, t2, t0, ALU.mult))
            c.op("dve", lambda v: v.tensor_tensor(t5, t3, k11i, ALU.mult))
            c.op("dve", lambda v: v.tensor_add(t4, t4, t5))
            c.op("dve", lambda v: v.tensor_tensor(t5, t3, t0, ALU.mult))
            c.op("dve", lambda v: v.tensor_tensor(t0, t2, k11i, ALU.mult))
            c.op("dve", lambda v: v.tensor_sub(t5, t5, t0))
            c.op("dve", lambda v: v.tensor_tensor(t4, t4, t1, ALU.mult))
            c.op("dve", lambda v: v.tensor_tensor(t5, t5, t1, ALU.mult))
            c.op("dve", lambda v: v.tensor_sub(t0, k00r, t4))
            c.op("dve", lambda v: v.tensor_sub(t1, k00i, t5))
            c.op("dve", lambda v: v.tensor_tensor(t2, ksc["cr"][:], t0, ALU.mult))
            c.op("dve", lambda v: v.tensor_tensor(t4, ksc["ci"][:], t1, ALU.mult))
            c.op("dve", lambda v: v.tensor_tensor(t3, ksc["cr"][:], t1, ALU.mult))
            c.op("dve", lambda v: v.tensor_tensor(t5, ksc["ci"][:], t0, ALU.mult))
            c.op("dve", lambda v: v.tensor_sub(t2, t2, t4))
            c.op("dve", lambda v: v.tensor_add(t3, t3, t5))

            def s1r_mm(t):
                t.matmul(ps[0][0:64, 0:64], ksc["t2"][:], ksc["fr"][:],
                         start=True, stop=False)
                return t.matmul(ps[0][0:64, 0:64], ksc["t3"][:], ksc["fni"][:],
                                start=False, stop=True)

            def s1i_mm(t):
                t.matmul(ps[1][0:64, 0:64], ksc["t2"][:], ksc["fi"][:],
                         start=True, stop=False)
                return t.matmul(ps[1][0:64, 0:64], ksc["t3"][:], ksc["fr"][:],
                                start=False, stop=True)
            c.op("pe", s1r_mm)
            c.op("pe", s1i_mm)
            c.op("dve", lambda v: v.tensor_copy(ksc["s1r"][:], ps[0][0:64, 0:64]))
            c.op("dve", lambda v: v.tensor_copy(ksc["s1i"][:], ps[1][0:64, 0:64]))
            c.op("dve", lambda v: v.tensor_tensor(ksc["br"][:], ksc["s1r"][:],
                                                  ksc["twr"][:], ALU.mult))
            c.op("dve", lambda v: v.tensor_tensor(ksc["t0"][:], ksc["s1i"][:],
                                                  ksc["twi"][:], ALU.mult))
            c.op("dve", lambda v: v.tensor_tensor(ksc["bi"][:], ksc["s1r"][:],
                                                  ksc["twi"][:], ALU.mult))
            c.op("dve", lambda v: v.tensor_tensor(ksc["t1"][:], ksc["s1i"][:],
                                                  ksc["twr"][:], ALU.mult))
            c.op("dve", lambda v: v.tensor_sub(ksc["br"][:], ksc["br"][:],
                                               ksc["t0"][:]))
            c.op("dve", lambda v: v.tensor_add(ksc["bi"][:], ksc["bi"][:],
                                               ksc["t1"][:]))

            def s2_mm(t):
                t.matmul(ps[2][0:64, 0:64], ksc["fr"][:], ksc["br"][:],
                         start=True, stop=False)
                return t.matmul(ps[2][0:64, 0:64], ksc["fni"][:], ksc["bi"][:],
                                start=False, stop=True)
            c.op("pe", s2_mm)
            c.op("act", lambda s: s.activation(ksc["Ksb"][:], ps[2][0:64, 0:64],
                                               AF.Copy, scale=1.0 / 4096.0))
            c.op("dve", lambda v: v.tensor_add(ksc["Ksb"][0:1, 0:1],
                                               ksc["Ksb"][0:1, 0:1], ksc["dD"][:]))
            if "K" in tap_t:
                c.op("pool", lambda g: g.dma_start(out=tap_t["K"],
                                                   in_=ksc["Ksb"][:]), dma=True)
            c.op("dve", lambda v: v.tensor_copy(ksc["Kbf"][:], ksc["Ksb"][:]))
            c.op("dve", lambda v: v.memset(ksc["z256"][:], 0.0))
            c.op("pool", lambda g: g.dma_start(out=Zbuf[0:1, 0:128],
                                               in_=ksc["z256"][0:1, 0:128]), dma=True)
            c.op("pool", lambda g: g.dma_start(out=Zbuf[0:1, 4224:4352],
                                               in_=ksc["z256"][0:1, 128:256]), dma=True)
            c.op("pool", lambda g: g.dma_start(
                out=Zbuf[0:1, 128:4224].rearrange("o (p f) -> (o p) f", p=64),
                in_=ksc["Kbf"][:]), dma=True)
            c.op("act", lambda s: s.dma_start(
                out=hbig[:], in_=AP(Zbuf.tensor, 0, [[1, 128], [1, 4224]])),
                dma=True)
            c.op("dve", lambda v: v.memset(toe[:, 0:3 * 128], 0.0))
            for tt in range(32):
                c.op("pe", lambda t, tt=tt: t.matmul(
                    ps[tt % 2][0:128, 0:128], c_revb[:],
                    hbig[:, 128 * tt + 1:128 * tt + 129], start=True, stop=True))
                c.op("dve", lambda v, tt=tt: v.tensor_copy(
                    toe[:, (3 + tt) * 128:(4 + tt) * 128],
                    ps[tt % 2][0:128, 0:128]))
            return c.dep()

        def emit_outproj(nm, src_w, row0, src_dep, resid_src, out_dst, with_b3,
                         extra_pump=0):
            npan = DM // 256
            ntiles = nGT
            pe_base = X.val("mpe")
            mdv0 = X.val("mdv")
            qst0 = X.val("qst")
            pan_v = {}
            rs_v = {}
            b3v = {}

            def slice_free(grp):
                return mdv0 + 2 * grp + 1

            def slice_done(grp):
                return mdv0 + 2 * (grp + 1)

            def store_done(grp):
                return qst0 + 16 * (grp + 1)

            @block.sync
            def _(sy):
                for pi in range(npan):
                    free = None
                    if pi >= 2:
                        free = ("mpe", pe_base + (pi - 1) * ntiles)
                    pan_v[pi] = load_panel_half(sy, nm, pi * 256, pi % 2, free)
                    for lt in range(ntiles):
                        k = pi * ntiles + lt
                        ring = k % 2
                        if k >= 2:
                            sy.wait_ge(X.handle("mdv"), slice_done(k - 2))
                        elif mdv0 > 0:
                            sy.wait_ge(X.handle("mdv"), mdv0)
                        gi = row0 // 128 + lt
                        X.inc(f"qr{ring}", sy.dma_start(
                            out=rsl[ring][:, 0:256],
                            in_=resid_src[gi * 128:(gi + 1) * 128,
                                          pi * 256:(pi + 1) * 256]), 16)
                        rs_v[k] = X.val(f"qr{ring}")
            if extra_pump:
                p0b_pump(extra_pump)

            @block.tensor
            def _(t):
                t.wait_ge(X.handle(src_dep[0]), src_dep[1])
                t.wait_ge(X.handle("mdv"), mdv0)
                for pi in range(npan):
                    t.wait_ge(X.handle(pan_v[pi][0]), pan_v[pi][1])
                    for lt in range(ntiles):
                        grp = pi * ntiles + lt
                        bank = grp % 5
                        if grp >= 5:
                            t.wait_ge(X.handle("mdv"), slice_free(grp - 5))
                        mm = None
                        for k in range(NKC):
                            mm = t.matmul(
                                ps[bank][:, 0:256],
                                x4[:, k * src_w + lt * 128:k * src_w + lt * 128 + 128],
                                wh_slice(pi % 2, k),
                                start=(k == 0),
                                stop=(k == NKC - 1 and not with_b3))
                        if with_b3:
                            mm = t.matmul(
                                ps[bank][:, 0:256], c_onesb[0:1, 0:128],
                                b3bf[0:1, pi * 256:(pi + 1) * 256],
                                start=False, stop=True)
                        X.inc("mpe", mm, 1)

            @block.vector
            def _(v):
                for pi in range(npan):
                    for lt in range(ntiles):
                        grp = pi * ntiles + lt
                        bank = grp % 5
                        ring = grp % 2
                        v.wait_ge(X.handle("mpe"), pe_base + grp + 1)
                        v.wait_ge(X.handle(f"qr{ring}"), rs_v[grp])
                        if grp >= 2:
                            v.wait_ge(X.handle("qst"), store_done(grp - 2))
                        elif qst0 > 0:
                            v.wait_ge(X.handle("qst"), qst0)
                        sl = ep1[:, (grp % 2) * 256:(grp % 2) * 256 + 256]
                        X.inc("mdv", v.tensor_add(sl, ps[bank][:, 0:256],
                                                  rsl[ring][:, 0:256]), 1)
                        X.inc("mdv", v.engine_nop(), 1)

            @block.gpsimd
            def _(g):
                for pi in range(npan):
                    for lt in range(ntiles):
                        grp = pi * ntiles + lt
                        gi = row0 // 128 + lt
                        g.wait_ge(X.handle("mdv"), slice_done(grp))
                        g.wait_ge(X.handle("qst"), X.val("qst"))
                        sl = ep1[:, (grp % 2) * 256:(grp % 2) * 256 + 256]
                        X.inc("qst", g.dma_start(
                            out=out_dst[gi * 128:(gi + 1) * 128,
                                        pi * 256:(pi + 1) * 256], in_=sl), 16)
            return ("qst", X.val("qst"))

        def emit_gemm1(hfm_dep):
            npair = DM // 128
            pe_base = X.val("mpe")
            mac0 = X.val("mac")
            mdv0 = X.val("mdv")
            pan_v = {}

            @block.sync
            def _(sy):
                for pi in range(npair):
                    for side in range(2):
                        q = (2 * pi + side) % 4
                        free = None
                        if pi >= 2:
                            free = ("mpe", pe_base + (pi - 1))
                        m0 = pi * 128 + side * (DFF // 2)
                        pan_v[(pi, side)] = load_panel_quarter(sy, "w12", m0, q,
                                                               free)

            @block.tensor
            def _(t):
                t.wait_ge(X.handle(hfm_dep[0]), hfm_dep[1])
                t.wait_ge(X.handle("mac"), mac0)
                t.wait_ge(X.handle("mdv"), mdv0)
                for pi in range(npair):
                    qa = (2 * pi) % 4
                    qb = (2 * pi + 1) % 4
                    t.wait_ge(X.handle(pan_v[(pi, 0)][0]), pan_v[(pi, 0)][1])
                    t.wait_ge(X.handle(pan_v[(pi, 1)][0]), pan_v[(pi, 1)][1])
                    bankA = (2 * pi) % 6
                    bankB = (2 * pi + 1) % 6
                    if pi >= 3:
                        t.wait_ge(X.handle("mac"), mac0 + pi - 2)
                        t.wait_ge(X.handle("mdv"), mdv0 + pi - 2)
                    for k in range(NKC):
                        t.matmul(ps[bankA][:], wqt_slice(qa, k),
                                 hfm[:, k * 512:(k + 1) * 512],
                                 start=(k == 0), stop=(k == NKC - 1))
                    mm = None
                    for k in range(NKC):
                        mm = t.matmul(ps[bankB][:], wqt_slice(qb, k),
                                      hfm[:, k * 512:(k + 1) * 512],
                                      start=(k == 0), stop=(k == NKC - 1))
                    X.inc("mpe", mm, 1)

            @block.scalar
            def _(s):
                for pi in range(npair):
                    bankA = (2 * pi) % 6
                    s.wait_ge(X.handle("mpe"), pe_base + pi + 1)
                    if pi >= 2:
                        s.wait_ge(X.handle("mdv"), mdv0 + pi - 1)
                    X.inc("mac", s.activation(
                        sil[:, (pi % 2) * 512:(pi % 2) * 512 + 512],
                        ps[bankA][:], AF.Silu, bias=b12_sb[:, pi:pi + 1]), 1)

            @block.vector
            def _(v):
                for pi in range(npair):
                    bankB = (2 * pi + 1) % 6
                    v.wait_ge(X.handle("mac"), mac0 + pi + 1)
                    X.inc("mdv", v.scalar_tensor_tensor(
                        x4[:, pi * 512:(pi + 1) * 512], ps[bankB][:],
                        b12_sb[:, 32 + pi:33 + pi],
                        sil[:, (pi % 2) * 512:(pi % 2) * 512 + 512],
                        ALU.add, ALU.mult), 1)
            return ("mdv", X.val("mdv"))

        def emit_p1():
            kv = {}
            with ExitStack() as p1es:
                def p1sb(name, shape, dt_):
                    return p1es.enter_context(nc.sbuf_tensor(name, shape, dt_))
                kv["toe"] = p1sb("p1_toe", [128, 35 * 128], BF16)
                kv["hbig"] = p1sb("p1_hbig", [128, 4224], BF16)
                ksc = {}
                for knm, kshape, kdt in [
                        ("lam", [64, 2], F32), ("p", [64, 2], F32),
                        ("q", [64, 2], F32), ("ct", [64, 2], F32),
                        ("vc", [64, 128], F32), ("bp", [64, 1], F32),
                        ("dD", [1, 1], F32), ("ls", [1, 1], F32),
                        ("omr", [64, 64], F32), ("omi", [64, 64], F32),
                        ("fr", [64, 64], F32), ("fi", [64, 64], F32),
                        ("fni", [64, 64], F32), ("twr", [64, 64], F32),
                        ("twi", [64, 64], F32), ("Bv", [64, 2], F32),
                        ("V", [64, 8], F32), ("sc", [1, 4], F32),
                        ("rs", [64, 1], F32),
                        ("t0", [64, 64], F32), ("t1", [64, 64], F32),
                        ("t2", [64, 64], F32), ("t3", [64, 64], F32),
                        ("t4", [64, 64], F32), ("t5", [64, 64], F32),
                        ("gr", [64, 64], F32), ("gi", [64, 64], F32),
                        ("cr", [64, 64], F32), ("ci", [64, 64], F32),
                        ("ir", [64, 512], F32), ("ii", [64, 512], F32),
                        ("cs", [8, 512], F32),
                        ("s1r", [64, 64], F32), ("s1i", [64, 64], F32),
                        ("br", [64, 64], F32), ("bi", [64, 64], F32),
                        ("Ksb", [64, 64], F32), ("Kbf", [64, 64], BF16),
                        ("z256", [1, 256], BF16)]:
                    ksc[knm] = p1sb("k_" + knm, kshape, kdt)
                ksc["gr8"] = ep1[0:1, 0:512]
                ksc["gi8"] = ep1[0:1, 512:1024]
                ksc["kcplx"] = gsl_sb
                ksc["d0"] = sil
                ksc["d1"] = sil
                ksc["kk"] = ep1
                ksc["kb"] = ep1
                kv["ksc"] = ksc
                toe_dep = emit_kcomp(kv)
                h1_dep = emit_ln_sweep(X_in, h_scr_a, prep_v,
                                       taph="h1" if "h1" in tap_t else None)
                p0b_pump(24)
                prev_pe = None
                for sl_i in range(nS):
                    hfm_dep = emit_hfm_load(h_scr_a, sl_i * 512, h1_dep,
                                            free_dep=prev_pe)
                    pe_base = X.val("mpe")
                    mdv0 = X.val("mdv")
                    qst0 = X.val("qst")
                    rs_v = {}

                    @block.sync
                    def _(sy, sl_i=sl_i, mdv0=mdv0):
                        for lt in range(nGT):
                            gi = sl_i * nGT + lt
                            for pn in range(8):
                                k = lt * 8 + pn
                                ring = k % 2
                                if mdv0 + k - 1 > 0:
                                    sy.wait_ge(X.handle("mdv"), mdv0 + k - 1)
                                X.inc(f"qr{ring}", sy.dma_start(
                                    out=rsl[ring][:],
                                    in_=X_in[gi * 128:(gi + 1) * 128,
                                             pn * 512:(pn + 1) * 512]), 16)
                                rs_v[k] = X.val(f"qr{ring}")

                    @block.tensor
                    def _(t, sl_i=sl_i, hfm_dep=hfm_dep, mdv0=mdv0):
                        t.wait_ge(X.handle(hfm_dep[0]), hfm_dep[1])
                        t.wait_ge(X.handle(toe_dep[0]), toe_dep[1])
                        t.wait_ge(X.handle("mdv"), mdv0)
                        for lt in range(nGT):
                            for pn in range(8):
                                grp = lt * 8 + pn
                                bank = grp % 5
                                if grp >= 5:
                                    t.wait_ge(X.handle("mdv"), mdv0 + grp - 4)
                                L0 = pn * 4
                                jmax = min(L0 + 4, NKC)
                                mm = None
                                for J in range(jmax):
                                    soff = (L0 - J + 3) * 128
                                    mm = t.matmul(
                                        ps[bank][:],
                                        hfm[:, J * 512 + lt * 128:
                                            J * 512 + (lt + 1) * 128],
                                        kv["toe"][:, soff:soff + 512],
                                        start=(J == 0), stop=(J == jmax - 1))
                                X.inc("mpe", mm, 1)

                    @block.vector
                    def _(v, sl_i=sl_i, mdv0=mdv0, pe_base=pe_base, qst0=qst0):
                        for lt in range(nGT):
                            for pn in range(8):
                                grp = lt * 8 + pn
                                bank = grp % 5
                                ring = grp % 2
                                v.wait_ge(X.handle("mpe"), pe_base + grp + 1)
                                v.wait_ge(X.handle(f"qr{ring}"), rs_v[grp])
                                if qst0 + 16 * (grp - 1) > 0:
                                    v.wait_ge(X.handle("qst"),
                                              qst0 + 16 * (grp - 1))
                                sl = ep1[:, (grp % 2) * 512:(grp % 2) * 512 + 512]
                                X.inc("mdv", v.tensor_add(sl, ps[bank][:],
                                                          rsl[ring][:]), 1)

                    @block.gpsimd
                    def _(g, sl_i=sl_i, mdv0=mdv0):
                        for lt in range(nGT):
                            gi = sl_i * nGT + lt
                            for pn in range(8):
                                grp = lt * 8 + pn
                                g.wait_ge(X.handle("mdv"), mdv0 + grp + 1)
                                g.wait_ge(X.handle("qst"), X.val("qst"))
                                sl = ep1[:, (grp % 2) * 512:(grp % 2) * 512 + 512]
                                X.inc("qst", g.dma_start(
                                    out=resid_a[gi * 128:(gi + 1) * 128,
                                                pn * 512:(pn + 1) * 512],
                                    in_=sl), 16)
                    prev_pe = ("mpe", X.val("mpe"))
                    p0b_pump(8)
                x1_dep = ("qst", X.val("qst"))
                if "x1" in tap_t:
                    cx = Chain(X, x1_dep)
                    cx.op("pool", lambda g: g.dma_start(out=tap_t["x1"],
                                                        in_=resid_a), dma=True)
            return x1_dep

        def emit_attnb(abt, h3_dep):
            kvring = abt["kvring"]
            cols = abt["cols"]
            gcols, dtcols, edcols, cdcols = (cols["g"], cols["dt"], cols["ed"],
                                             cols["cd"])
            ctm2, stm2, dtg = abt["ctm2"], abt["stm2"], abt["dtg"]
            prev_pe = None
            dt_deps = []
            kv_spill = None
            for sl_i in range(nS):
                row0 = sl_i * 512
                prev_rope = ("mpo", X.val("mpo")) if sl_i else None
                prev_spill = ("qg", X.val("qg")) if sl_i else None
                hfm_dep = emit_hfm_load(h_scr_a, row0, h3_dep, free_dep=prev_pe)
                pe_base = X.val("mpe")
                mac0 = X.val("mac")
                pan_v = {}
                kq_done = {}

                @block.sync
                def _(sy, sl_i=sl_i, pe_base=pe_base):
                    for j in range(8):
                        nm2 = "wk" if j < 4 else "wv"
                        m0 = (j % 4) * 256
                        slot = j % 2
                        free = None
                        if sl_i == 0:
                            if j >= 2:
                                free = ("mpe", pe_base + (j - 1) * 4)
                        else:
                            free = ("mpe", pe_base + max(0, (j - 1) * 4))
                        pan_v[j] = load_panel_half(sy, nm2, m0, slot, free)

                @block.tensor
                def _(t, sl_i=sl_i, hfm_dep=hfm_dep, mac0=mac0):
                    t.wait_ge(X.handle(hfm_dep[0]), hfm_dep[1])
                    t.wait_ge(X.handle(prep_v[0]), prep_v[1])
                    t.wait_ge(X.handle("mac"), mac0)
                    gidx = 0
                    for j in range(8):
                        slot = j % 2
                        t.wait_ge(X.handle(pan_v[j][0]), pan_v[j][1])
                        for lt in range(nGT):
                            bank = gidx % 6
                            if gidx >= 6:
                                t.wait_ge(X.handle("mac"), mac0 + gidx - 5)
                            mm = None
                            for k in range(NKC):
                                mm = t.matmul(
                                    ps[bank][:, 0:256],
                                    hfm[:, k * 512 + lt * 128:
                                        k * 512 + lt * 128 + 128],
                                    wh_slice(slot, k),
                                    start=(k == 0), stop=(k == NKC - 1))
                            X.inc("mpe", mm, 1)
                            gidx += 1
                    bank = gidx % 6
                    if gidx >= 6:
                        t.wait_ge(X.handle("mac"), mac0 + gidx - 5)
                    mm = None
                    for k in range(NKC):
                        mm = t.matmul(ps[bank][0:NH, :],
                                      winT[:, k * NH:(k + 1) * NH],
                                      hfm[:, k * 512:(k + 1) * 512],
                                      start=(k == 0), stop=(k == NKC - 1))
                    X.inc("mpe", mm, 1)
                dt_bank = 32 % 6

                @block.scalar
                def _(s, sl_i=sl_i, pe_base=pe_base, prev_rope=prev_rope,
                      prev_spill=prev_spill):
                    if prev_rope:
                        s.wait_ge(X.handle(prev_rope[0]), prev_rope[1])
                    if prev_spill:
                        s.wait_ge(X.handle(prev_spill[0]), prev_spill[1])
                    gidx = 0
                    for j in range(8):
                        dstt = "k" if j < 4 else "v"
                        mp = j % 4
                        for lt in range(nGT):
                            bank = gidx % 6
                            s.wait_ge(X.handle("mpe"), pe_base + gidx + 1)
                            X.inc("mac", s.copy(
                                kvring[dstt][:, lt * 1024 + mp * 256:
                                             lt * 1024 + (mp + 1) * 256],
                                ps[bank][:, 0:256]), 1)
                            gidx += 1

                @block.gpsimd
                def _(g, sl_i=sl_i, mac0=mac0):
                    for lt in range(nGT):
                        gi = sl_i * nGT + lt
                        g.wait_ge(X.handle("mac"), mac0 + 3 * 4 + lt + 1)
                        g.wait_ge(X.handle("qg"), X.val("qg"))
                        X.inc("qg", g.dma_start(
                            out=ctm2[:],
                            in_=ins["c_costm"][:, gi * 128:(gi + 1) * 128]), 16)
                        X.inc("qg", g.dma_start(
                            out=stm2[:],
                            in_=ins["c_sintm"][:, gi * 128:(gi + 1) * 128]), 16)
                        g.wait_ge(X.handle("qg"), X.val("qg"))
                        for kvh in range(NKV):
                            base = lt * 1024 + kvh * 128
                            klo = kvring["k"][:, base:base + 64]
                            khi = kvring["k"][:, base + 64:base + 128]
                            X.inc("mpo", g.tensor_tensor(
                                gsl_sb[:, 0:128], kvring["k"][:, base:base + 128],
                                ctm2[:], ALU.mult), 1)
                            X.inc("mpo", g.tensor_tensor(
                                gsl_sb[:, 128:192], khi, stm2[:, 0:64],
                                ALU.mult), 1)
                            X.inc("mpo", g.tensor_tensor(
                                gsl_sb[:, 192:256], klo, stm2[:, 64:128],
                                ALU.mult), 1)
                            g.wait_ge(X.handle("mpo"), X.val("mpo"))
                            X.inc("mpo", g.tensor_sub(
                                klo, gsl_sb[:, 0:64], gsl_sb[:, 128:192]), 1)
                            X.inc("mpo", g.tensor_add(
                                khi, gsl_sb[:, 64:128], gsl_sb[:, 192:256]), 1)
                            g.wait_ge(X.handle("mpo"), X.val("mpo"))
                        g.wait_ge(X.handle("qg"), X.val("qg"))
                        X.inc("qg", g.dma_start(
                            out=k_scr[gi * 128:(gi + 1) * 128, :],
                            in_=kvring["k"][:, lt * 1024:(lt + 1) * 1024]), 16)
                    for lt in range(nGT):
                        gi = sl_i * nGT + lt
                        g.wait_ge(X.handle("mac"), mac0 + 7 * 4 + lt + 1)
                        g.wait_ge(X.handle("qg"), X.val("qg"))
                        X.inc("qg", g.dma_start(
                            out=v_scr[gi * 128:(gi + 1) * 128, :],
                            in_=kvring["v"][:, lt * 1024:(lt + 1) * 1024]), 16)
                kv_spill = ("qg", X.val("qg"))

                cdt = Chain(X, ("mpe", X.val("mpe")),
                            dt_deps[-1] if dt_deps else None)
                cdt.op("act", lambda s: s.activation(
                    dtg["a"], ps[dt_bank][0:NH, :], AF.Abs,
                    bias=dtb_sb[:]))
                cdt.op("act", lambda s: s.activation(
                    dtg["a"], dtg["a"], AF.Exp, scale=-1.0))
                cdt.op("act", lambda s: s.activation(
                    dtg["a"], dtg["a"], AF.Ln,
                    bias=cbias[0:NH, 1:2]))
                cdt.op("act", lambda s: s.activation(
                    dtg["b"], ps[dt_bank][0:NH, :], AF.Relu,
                    bias=dtb_sb[:]))
                cdt.op("dve", lambda v: v.tensor_add(
                    dtg["b"], dtg["b"], dtg["a"]))
                cdt.op("pool", lambda g, row0=row0: g.dma_start(
                    out=dt_scr[:, row0:row0 + 512], in_=dtg["b"]),
                    dma=True)
                cdt.op("dve", lambda v: v.tensor_scalar(
                    dtg["a"], dtg["b"], expa_sb[:], -1.0,
                    ALU.mult, ALU.mult))
                for cc4 in range(4):
                    cdt.op("dve", lambda v, cc4=cc4: v.tensor_tensor_scan(
                        dtg["b"][0:NH, cc4 * 128:(cc4 + 1) * 128],
                        dtg["ones"][:],
                        dtg["a"][0:NH, cc4 * 128:(cc4 + 1) * 128],
                        0.0, ALU.mult, ALU.add))
                cdt.op("pool", lambda g, row0=row0: g.dma_start(
                    out=g_scr[:, row0:row0 + 512], in_=dtg["b"]),
                    dma=True)
                dt_deps.append(cdt.dep())
                prev_pe = ("mpe", X.val("mpe"))
                p0b_pump(10)
            gdt_dep = dt_deps[-1]
            for tn, src in (("G", g_scr), ("dt", dt_scr), ("kT", k_scr),
                            ("vT", v_scr)):
                if tn in tap_t:
                    cg = Chain(X, gdt_dep, kv_spill)
                    cg.op("pool", lambda g, tn=tn, src=src: g.dma_start(
                        out=tap_t[tn], in_=src), dma=True)

            col_dep = None
            for h in range(NH):
                chh = Chain(X, gdt_dep, col_dep)
                chh.op("act", lambda s, h=h: s.dma_start(
                    out=g16[:nT, :], in_=g_scr[h:h + 1, :]
                    .rearrange("o (n j) -> (o n) j", j=128)), dma=True)
                chh.op("pe", lambda t: t.transpose(
                    ps[5][0:128, 0:nT], g16[0:nT, :], c_id32[0:nT, 0:nT]))
                chh.op("dve", lambda v, h=h: v.tensor_copy(
                    gcols[:, h * 16:h * 16 + nT], ps[5][0:128, 0:nT]))
                chh.op("pe", lambda t: t.transpose(
                    ps[5][0:1, 256:256 + nT], g16[0:nT, 127:128],
                    c_id32[0:nT, 0:nT]))
                chh.op("dve", lambda v: v.tensor_copy(
                    grow[0:1, 0:nT], ps[5][0:1, 256:256 + nT]))
                chh.op("act", lambda s: s.activation(
                    grow[0:1, 32:32 + nT], grow[0:1, 0:nT], AF.Exp))
                chh.op("pe", lambda t: t.matmul(
                    ps[5][0:128, 256:256 + nT], c_ones[:], grow[0:1, 32:32 + nT],
                    start=True, stop=True))
                chh.op("dve", lambda v, h=h: v.tensor_copy(
                    cdcols[:, h * 16:h * 16 + nT], ps[5][0:128, 256:256 + nT]))
                chh.op("pe", lambda t: t.matmul(
                    ps[5][0:128, 256:256 + nT], c_ones[:], grow[0:1, 0:nT],
                    start=True, stop=True))
                chh.op("dve", lambda v, h=h: v.tensor_sub(
                    edcols[:, h * 16:h * 16 + nT], ps[5][0:128, 256:256 + nT],
                    gcols[:, h * 16:h * 16 + nT]))
                chh.op("act", lambda s, h=h: s.activation(
                    edcols[:, h * 16:h * 16 + nT],
                    edcols[:, h * 16:h * 16 + nT], AF.Exp))
                chh.op("act", lambda s, h=h: s.dma_start(
                    out=g16[:nT, :], in_=dt_scr[h:h + 1, :]
                    .rearrange("o (n j) -> (o n) j", j=128)), dma=True)
                chh.op("pe", lambda t: t.transpose(
                    ps[5][0:128, 0:nT], g16[0:nT, :], c_id32[0:nT, 0:nT]))
                chh.op("dve", lambda v, h=h: v.tensor_copy(
                    dtcols[:, h * 16:h * 16 + nT], ps[5][0:128, 0:nT]))
                chh.op("dve", lambda v, h=h: v.tensor_tensor(
                    edcols[:, h * 16:h * 16 + nT],
                    edcols[:, h * 16:h * 16 + nT],
                    dtcols[:, h * 16:h * 16 + nT], ALU.mult))
                col_dep = chh.dep()

            cch = Chain(X, col_dep, kv_spill)
            cch.op("dve", lambda v: v.memset(S_all[:], 0.0))
            for n in range(nT):
                ring = n % 2
                cch.op("act", lambda s, n=n, ring=ring: s.dma_start(
                    out=kvring["k"][:, ring * 1024:ring * 1024 + 1024],
                    in_=k_scr[n * 128:(n + 1) * 128, :]), dma=True)
                cch.op("act", lambda s, n=n, ring=ring: s.dma_start(
                    out=kvring["v"][:, ring * 1024:ring * 1024 + 1024],
                    in_=v_scr[n * 128:(n + 1) * 128, :]), dma=True)
                for h in range(NH):
                    kvh = h // 4
                    idx = h * 16 + n
                    ksl = kvring["k"][:, ring * 1024 + kvh * 128:
                                      ring * 1024 + kvh * 128 + 128]
                    vsl = kvring["v"][:, ring * 1024 + kvh * 128:
                                      ring * 1024 + kvh * 128 + 128]
                    cch.op("dve", lambda v, ksl=ksl, idx=idx: v.tensor_scalar_mul(
                        kdd_sb[:], ksl, edcols[:, idx:idx + 1]))
                    cch.op("pe", lambda t, vsl=vsl, h=h: t.matmul(
                        ps[h % 2][0:128, 0:128], kdd_sb[:], vsl,
                        start=True, stop=True))
                    cch.op("dve", lambda v, h=h, idx=idx: v.scalar_tensor_tensor(
                        S_all[:, h * 128:(h + 1) * 128],
                        S_all[:, h * 128:(h + 1) * 128],
                        cdcols[:, idx:idx + 1], ps[h % 2][0:128, 0:128],
                        ALU.mult, ALU.add))
            chain_dep = cch.dep()
            if "S" in tap_t:
                cs2 = Chain(X, chain_dep)
                cs2.op("pool", lambda g: g.dma_start(out=tap_t["S"],
                                                     in_=S_all[:]), dma=True)

            @block.gpsimd
            def _(g):
                g.wait_ge(X.handle(chain_dep[0]), chain_dep[1])
                g.wait_ge(X.handle(consts_v[0]), consts_v[1])
                g.wait_ge(X.handle("qg"), X.val("qg"))
                par = g.value_load(c_magic[0:1, 1:2])
                off = g.compute_val(par * (128 * NH * 128))
                pslot = AP(pay.tensor, off, [[NH * 128, 128], [1, NH * 128]])
                gv = X.inc("qg", g.dma_start(out=pslot, in_=S_all[:]), 16)
                g.wait_ge(X.handle("qg"), gv)
                foff = g.compute_val(par * 4)
                fslot = AP(flg.tensor, foff, [[4, 1], [1, 4]])
                gv = X.inc("qg", g.dma_start(out=fslot, in_=c_magic[:]), 16)
                g.wait_ge(X.handle("qg"), gv)
                f0 = AP(flg.tensor, 0, [[4, 1], [1, 1]])
                with g.register("creg") as creg, g.register("cnt") as cnt, \
                        g.register("itr") as itr:
                    g.reg_mov(cnt, 0)
                    g.reg_mov(itr, 0)

                    def poll():
                        g.reg_add(cnt, cnt, 16)
                        g.reg_add(itr, itr, 1)
                        g.dma_start(out=sflag[0:1, 0:1], in_=f0).then_inc(
                            X.handle("qpl"), 16)
                        g.wait_ge(X.handle("qpl"), cnt)
                        vv = g.value_load(sflag[0:1, 0:1])
                        mg = g.value_load(c_magic[0:1, 0:1])
                        g.reg_alu(creg, vv, mg, ALU.not_equal)
                        with g.If_cmp(itr, MAXPOLL, "IS_GE"):
                            g.reg_mov(creg, 0)
                    poll()
                    with g.While(creg):
                        poll()
                p0 = AP(pay.tensor, 0, [[NH * 128, 128], [1, NH * 128]])
                gv = X.inc("qg", g.dma_start(out=S_all[:], in_=p0), 16)
                g.wait_ge(X.handle("qg"), gv)
                X.inc("mpo", g.tensor_scalar_mul(S_all[:], S_all[:],
                                                 c_mask[:, 0:1]), 1)
                g.wait_ge(X.handle("mpo"), X.val("mpo"))
                X.inc("mpo", g.tensor_copy(Sb_host[:].bitcast(BF16)[:, 0:DM],
                                           S_all[:]), 1)
            return ("mpo", X.val("mpo"))

        def emit_attnb_groups(abt, sinit_dep, h3_dep):
            kvring = abt["kvring"]
            cols = abt["cols"]
            gcols, dtcols, edcols, cdcols = (cols["g"], cols["dt"], cols["ed"],
                                             cols["cd"])
            cosq, sinq = abt["cosq"], abt["sinq"]
            decr_l = [dec_sb, eg_sb]
            egxr_l = [ytile, ysq_sb]
            ynr_l = abt["yn"]
            sddr_l = [sdd_sb, abt["sdd1"]]
            kddr_l = [kdd_sb, abt["kdd1"]]
            qexr_l = [qex_sb, abt["qex1"]]
            colb_l = abt["colb"]
            gnwb = abt["gnwb"]
            nTt = T // 128
            cgn = Chain(X, sinit_dep)
            cgn.op("pe", lambda t: t.transpose(ps[5][0:1, 0:128], gnw_sb[:],
                                               c_id32[:]))
            cgn.op("dve", lambda v: v.tensor_copy(grow[0:1, 0:128],
                                                  ps[5][0:1, 0:128]))
            cgn.op("pe", lambda t: t.matmul(ps[5][0:128, 128:256], c_ones[:],
                                            grow[0:1, 0:128],
                                            start=True, stop=True))
            cgn.op("dve", lambda v: v.tensor_copy(gnwb[:],
                                                  ps[5][0:128, 128:256]))
            cgn.op("dve", lambda v: v.memset(colb_l[0][:], 0.0))
            cgn.op("dve", lambda v: v.memset(colb_l[1][:], 0.0))
            gnwb_dep = cgn.dep()
            sd = {k: {} for k in
                  ("kload", "vload", "kfT", "kfc", "colb", "M1", "M2",
                   "M3a", "M3b", "M4", "Tyn", "decstt", "sdd", "qex", "ynstt",
                   "qslc", "Sstt", "egx", "dexp", "rexp", "sbc", "kdd", "ysq",
                   "M2e", "M4e")}
            out_dep = None
            for grp in range(nS):
                row0 = grp * 512
                hfm_dep = emit_hfm_load(h_scr_a, row0, h3_dep,
                                        free_dep=("mpe", X.val("mpe")))

                mdv_guard = X.val("mdv")
                mac_guard = X.val("mac")
                mpo_guard = X.val("mpo")

                @block.sync
                def _(sy, row0=row0, mdv_guard=mdv_guard, mac_guard=mac_guard,
                      mpo_guard=mpo_guard):
                    sy.wait_ge(X.handle("mdv"), mdv_guard)
                    sy.wait_ge(X.handle("mac"), mac_guard)
                    sy.wait_ge(X.handle("mpo"), mpo_guard)
                    X.inc("qrt", sy.dma_start(
                        out=cosq[:], in_=ins["c_cosfm"][:, row0:row0 + 512]), 16)
                    X.inc("qrt", sy.dma_start(
                        out=sinq[:], in_=ins["c_sinfm"][:, row0:row0 + 512]), 16)
                rope_tab = ("qrt", X.val("qrt"))

                pe_base = X.val("mpe")
                mac0 = X.val("mac")
                qst0g = X.val("qst")
                pan_v = {}

                @block.sync
                def _(sy, grp=grp, pe_base=pe_base):
                    for j in range(2 * (DM // 256)):
                        pi, wi = j // 2, j % 2
                        nm = "wq" if wi == 0 else "wg"
                        hq = j % 2
                        free = None
                        if j >= 2:
                            free = ("mpe", pe_base + (j - 1) * 2)
                        pan_v[j] = load_panel_half(sy, nm, pi * 256, hq, free)

                mdv_proj = X.val("mdv")

                @block.tensor
                def _(t, grp=grp, hfm_dep=hfm_dep, mac0=mac0,
                      mdv_proj=mdv_proj):
                    t.wait_ge(X.handle(hfm_dep[0]), hfm_dep[1])
                    t.wait_ge(X.handle("mac"), mac0)
                    t.wait_ge(X.handle("mdv"), mdv_proj)
                    gidx = 0
                    for j in range(2 * (DM // 256)):
                        pi, wi = j // 2, j % 2
                        hq = j % 2
                        t.wait_ge(X.handle(pan_v[j][0]), pan_v[j][1])
                        for mc2 in range(2):
                            mh = pi * 2 + mc2
                            bank = gidx % 6
                            if gidx >= 6:
                                t.wait_ge(X.handle("mac"), mac0 + gidx - 5)
                            mm = None
                            for k in range(NKC):
                                mm = t.matmul(
                                    ps[bank][:],
                                    wh_slice(hq, k, mc2 * 128, 128),
                                    hfm[:, k * 512:(k + 1) * 512],
                                    start=(k == 0), stop=(k == NKC - 1))
                            X.inc("mpe", mm, 1)
                            gidx += 1

                gate_st = {}

                @block.scalar
                def _(s, grp=grp, pe_base=pe_base):
                    gidx = 0
                    for j in range(2 * (DM // 256)):
                        pi, wi = j // 2, j % 2
                        for mc2 in range(2):
                            mh = pi * 2 + mc2
                            bank = gidx % 6
                            s.wait_ge(X.handle("mpe"), pe_base + gidx + 1)
                            if wi == 0:
                                X.inc("mac", s.copy(
                                    x4[:, mh * 512:(mh + 1) * 512],
                                    ps[bank][:]), 1)
                            else:
                                if gidx >= 4:
                                    s.wait_ge(X.handle("qst"),
                                              gate_st.get(gidx - 4, 0))
                                X.inc("mac", s.copy(
                                    kvring["g2"][:, (gidx % 2) * 512:
                                                 (gidx % 2) * 512 + 512],
                                    ps[bank][:]), 1)
                            gidx += 1

                @block.gpsimd
                def _(g, grp=grp, mac0=mac0):
                    gidx = 0
                    for j in range(2 * (DM // 256)):
                        pi, wi = j // 2, j % 2
                        for mc2 in range(2):
                            mh = pi * 2 + mc2
                            if wi == 1:
                                g.wait_ge(X.handle("mac"), mac0 + gidx + 1)
                                g.wait_ge(X.handle("qst"), X.val("qst"))
                                X.inc("qst", g.dma_start(
                                    out=gate_scr[mh * 128:(mh + 1) * 128, :],
                                    in_=kvring["g2"][:, (gidx % 2) * 512:
                                                     (gidx % 2) * 512 + 512]), 16)
                                gate_st[gidx] = X.val("qst")
                            gidx += 1
                gate_done = ("qst", X.val("qst"))

                pe0r = X.val("mpe")
                mdv0r = X.val("mdv")

                @block.tensor
                def _(t, grp=grp, mac0=mac0, mdv0r=mdv0r):
                    t.wait_ge(X.handle("mac"), mac0 + 64)
                    t.wait_ge(X.handle("mdv"), mdv0r)
                    for mh in range(NH):
                        _gq = (mh // 2) * 4 + (mh % 2)
                        if mh >= 2:
                            t.wait_ge(X.handle("mdv"), mdv0r + 3 * (mh - 1))
                        mm = t.matmul(ps[mh % 2][0:128, 0:512], c_protb[:],
                                      x4[:, mh * 512:(mh + 1) * 512],
                                      start=True, stop=True)
                        X.inc("mpe", mm, 1)

                @block.vector
                def _(v, grp=grp, pe0r=pe0r, qst0g=qst0g):
                    v.wait_ge(X.handle(rope_tab[0]), rope_tab[1])
                    v.wait_ge(X.handle("qst"), qst0g)
                    for mh in range(NH):
                        v.wait_ge(X.handle("mpe"), pe0r + mh + 1)
                        X.inc("mdv", v.tensor_tensor(
                            ep1[:, 0:512], x4[:, mh * 512:(mh + 1) * 512],
                            cosq[:], ALU.mult), 1)
                        X.inc("mdv", v.tensor_tensor(
                            ep1[:, 512:1024], ps[mh % 2][0:128, 0:512],
                            sinq[:], ALU.mult), 1)
                        X.inc("mdv", v.tensor_add(
                            x4[:, mh * 512:(mh + 1) * 512],
                            ep1[:, 0:512], ep1[:, 512:1024]), 1)
                if "q" in tap_t and grp == 0:
                    cq = Chain(X, ("mdv", X.val("mdv")))
                    cq.op("pool", lambda g: g.dma_start(
                        out=tap_t["q"], in_=x4[:, 0:NKC * 512]), dma=True)

                P = Plan(X, serial=bool(globals().get("SCAN_SERIAL")))
                rope_end = ("mdv", X.val("mdv"))
                mac_ent = ("mac", X.val("mac"))
                g0t = grp * nGT * NH

                def eloads(gx2):
                    if gx2 >= nTt:
                        return
                    r2 = gx2 % 2
                    war = [sd["M4e"].get(gx2 - 2),
                           sinit_dep if gx2 < 2 else None]
                    sd["kload"][gx2] = P.dma(
                        "pool", lambda g, gx2=gx2, r2=r2: g.dma_start(
                            out=kvring["k"][:, r2 * 1024:r2 * 1024 + 1024],
                            in_=k_scr[gx2 * 128:(gx2 + 1) * 128, :]), war)
                    sd["vload"][gx2] = P.dma(
                        "pool", lambda g, gx2=gx2, r2=r2: g.dma_start(
                            out=kvring["v"][:, r2 * 1024:r2 * 1024 + 1024],
                            in_=v_scr[gx2 * 128:(gx2 + 1) * 128, :]), war)
                if grp == 0:
                    eloads(0)
                    eloads(1)
                for nl in range(nGT):
                    gx = grp * nGT + nl
                    ring = gx % 2
                    kfb = kvring["kf"] if ring == 0 else kvring["g2"]
                    for kvh in range(NKV):
                        s_ = gx * 8 + kvh
                        poff = ((s_ // 2) % 8) * 128
                        sd["kfT"][(gx, kvh)] = P.add(
                            "pe", lambda t, ring=ring, kvh=kvh, poff=poff:
                            t.transpose(
                                pbf[kvh % 2][0:128, poff:poff + 128],
                                kvring["k"][:, ring * 1024 + kvh * 128:
                                            ring * 1024 + kvh * 128 + 128],
                                c_idb[:]),
                            [sd["kload"][gx],
                             sd["kfc"].get(divmod(s_ - 2, 8) if s_ >= 2
                                           else (-1, 0))])
                        sd["kfc"][(gx, kvh)] = P.add(
                            "act", lambda s, kvh=kvh, poff=poff, kfb=kfb:
                            s.copy(
                                kfb[:, kvh * 128:(kvh + 1) * 128],
                                pbf[kvh % 2][0:128, poff:poff + 128]),
                            [sd["kfT"][(gx, kvh)], sd["M2e"].get(gx - 2),
                             gate_done if ring == 1 else None])
                    for h in range(NH):
                        t_ = gx * NH + h
                        p2 = t_ % 2
                        p4 = t_ % 4
                        kvh = h // 4
                        cidx = h * 16 + gx
                        ent = ([rope_end, mac_ent, gnwb_dep]
                               if t_ - g0t < 4 else [])
                        qsl = x4[:, h * 512 + nl * 128:
                                 h * 512 + nl * 128 + 128]
                        ksl = kvring["k"][:, ring * 1024 + kvh * 128:
                                          ring * 1024 + kvh * 128 + 128]
                        vsl = kvring["v"][:, ring * 1024 + kvh * 128:
                                          ring * 1024 + kvh * 128 + 128]
                        kfs = kfb[:, kvh * 128:(kvh + 1) * 128]
                        ssl = S_all[:, h * 128:(h + 1) * 128]
                        sbl = Sb(h)
                        bA = ps[0] if p2 == 0 else ps[2]
                        bB = ps[1] if p2 == 0 else ps[3]
                        ps0 = bA[:, 0:128]
                        ps2 = bA[:, 128:256]
                        ps3 = bB[:, 0:128]
                        ps1 = bB[:, 128:256]
                        ps5q = (ps[4] if p2 == 0 else ps[5])[:, 0:128]
                        ssc = small[:, p4:p4 + 1]
                        rsc = small[:, 4 + p4:5 + p4]
                        dec = decr_l[p2]
                        egx = egxr_l[p2]
                        sdd = sddr_l[p2]
                        kdd = kddr_l[p2]
                        qex = qexr_l[p2]
                        ynr = ynr_l[p2]
                        colb = colb_l[p2]
                        sd["colb"][t_] = P.add(
                            "dve", lambda v, colb=colb, cidx=cidx:
                            v.tensor_scalar(colb[:], colb[:], 0.0,
                                            gcols[:, cidx:cidx + 1],
                                            ALU.mult, ALU.add),
                            [sd["M1"].get(t_ - 2), gnwb_dep])
                        sd["M1"][t_] = P.add(
                            "pe", lambda t, ps3=ps3, colb=colb: t.transpose(
                                ps3, colb[:], c_id32[:]),
                            [sd["colb"][t_], sd["ynstt"].get(t_ - 2)] + ent)
                        sd["egx"][t_] = P.add(
                            "act", lambda s, egx=egx, ps3=ps3: s.activation(
                                egx[:], ps3, AF.Exp),
                            [sd["M1"][t_], sd["qex"].get(t_ - 2)])
                        sd["decstt"][t_] = P.add(
                            "dve", lambda v, dec=dec, ps3=ps3, cidx=cidx:
                            v.scalar_tensor_tensor(
                                dec[:], ps3, gcols[:, cidx:cidx + 1],
                                c_maskneg[:], ALU.subtract, ALU.add),
                            [sd["egx"][t_]])
                        sd["dexp"][t_] = P.add(
                            "act", lambda s, dec=dec: s.activation(
                                dec[:], dec[:], AF.Exp),
                            [sd["decstt"][t_]])
                        sd["qex"][t_] = P.add(
                            "dve", lambda v, qex=qex, qsl=qsl, egx=egx:
                            v.tensor_tensor(qex[:], qsl, egx[:], ALU.mult),
                            [sd["egx"][t_], sd["M3b"].get(t_ - 2)])
                        sd["M2"][t_] = P.add(
                            "pe", lambda t, ps0=ps0, kfs=kfs, qsl=qsl:
                            t.matmul(ps0, kfs, qsl, start=True, stop=True),
                            [sd["kfc"][(gx, kvh)], sd["Sstt"].get(t_ - 2)]
                            + ent)
                        sd["sdd"][t_] = P.add(
                            "dve", lambda v, sdd=sdd, ps0=ps0, cidx=cidx,
                            dec=dec: v.scalar_tensor_tensor(
                                sdd[:], ps0, dtcols[:, cidx:cidx + 1],
                                dec[:], ALU.mult, ALU.mult),
                            [sd["M2"][t_], sd["dexp"][t_]])
                        sd["M3a"][t_] = P.add(
                            "pe", lambda t, ps1=ps1, sdd=sdd, vsl=vsl:
                            t.matmul(ps1, sdd[:], vsl, start=True, stop=False),
                            [sd["sdd"][t_], sd["vload"][gx], sd["egx"][t_]]
                            + ent)
                        sd["M3b"][t_] = P.add(
                            "pe", lambda t, ps1=ps1, qex=qex, sbl=sbl:
                            t.matmul(ps1, qex[:], sbl, start=False, stop=True),
                            [sd["qex"][t_],
                             sd["sbc"].get(t_ - 32,
                                           sinit_dep if t_ < 32 else None)])
                        sd["kdd"][t_] = P.add(
                            "dve", lambda v, kdd=kdd, ksl=ksl, cidx=cidx:
                            v.tensor_scalar_mul(kdd[:], ksl,
                                                edcols[:, cidx:cidx + 1]),
                            [sd["kload"][gx], sd["M4"].get(t_ - 2)])
                        sd["M4"][t_] = P.add(
                            "pe", lambda t, ps2=ps2, kdd=kdd, vsl=vsl:
                            t.matmul(ps2, kdd[:], vsl, start=True, stop=True),
                            [sd["kdd"][t_]] + ent)
                        sd["Sstt"][t_] = P.add(
                            "dve", lambda v, ssl=ssl, cidx=cidx, ps2=ps2:
                            v.scalar_tensor_tensor(
                                ssl, ssl, cdcols[:, cidx:cidx + 1], ps2,
                                ALU.mult, ALU.add),
                            [sd["M4"][t_],
                             sd["sbc"].get(t_ - 32,
                                           sinit_dep if t_ < 32 else None)])
                        sd["sbc"][t_] = P.add(
                            "dve", lambda v, sbl=sbl, ssl=ssl:
                            v.tensor_copy(sbl, ssl),
                            [sd["Sstt"][t_], sd["M3b"][t_]])
                        ysqd = P.add(
                            "act", lambda s, ps1=ps1, ps5q=ps5q, ssc=ssc:
                            s.activation(ps5q, ps1, AF.Square, accum_out=ssc),
                            [sd["M3b"][t_], sd["qslc"].get(t_ - 2)])
                        sd["ysq"][t_] = ysqd
                        lnd = P.add(
                            "act", lambda s, ssc=ssc, rsc=rsc: s.activation(
                                rsc, ssc, AF.Ln, scale=1.0 / DH,
                                bias=cbias[:, 2:3]),
                            [ysqd, sd["ynstt"].get(t_ - 4)])
                        sd["rexp"][t_] = P.add(
                            "act", lambda s, rsc=rsc: s.activation(
                                rsc, rsc, AF.Exp, scale=-0.5),
                            [lnd])
                        sd["ynstt"][t_] = P.add(
                            "dve", lambda v, ynr=ynr, ps1=ps1, rsc=rsc:
                            v.scalar_tensor_tensor(
                                ynr[:], ps1, rsc, gnwb[:], ALU.mult, ALU.mult),
                            [sd["rexp"][t_], sd["Tyn"].get(t_ - 2)])
                        sd["Tyn"][t_] = P.add(
                            "pe", lambda t, ps5q=ps5q, ynr=ynr: t.transpose(
                                ps5q, ynr[:], c_id32[:]),
                            [sd["ynstt"][t_], sd["ysq"][t_]])
                        sd["qslc"][t_] = P.add(
                            "dve", lambda v, qsl=qsl, ps5q=ps5q:
                            v.tensor_copy(qsl, ps5q),
                            [sd["Tyn"][t_]])
                    sd["M2e"][gx] = sd["M2"][gx * NH + NH - 1]
                    sd["M4e"][gx] = sd["M4"][gx * NH + NH - 1]
                    eloads(gx + 2)
                P.emit()
                scan_end = ("mdv", P.n["dve"])
                gp = Chain(X, scan_end, gate_done)
                for cblk in range(NKC):
                    rr = cblk % 2
                    gp.op("act", lambda s, cblk=cblk, rr=rr: s.dma_start(
                        out=kvring["g2"][:, rr * 512:rr * 512 + 512],
                        in_=gate_scr[cblk * 128:(cblk + 1) * 128, :]), dma=True)
                    gp.op("act", lambda s, rr=rr: s.activation(
                        gsl_sb[:], kvring["g2"][:, rr * 512:rr * 512 + 512],
                        AF.Silu))
                    gp.op("dve", lambda v, cblk=cblk: v.tensor_tensor(
                        x4[:, cblk * 512:(cblk + 1) * 512],
                        x4[:, cblk * 512:(cblk + 1) * 512], gsl_sb[:], ALU.mult))
                if "y" in tap_t and grp == 0:
                    cy = Chain(X, gp.dep())
                    cy.op("pool", lambda g: g.dma_start(
                        out=tap_t["y"], in_=x4[:, 0:NKC * 512]), dma=True)
                out_dep = emit_outproj("wo", 512, row0, gp.dep(),
                                       resid_b, resid_a, False)
            return out_dep

        def mark(nm2):
            PHASE_MARKS.append((nm2, nc.n_instructions()))

        PHASE_MARKS.clear()
        for nm, _M in WSPEC:
            p0b_enqueue(nm)
        p0b_mark_finals()
        p0b_pump(40)
        mark("P1")
        x1_dep = emit_p1()
        p0b_drain("w12")
        fin_src = None
        ab_dep = None
        if upto == "p1":
            fin_src = resid_a
        else:
            mark("h2")
            h2_dep = emit_ln_sweep(resid_a, h_scr_b, x1_dep,
                                   taph="h2" if "h2" in tap_t else None)
            p0b_drain("w3")
            op_dep = None
            mark("FFN1")
            for sl_i in range(nS):
                hfm_dep = emit_hfm_load(h_scr_b, sl_i * 512, h2_dep,
                                        free_dep=None if sl_i == 0 else
                                        ("mpe", X.val("mpe")))
                g1 = emit_gemm1(hfm_dep)
                p0b_pump(14)
                op_dep = emit_outproj("w3", 512, sl_i * 512, g1,
                                      resid_a, resid_b, True, extra_pump=8)
            if upto == "p2":
                fin_src = resid_b
            else:
                mark("h3")
                h3_dep = emit_ln_sweep(resid_b, h_scr_a, op_dep,
                                       tapx="x2" if "x2" in tap_t else None,
                                       taph="h3" if "h3" in tap_t else None)
                p0b_drain("wk")
                p0b_drain("wv")
                with ExitStack() as abes:
                    def absb(name, shape, dt_):
                        return abes.enter_context(
                            nc.sbuf_tensor(name, shape, dt_))
                    abt = {}
                    abt["kvring"] = {"k": absb("ab_k", [128, 4096], BF16),
                                     "v": absb("ab_v", [128, 4096], BF16),
                                     "kf": absb("ab_kf", [128, 1024], BF16),
                                     "g2": absb("ab_g2", [128, 1024], BF16)}
                    abt["cols"] = {"g": absb("ab_gc", [128, NH * 16], F32),
                                   "dt": absb("ab_dc", [128, NH * 16], F32),
                                   "ed": absb("ab_ec", [128, NH * 16], F32),
                                   "cd": absb("ab_cc", [128, NH * 16], F32)}
                    abt["ctm2"] = absb("ab_ctm", [128, 128], F32)
                    abt["stm2"] = absb("ab_stm", [128, 128], F32)
                    abt["cosq"] = gsl_sb
                    abt["sinq"] = lnsq
                    abt["dtg"] = {"a": sil[0:NH, 0:512],
                                  "b": sil[0:NH, 512:1024],
                                  "ones": absb("ab_dt1", [NH, 128], F32)}
                    silb = sil[:].bitcast(BF16)
                    abt["yn"] = [sil[:, 0:128], sil[:, 128:256]]
                    abt["gnwb"] = sil[:, 256:384]
                    abt["colb"] = [sil[:, 384:512], sil[:, 512:640]]
                    abt["sdd1"] = silb[:, 1280:1408]
                    abt["kdd1"] = silb[:, 1408:1536]
                    abt["qex1"] = silb[:, 1536:1664]
                    cinit = Chain(X, consts_v)
                    cinit.op("dve", lambda v: v.memset(abt["dtg"]["ones"][:],
                                                       1.0))
                    mark("ABp1")
                    sinit_dep = emit_attnb(abt, h3_dep)
                    p0b_drain("wq")
                    p0b_drain("wg")
                    p0b_drain("wo")
                    if upto == "p4a":
                        fin_src = resid_b
                    else:
                        mark("ABgrp")
                        ab_dep = emit_attnb_groups(abt, sinit_dep, h3_dep)
                        if upto == "p4":
                            fin_src = resid_a
                if fin_src is None:
                    mark("h4")
                    h4_dep = emit_ln_sweep(resid_a, h_scr_b, ab_dep,
                                           tapx="x3" if "x3" in tap_t else None)
                    mark("FFN2")
                    for sl_i in range(nS):
                        hfm_dep = emit_hfm_load(h_scr_b, sl_i * 512, h4_dep,
                                                free_dep=None if sl_i == 0 else
                                                ("mpe", X.val("mpe")))
                        g1 = emit_gemm1(hfm_dep)
                        emit_outproj("w3", 512, sl_i * 512, g1,
                                     resid_a, Y_out, True)

        mark("tail")
        p0b_pump(len(p0b_jobs))

        @block.gpsimd
        def _(g):
            if fin_src is not None:
                g.wait_ge(X.handle("qst"), X.val("qst"))
                g.wait_ge(X.handle("mdv"), X.val("mdv"))
                g.wait_ge(X.handle("mac"), X.val("mac"))
                for i in range(nT):
                    g.wait_ge(X.handle("qg"), X.val("qg"))
                    X.inc("qg", g.dma_start(
                        out=Y_out[i * 128:(i + 1) * 128, :],
                        in_=fin_src[i * 128:(i + 1) * 128, :]), 16)
            g.wait_ge(X.handle("qg"), X.val("qg"))
            g.wait_ge(X.handle("qst"), X.val("qst"))
            g.wait_ge(X.handle("qtp"), X.val("qtp"))

        @block.sync
        def _(sy):
            sy.wait_ge(X.handle("qs"), X.val("qs"))
    return nc



def _host_consts(T, half, rng):
    c = {}
    c["c_ones"] = np.ones((1, 128), np.float32)
    c["c_onescol"] = np.ones((128, 1), np.float32)
    c["c_id32"] = np.eye(128, dtype=np.float32)
    c["c_rev"] = np.eye(128, dtype=np.float32)[::-1].copy()
    i_idx = np.arange(128)
    c["c_maskneg"] = np.where(i_idx[None, :] >= i_idx[:, None], 0.0,
                              -1e30).astype(np.float32)
    prot = np.zeros((128, 128), np.float32)
    for dp in range(64):
        prot[dp, dp + 64] = -1.0
    for dp in range(64, 128):
        prot[dp, dp - 64] = 1.0
    c["c_prot"] = prot.T.copy()
    k = np.arange(4096).reshape(64, 64)
    c["c_omr"] = np.cos(2 * np.pi * k / 4096).astype(np.float32)
    c["c_omi"] = np.sin(2 * np.pi * k / 4096).astype(np.float32)
    kn = np.arange(64)
    F = np.exp(2j * np.pi * kn[:, None] * kn[None, :] / 64)
    c["c_fr"] = F.real.astype(np.float32)
    c["c_fi"] = F.imag.astype(np.float32)
    c["c_fni"] = (-F.imag).astype(np.float32)
    TW = np.exp(2j * np.pi * kn[:, None] * kn[None, :] / 4096)
    c["c_twr"] = TW.real.astype(np.float32)
    c["c_twi"] = TW.imag.astype(np.float32)
    pos0 = half * T
    invf = (10000.0 ** (-np.arange(0, 128, 2) / 128.0))
    pos = pos0 + np.arange(T)
    ang = pos[:, None] * invf[None, :]
    cos_t = np.concatenate([np.cos(ang), np.cos(ang)], 1).astype(np.float32)
    sin_t = np.concatenate([np.sin(ang), np.sin(ang)], 1).astype(np.float32)
    c["c_cosfm"] = cos_t.T.copy()
    c["c_sinfm"] = sin_t.T.copy()
    nT = T // 128
    ctm = np.zeros((128, T), np.float32)
    stm = np.zeros((128, T), np.float32)
    for gi in range(nT):
        ctm[:, gi * 128:(gi + 1) * 128] = cos_t[gi * 128:(gi + 1) * 128, :]
        stm[:, gi * 128:(gi + 1) * 128] = sin_t[gi * 128:(gi + 1) * 128, :]
    c["c_costm"] = ctm
    c["c_sintm"] = stm
    c["c_mask"] = np.full((128, 1), 1.0 if half == 1 else 0.0, np.float32)
    cb = np.zeros((128, 4), np.float32)
    cb[:, 0] = EPS_LN
    cb[:, 1] = 1.0
    cb[:, 2] = EPS_RMS
    c["c_bias"] = cb
    magic = int(rng.integers(1, 2 ** 31 - 1))
    c["c_magic"] = np.array([[magic, half, 0, 0]], np.int32)
    return c


def build_inmaps(inputs, T):
    rng = np.random.default_rng(abs(int(
        np.float64(np.asarray(inputs["L0_attn_a_D"],
                              np.float32).reshape(-1)[0]) * 1e9)) % (2 ** 31)
        + 7919)
    u = np.ascontiguousarray(np.asarray(inputs["L0_attn_a_u"], np.float32))
    maps = []
    for core in range(8):
        b, half = core // 2, core % 2
        m = {"x": np.ascontiguousarray(u[b, half * T:(half + 1) * T, :])}
        m["a_lam"] = np.ascontiguousarray(inputs["L0_attn_a_Lambda_ri"],
                                          np.float32)
        m["a_p"] = np.ascontiguousarray(inputs["L0_attn_a_p_ri"], np.float32)
        m["a_q"] = np.ascontiguousarray(inputs["L0_attn_a_q_ri"], np.float32)
        m["a_vc"] = np.ascontiguousarray(
            inputs["L0_attn_a_Vc_ri"], np.float32).reshape(64, 128)
        m["a_ct"] = np.ascontiguousarray(
            inputs["L0_attn_a_Ct_ri"], np.float32).reshape(64, 2)
        m["a_bp"] = np.ascontiguousarray(inputs["L0_attn_a_Bp"], np.float32)
        m["a_d"] = np.ascontiguousarray(inputs["L0_attn_a_D"],
                                        np.float32).reshape(1, 1)
        m["a_ls"] = np.ascontiguousarray(inputs["L0_attn_a_log_step"],
                                         np.float32).reshape(1, 1)
        m["wq"] = np.ascontiguousarray(inputs["L0_attn_b_Wq"], np.float32)
        m["wk"] = np.ascontiguousarray(inputs["L0_attn_b_Wk"], np.float32)
        m["wv"] = np.ascontiguousarray(inputs["L0_attn_b_Wv"], np.float32)
        m["wo"] = np.ascontiguousarray(inputs["L0_attn_b_Wo"], np.float32)
        m["wg"] = np.ascontiguousarray(inputs["L0_attn_b_Wg"], np.float32)
        m["win"] = np.ascontiguousarray(inputs["L0_attn_b_Win"], np.float32)
        m["dtb"] = np.ascontiguousarray(inputs["L0_attn_b_dt_bias"],
                                        np.float32).reshape(NH, 1)
        m["alb"] = np.ascontiguousarray(inputs["L0_attn_b_A_log_bias"],
                                        np.float32).reshape(NH, 1)
        m["gnw"] = np.ascontiguousarray(inputs["L0_attn_b_gnorm_w"],
                                        np.float32).reshape(DH, 1)
        m["w12"] = np.ascontiguousarray(inputs["L0_ffn_w12"], np.float32)
        m["b12"] = np.ascontiguousarray(
            inputs["L0_ffn_b12"], np.float32).reshape(64, 128).T.copy()
        m["w3"] = np.ascontiguousarray(inputs["L0_ffn_w3"], np.float32)
        m["b3"] = np.ascontiguousarray(inputs["L0_ffn_b3"],
                                       np.float32).reshape(1, DM)
        m.update(_host_consts(T, half, rng))
        maps.append(m)
    return maps


def kernel(**inputs):
    T = 2048
    nc = build(T=T, upto="all")
    maps = build_inmaps(inputs, T)
    res = run_bass_kernel_spmd(nc, maps, core_ids=list(range(8))).results
    out = np.empty((4, 2 * T, DM), np.float32)
    for core in range(8):
        b, half = core // 2, core % 2
        out[b, half * T:(half + 1) * T, :] = res[core]["y"]
    return out


# version 10 (final)
# speedup vs baseline: 1.4238x; 1.0260x over previous
import numpy as np
import ml_dtypes
import concourse.bass as bass
import concourse.mybir as mybir
from concourse.bass_utils import run_bass_kernel_spmd
from concourse.ap import AP
from contextlib import ExitStack

F32 = mybir.dt.float32
BF16 = mybir.dt.bfloat16
I32 = mybir.dt.int32
AF = mybir.ActivationFunctionType
ALU = mybir.AluOpType
AX = mybir.AxisListType

DM = 4096
DFF = 8192
NH = 32
NKV = 8
DH = 128
NKC = DM // 128
EPS_LN = 1e-5
EPS_RMS = 1e-5
MAXPOLL = 3_000_000

ENG_ATTR = {"pe": "tensor", "dve": "vector", "act": "scalar",
            "pool": "gpsimd", "sp": "sync"}
PHASE_MARKS = []
ENG_SEM = {"pe": "mpe", "dve": "mdv", "act": "mac", "pool": "mpo", "sp": "msp"}
DMA_SEM = {"pool": "qg", "sp": "qs", "act": "qa"}


class Ctx:
    def __init__(self, nc, block, es):
        self.nc, self.block, self.es = nc, block, es
        self.sems = {}

    def sem(self, name):
        if name not in self.sems:
            self.sems[name] = [self.es.enter_context(self.nc.semaphore(name)), 0]
        return self.sems[name]

    def inc(self, name, instr, n=1):
        s = self.sem(name)
        instr.then_inc(s[0], n)
        s[1] += n
        return s[1]

    def val(self, name):
        return self.sem(name)[1]

    def handle(self, name):
        return self.sem(name)[0]


class Chain:

    def __init__(self, ctx, *deps):
        self.ctx = ctx
        self.prev = [d for d in deps if d is not None]

    def op(self, eng, fn, dma=False, also=()):
        ctx = self.ctx
        prev = self.prev + [a for a in also if a is not None]
        sem = DMA_SEM[eng] if dma else ENG_SEM[eng]
        ser = ctx.val(sem) if dma else None
        dec = getattr(ctx.block, ENG_ATTR[eng])
        out = {}

        def body(e):
            if ser:
                e.wait_ge(ctx.handle(sem), ser)
            for (snm, sval) in prev:
                e.wait_ge(ctx.handle(snm), sval)
            out["v"] = ctx.inc(sem, fn(e), 16 if dma else 1)
        dec(body)
        self.prev = [(sem, out["v"])]
        return self.prev[0]

    def dep(self):
        return self.prev[0]


class Plan:

    def __init__(self, ctx, serial=False):
        self.ctx = ctx
        self.serial = serial
        self.last = None
        self.ops = {e: [] for e in ENG_SEM}
        self.n = {e: ctx.val(ENG_SEM[e]) for e in ENG_SEM}
        self.q = {e: ctx.val(DMA_SEM[e]) for e in DMA_SEM}

    def add(self, eng, fn, deps=()):
        if self.serial:
            deps = tuple(deps) + (self.last,)
        self.n[eng] += 1
        self.ops[eng].append((fn, tuple(d for d in deps if d), False))
        self.last = (ENG_SEM[eng], self.n[eng])
        return self.last

    def dma(self, eng, fn, deps=()):
        if self.serial:
            deps = tuple(deps) + (self.last,)
        self.q[eng] += 16
        self.ops[eng].append((fn, tuple(d for d in deps if d), True))
        self.last = (DMA_SEM[eng], self.q[eng])
        return self.last

    def emit(self):
        ctx = self.ctx
        for eng, ops in self.ops.items():
            if not ops:
                continue
            dec = getattr(ctx.block, ENG_ATTR[eng])

            def body(e, eng=eng, ops=ops):
                seen = {}
                for fn, deps, isdma in ops:
                    if isdma:
                        qs = DMA_SEM[eng]
                        deps = deps + ((qs, ctx.val(qs)),)
                    for snm, sval in deps:
                        if seen.get(snm, -1) >= sval:
                            continue
                        seen[snm] = sval
                        e.wait_ge(ctx.handle(snm), sval)
                    ctx.inc(DMA_SEM[eng] if isdma else ENG_SEM[eng],
                            fn(e), 16 if isdma else 1)
            dec(body)
        for e in ENG_SEM:
            assert self.ctx.val(ENG_SEM[e]) == self.n[e], (e, "plan drift")


def build(T=2048, taps=(), upto="all"):
    nT = T // 128
    nS = T // 512
    nGT = 4

    nc = bass.Bass("TRN2", target_bir_lowering=False)

    ins = {}
    for nm, shp in [("x", [T, DM]),
                    ("a_lam", [64, 2]), ("a_p", [64, 2]), ("a_q", [64, 2]),
                    ("a_vc", [64, 128]), ("a_ct", [64, 2]), ("a_bp", [64, 1]),
                    ("a_d", [1, 1]), ("a_ls", [1, 1]),
                    ("wq", [DM, DM]), ("wk", [1024, DM]), ("wv", [1024, DM]),
                    ("wo", [DM, DM]), ("wg", [DM, DM]), ("win", [NH, DM]),
                    ("dtb", [NH, 1]), ("alb", [NH, 1]), ("gnw", [DH, 1]),
                    ("w12", [DFF, DM]), ("b12", [128, 64]), ("w3", [DM, DM]),
                    ("b3", [1, DM]),
                    ("c_ones", [1, 128]), ("c_onescol", [128, 1]),
                    ("c_id32", [128, 128]), ("c_rev", [128, 128]),
                    ("c_maskneg", [128, 128]), ("c_prot", [128, 128]),
                    ("c_omr", [64, 64]), ("c_omi", [64, 64]),
                    ("c_fr", [64, 64]), ("c_fi", [64, 64]), ("c_fni", [64, 64]),
                    ("c_twr", [64, 64]), ("c_twi", [64, 64]),
                    ("c_cosfm", [128, T]), ("c_sinfm", [128, T]),
                    ("c_costm", [128, T]), ("c_sintm", [128, T]),
                    ("c_mask", [128, 1]), ("c_bias", [128, 4])]:
        ins[nm] = nc.dram_tensor(nm, shp, F32, kind="ExternalInput").ap()
    ins["c_magic"] = nc.dram_tensor("c_magic", [1, 4], I32, kind="ExternalInput").ap()
    X_in = ins["x"]
    Y_out = nc.dram_tensor("y", [T, DM], F32, kind="ExternalOutput").ap()

    WSPEC = [("w12", DFF), ("w3", DM), ("wq", DM), ("wk", 1024),
             ("wv", 1024), ("wg", DM), ("wo", DM)]
    WM = dict(WSPEC)
    wbf = {nm: nc.dram_tensor(f"{nm}_bf", [M, DM], BF16, kind="Internal").ap()
           for nm, M in WSPEC}
    wtl = {nm: nc.dram_tensor(f"{nm}_t", [DM, M], BF16, kind="Internal").ap()
           for nm, M in WSPEC}
    Zbuf = nc.dram_tensor("zbuf", [1, 4352], BF16, kind="Internal").ap()
    resid_a = nc.dram_tensor("resid_a", [T, DM], F32, kind="Internal").ap()
    resid_b = nc.dram_tensor("resid_b", [T, DM], F32, kind="Internal").ap()
    h_scr_a = nc.dram_tensor("h_scr_a", [T, DM], BF16, kind="Internal").ap()
    h_scr_b = nc.dram_tensor("h_scr_b", [T, DM], BF16, kind="Internal").ap()
    k_scr = nc.dram_tensor("k_scr", [T, 1024], BF16, kind="Internal").ap()
    v_scr = nc.dram_tensor("v_scr", [T, 1024], BF16, kind="Internal").ap()
    gate_scr = nc.dram_tensor("gate_scr", [DM, 512], BF16, kind="Internal").ap()
    g_scr = nc.dram_tensor("g_scr", [NH, T], F32, kind="Internal").ap()
    dt_scr = nc.dram_tensor("dt_scr", [NH, T], F32, kind="Internal").ap()
    grows_d = nc.dram_tensor("k_grows", [4, DM], F32, kind="Internal").ap()
    crows_d = nc.dram_tensor("k_crows", [16, DM], F32, kind="Internal").ap()
    pay = nc.dram_tensor("pay", [2 * 128, NH * 128], F32, kind="Internal",
                         addr_space="Shared").ap()
    flg = nc.dram_tensor("flg", [2, 4], I32, kind="Internal",
                         addr_space="Shared").ap()

    tap_t = {}
    TAPSPEC = {"h1": ([T, DM], BF16), "x1": ([T, DM], F32), "K": ([64, 64], F32),
               "h2": ([T, DM], BF16), "x2": ([T, DM], F32), "h3": ([T, DM], BF16),
               "x3": ([T, DM], F32), "kT": ([T, 1024], BF16),
               "vT": ([T, 1024], BF16), "G": ([NH, T], F32),
               "dt": ([NH, T], F32), "S": ([128, NH * 128], F32),
               "q": ([128, NKC * 512], BF16), "y": ([128, NKC * 512], BF16)}
    for tn in taps:
        shp, dt_ = TAPSPEC[tn]
        tap_t[tn] = nc.dram_tensor(f"tap_{tn}", shp, dt_, kind="ExternalOutput").ap()

    es = ExitStack()
    with es:
        def sbuf(name, shape, dt_):
            return es.enter_context(nc.sbuf_tensor(name, shape, dt_))

        c_ones = sbuf("s_ones", [1, 128], F32)
        c_onescol = sbuf("s_onescol", [128, 1], F32)
        c_id32 = sbuf("s_id32", [128, 128], F32)
        c_idb = sbuf("s_idb", [128, 128], BF16)
        c_revb = sbuf("s_revb", [128, 128], BF16)
        c_protb = sbuf("s_protb", [128, 128], BF16)
        c_maskneg = sbuf("s_maskneg", [128, 128], F32)
        c_mask = sbuf("s_mask", [128, 1], F32)
        cbias = sbuf("s_cbias", [128, 4], F32)
        c_magic = sbuf("s_magic", [1, 4], I32)
        b12_sb = sbuf("s_b12", [128, 64], F32)
        b3bf = sbuf("s_b3bf", [1, DM], BF16)
        c_onesb = sbuf("s_onesb", [1, 128], BF16)
        dtb_sb = sbuf("s_dtb", [NH, 1], F32)
        expa_sb = sbuf("s_expa", [NH, 1], F32)
        gnw_sb = sbuf("s_gnw", [DH, 1], F32)
        winT = sbuf("s_winT", [128, NKC * NH], BF16)
        wpan = sbuf("s_wpan", [128, 16384], BF16)
        hfm = sbuf("s_hfm", [128, NKC * 512], BF16)
        x4 = sbuf("s_x4", [128, NKC * 512], BF16)
        p0t = [sbuf(f"s_p0t{i}", [128, 4096], BF16) for i in range(2)]
        xt = [sbuf(f"s_xt{i}", [128, DM], F32) for i in range(2)]
        ep1 = sbuf("s_ep1", [128, 1024], F32)
        sil = sbuf("s_sil", [128, 1024], F32)
        rsl = [sbuf(f"s_rsl{i}", [128, 512], F32) for i in range(2)]
        hbt = sbuf("s_hbt", [128, 512], BF16)
        stat = sbuf("s_stat", [128, 16], F32)
        ytile = sbuf("s_ytile", [128, 128], F32)
        dec_sb = sbuf("s_dec", [128, 128], F32)
        eg_sb = sbuf("s_eg", [128, 128], F32)
        sdd_sb = sbuf("s_sdd", [128, 128], BF16)
        kdd_sb = sbuf("s_kdd", [128, 128], BF16)
        qex_sb = sbuf("s_qex", [128, 128], BF16)
        ysq_sb = sbuf("s_ysq", [128, 128], F32)
        gsl_sb = sbuf("s_gsl", [128, 512], F32)
        lnsq = sbuf("s_lnsq", [128, 512], F32)
        grow = sbuf("s_grow", [1, 128], F32)
        g16 = sbuf("s_g16", [16, 128], F32)
        small = sbuf("s_small", [128, 8], F32)
        sflag = sbuf("s_sflag", [1, 4], I32)
        S_all = xt[0]
        Sb_host = xt[1]

        def Sb(h):
            return Sb_host[:].bitcast(BF16)[:, h * 128:(h + 1) * 128]

        ps = [es.enter_context(nc.psum_tensor(f"ps{i}", [128, 512], F32))
              for i in range(6)]
        pbf = [es.enter_context(nc.psum_tensor(f"pbf{i}", [128, 1024], BF16))
               for i in range(2)]

        block = es.enter_context(nc.Block())
        X = Ctx(nc, block, es)
        for s_ in list(DMA_SEM.values()) + list(ENG_SEM.values()):
            X.sem(s_)
        for i in range(2):
            X.sem(f"qx{i}")
        for i in range(4):
            X.sem(f"qw{i}")
        for i in range(2):
            X.sem(f"qr{i}")
        X.sem("qhf")
        X.sem("qst")
        X.sem("qtp")
        X.sem("qcast")
        X.sem("qpl")
        for i in range(2):
            X.sem(f"qpt{i}")
            X.sem(f"qpw{i}")
        X.sem("qb3")
        X.sem("qrt")

        X.sem("qc")

        @block.sync
        def _(sy):
            for dst, key in [(c_ones, "c_ones"), (c_onescol, "c_onescol"),
                             (c_id32, "c_id32"), (c_maskneg, "c_maskneg"),
                             (c_mask, "c_mask"), (cbias, "c_bias"),
                             (b12_sb, "b12"),
                             (dtb_sb, "dtb"), (expa_sb, "alb"), (gnw_sb, "gnw")]:
                X.inc("qc", sy.dma_start(out=dst[:], in_=ins[key]), 16)
            X.inc("qc", sy.dma_start(out=c_magic[:], in_=ins["c_magic"]), 16)
        consts_v = ("qc", X.val("qc"))

        cast_done = {}

        @block.gpsimd
        def _(g):
            for nm, M in WSPEC:
                g.wait_ge(X.handle("qcast"), X.val("qcast"))
                cast_done[nm] = ("qcast",
                                 X.inc("qcast", g.dma_start(out=wbf[nm],
                                                            in_=ins[nm]), 16))

        ch = Chain(X, consts_v)
        ch.op("dve", lambda v: v.tensor_copy(c_idb[:], c_id32[:]))
        idb_v = ch.dep()
        ch.op("sp", lambda sy: sy.dma_start(out=xt[0][:, 0:128], in_=ins["c_rev"]),
              dma=True)
        ch.op("dve", lambda v: v.tensor_copy(c_revb[:], xt[0][:, 0:128]))
        ch.op("sp", lambda sy: sy.dma_start(out=xt[0][:, 0:128], in_=ins["c_prot"]),
              dma=True)
        ch.op("dve", lambda v: v.tensor_copy(c_protb[:], xt[0][:, 0:128]))
        ch.op("act", lambda s: s.activation(expa_sb[:], expa_sb[:], AF.Exp))
        ch.op("sp", lambda sy: sy.dma_start(out=xt[0][0:1, :], in_=ins["b3"]),
              dma=True)
        ch.op("dve", lambda v: v.tensor_copy(b3bf[:], xt[0][0:1, :]))
        ch.op("dve", lambda v: v.tensor_copy(c_onesb[:], c_ones[:]))
        ch.op("sp", lambda sy: sy.dma_start(out=xt[0][0:NH, :], in_=ins["win"]),
              dma=True)
        ch.op("act", lambda s: s.copy(x4[0:NH, 0:DM], xt[0][0:NH, :]))
        for kc in range(NKC):
            ch.op("pe", lambda t, kc=kc: t.transpose(
                pbf[kc % 2][0:128, 0:NH], x4[0:NH, kc * 128:(kc + 1) * 128],
                c_idb[0:NH, 0:NH]), also=[idb_v])
            ch.op("dve", lambda v, kc=kc: v.tensor_copy(
                winT[:, kc * NH:(kc + 1) * NH], pbf[kc % 2][0:128, 0:NH]))
        prep_v = ch.dep()

        wtl_done = {}
        p0t_last = [0, 0]
        p0b_jobs = []
        p0b_idx = [0]
        p0b_ring = [0]
        p0b_final = {}

        def p0b_enqueue(nm):
            M = WM[nm]
            for kc in range(NKC):
                if M > 4096:
                    p0b_jobs.append((nm, kc, 0, 4096))
                    p0b_jobs.append((nm, kc, 4096, M - 4096))
                else:
                    p0b_jobs.append((nm, kc, 0, M))

        def p0b_mark_finals():
            for nm, M in WSPEC:
                last = [j for j in p0b_jobs if j[0] == nm][-1]
                p0b_final[nm] = last

        def p0b_pump(njobs):
            end = min(p0b_idx[0] + njobs, len(p0b_jobs))
            if end <= p0b_idx[0]:
                return
            jobs = p0b_jobs[p0b_idx[0]:end]
            p0b_idx[0] = end

            @block.sync
            def _(sy):
                for job in jobs:
                    nm, kc, m0, mw = job
                    ring = p0b_ring[0]
                    p0b_ring[0] ^= 1
                    sy.wait_ge(X.handle(cast_done[nm][0]), cast_done[nm][1])
                    if p0t_last[ring]:
                        sy.wait_ge(X.handle(f"qpw{ring}"), p0t_last[ring])
                    tv = X.inc(f"qpt{ring}", sy.dma_start(
                        out=p0t[ring][:, 0:mw],
                        in_=wbf[nm][m0:m0 + mw, kc * 128:(kc + 1) * 128],
                        transpose=True), 16)
                    sy.wait_ge(X.handle(f"qpt{ring}"), tv)
                    p0t_last[ring] = X.inc(f"qpw{ring}", sy.dma_start(
                        out=wtl[nm][kc * 128:(kc + 1) * 128, m0:m0 + mw],
                        in_=p0t[ring][:, 0:mw]), 16)
                    if job == p0b_final.get(nm):
                        wtl_done[nm] = tuple(
                            ("qpw%d" % r, X.val("qpw%d" % r)) for r in (0, 1)
                            if X.val("qpw%d" % r) > 0)

        def p0b_drain(nm):
            while nm not in wtl_done:
                p0b_pump(8)

        def load_panel_half(sy, nm, m0, hq, free_dep):
            if free_dep is None:
                free_dep = ("mpe", X.val("mpe"))
            sy.wait_ge(X.handle(free_dep[0]), free_dep[1])
            for dsem, dval in wtl_done[nm]:
                sy.wait_ge(X.handle(dsem), dval)
            M = WM[nm]
            sem = f"qw{hq}"
            for cc in range(4):
                src = AP(wtl[nm].tensor, (cc * 8 * 128) * M + m0,
                         [[M, 128], [128 * M, 8], [1, 256]])
                dst = AP(wpan[:].tensor, hq * 8192 + cc * 8 * 256,
                         [[16384, 128], [256, 8], [1, 256]])
                X.inc(sem, sy.dma_start(out=dst, in_=src), 16)
            return (sem, X.val(sem))

        def wh_slice(hq, k, off=0, w=256):
            return wpan[:, hq * 8192 + k * 256 + off:hq * 8192 + k * 256 + off + w]

        def load_panel_quarter(sy, nm, m0, q, free_dep):
            if free_dep is None:
                free_dep = ("mpe", X.val("mpe"))
            sy.wait_ge(X.handle(free_dep[0]), free_dep[1])
            for dsem, dval in wtl_done[nm]:
                sy.wait_ge(X.handle(dsem), dval)
            M = WM[nm]
            sem = f"qw{q}"
            for cc in range(2):
                src = AP(wtl[nm].tensor, (cc * 16 * 128) * M + m0,
                         [[M, 128], [128 * M, 16], [1, 128]])
                dst = AP(wpan[:].tensor, q * 4096 + cc * 16 * 128,
                         [[16384, 128], [128, 16], [1, 128]])
                X.inc(sem, sy.dma_start(out=dst, in_=src), 16)
            return (sem, X.val(sem))

        def wqt_slice(q, k):
            return wpan[:, q * 4096 + k * 128:q * 4096 + (k + 1) * 128]

        def emit_hfm_load(h_src, row0, dep, free_dep=None):
            @block.sync
            def _(sy):
                for kc in range(NKC):
                    if dep is not None:
                        sy.wait_ge(X.handle(dep[0]), dep[1])
                    if kc == 0 and free_dep is not None:
                        sy.wait_ge(X.handle(free_dep[0]), free_dep[1])
                    X.inc("qhf", sy.dma_start(
                        out=hfm[:, kc * 512:(kc + 1) * 512],
                        in_=h_src[row0:row0 + 512, kc * 128:(kc + 1) * 128],
                        transpose=True), 16)
            return ("qhf", X.val("qhf"))

        def emit_ln_sweep(x_src, h_dst, dep, tapx=None, taph=None):
            prev_store = None
            xt_cons = []
            for i in range(nT):
                ring = i % 2

                @block.sync
                def _(sy, i=i, ring=ring):
                    sy.wait_ge(X.handle(dep[0]), dep[1])
                    if i >= 2:
                        sy.wait_ge(X.handle("mac"), xt_cons[i - 2])
                    X.inc(f"qx{ring}", sy.dma_start(
                        out=xt[ring][:], in_=x_src[i * 128:(i + 1) * 128, :]), 16)
                xdep = (f"qx{ring}", X.val(f"qx{ring}"))
                c = Chain(X, xdep)
                s1d = c.op("dve", lambda v, ring=ring: v.reduce_sum(
                    stat[:, 0:1], xt[ring][:], axis=AX.X))
                c2 = Chain(X, xdep)
                for sq in range(8):
                    c2.op("act", lambda s, ring=ring, sq=sq: s.activation(
                        lnsq[:], xt[ring][:, sq * 512:(sq + 1) * 512],
                        AF.Square, accum_out=stat[:, 8 + sq:9 + sq]))
                c2.op("dve", lambda v: v.reduce_sum(stat[:, 1:2], stat[:, 8:16],
                                                    axis=AX.X), also=[s1d])
                c2.op("dve", lambda v: v.tensor_scalar(stat[:, 2:3], stat[:, 0:1],
                                                       1.0 / DM, None, ALU.mult))
                c2.op("dve", lambda v: v.tensor_tensor(stat[:, 3:4], stat[:, 2:3],
                                                       stat[:, 2:3], ALU.mult))
                c2.op("dve", lambda v: v.scalar_tensor_tensor(
                    stat[:, 4:5], stat[:, 1:2], 1.0 / DM, stat[:, 3:4],
                    ALU.mult, ALU.subtract))
                c2.op("act", lambda s: s.activation(stat[:, 5:6], stat[:, 4:5],
                                                    AF.Ln, bias=cbias[:, 0:1]))
                c2.op("act", lambda s: s.activation(stat[:, 6:7], stat[:, 5:6],
                                                    AF.Exp, scale=-0.5))
                c2.op("dve", lambda v: v.scalar_tensor_tensor(
                    stat[:, 7:8], stat[:, 2:3], -1.0, stat[:, 6:7],
                    ALU.mult, ALU.mult))
                nv = None
                for sq in range(8):
                    nv = c2.op("act", lambda s, ring=ring, sq=sq: s.activation(
                        hbt[:], xt[ring][:, sq * 512:(sq + 1) * 512], AF.Identity,
                        bias=stat[:, 7:8], scale=stat[:, 6:7]),
                        also=[prev_store] if (sq == 0 and prev_store) else ())
                    prev_store = c2.op("pool", lambda g, i=i, sq=sq: g.dma_start(
                        out=h_dst[i * 128:(i + 1) * 128, sq * 512:(sq + 1) * 512],
                        in_=hbt[:]), dma=True)
                xt_cons.append(nv[1])
                if taph is not None:
                    ct = Chain(X, prev_store)
                    prev_store = ct.op("pool", lambda g, i=i: g.dma_start(
                        out=tap_t[taph][i * 128:(i + 1) * 128, :],
                        in_=h_dst[i * 128:(i + 1) * 128, :]), dma=True)
                if tapx is not None:
                    ct = Chain(X, xdep, ("mac", nv[1]))
                    ct.op("pool", lambda g, i=i, ring=ring: g.dma_start(
                        out=tap_t[tapx][i * 128:(i + 1) * 128, :],
                        in_=xt[ring][:]), dma=True)
            return prev_store

        def emit_kcomp(kv):
            ksc, toe, hbig = kv["ksc"], kv["toe"], kv["hbig"]
            c = Chain(X, consts_v, prep_v)
            for dst, key in [(ksc["lam"], "a_lam"), (ksc["p"], "a_p"),
                             (ksc["q"], "a_q"), (ksc["ct"], "a_ct"),
                             (ksc["vc"], "a_vc"), (ksc["bp"], "a_bp"),
                             (ksc["dD"], "a_d"), (ksc["ls"], "a_ls"),
                             (ksc["omr"], "c_omr"), (ksc["omi"], "c_omi"),
                             (ksc["fr"], "c_fr"), (ksc["fi"], "c_fi"),
                             (ksc["fni"], "c_fni"), (ksc["twr"], "c_twr"),
                             (ksc["twi"], "c_twi")]:
                c.op("act", lambda s, dst=dst, key=key: s.dma_start(
                    out=dst[:], in_=ins[key]), dma=True)
            c.op("pe", lambda t: t.transpose(ps[0][0:1, 0:64], ksc["bp"][:],
                                             c_id32[0:64, 0:64]))
            c.op("dve", lambda v: v.tensor_copy(grow[0:1, 0:64], ps[0][0:1, 0:64]))
            c.op("pe", lambda t: t.matmul(ps[1][0:64, 0:64], c_ones[0:1, 0:64],
                                          grow[0:1, 0:64], start=True, stop=True))
            vc_r = AP(ksc["vc"][:].tensor, 0, [[128, 64], [2, 64]])
            vc_i = AP(ksc["vc"][:].tensor, 1, [[128, 64], [2, 64]])
            c.op("dve", lambda v: v.tensor_tensor(ksc["t0"][:], vc_r,
                                                  ps[1][0:64, 0:64], ALU.mult))
            c.op("dve", lambda v: v.tensor_tensor(ksc["t1"][:], vc_i,
                                                  ps[1][0:64, 0:64], ALU.mult))
            c.op("dve", lambda v: v.reduce_sum(ksc["Bv"][:, 0:1], ksc["t0"][:],
                                               axis=AX.X))
            c.op("dve", lambda v: v.reduce_sum(ksc["Bv"][:, 1:2], ksc["t1"][:],
                                               axis=AX.X))

            def cconj(dst_r, dst_i, a, b):
                ar, ai = a[:, 0:1], a[:, 1:2]
                br, bi = b[:, 0:1], b[:, 1:2]
                c.op("dve", lambda v: v.tensor_tensor(ksc["t0"][:, 0:1], ar, br,
                                                      ALU.mult))
                c.op("dve", lambda v: v.tensor_tensor(ksc["t0"][:, 1:2], ai, bi,
                                                      ALU.mult))
                c.op("dve", lambda v: v.tensor_tensor(ksc["t1"][:, 0:1], ar, bi,
                                                      ALU.mult))
                c.op("dve", lambda v: v.tensor_tensor(ksc["t1"][:, 1:2], ai, br,
                                                      ALU.mult))
                c.op("dve", lambda v: v.tensor_add(dst_r, ksc["t0"][:, 0:1],
                                                   ksc["t0"][:, 1:2]))
                c.op("dve", lambda v: v.tensor_sub(dst_i, ksc["t1"][:, 0:1],
                                                   ksc["t1"][:, 1:2]))
            V = ksc["V"]
            cconj(V[:, 0:1], V[:, 4:5], ksc["ct"], ksc["Bv"])
            cconj(V[:, 1:2], V[:, 5:6], ksc["ct"], ksc["p"])
            cconj(V[:, 2:3], V[:, 6:7], ksc["q"], ksc["Bv"])
            cconj(V[:, 3:4], V[:, 7:8], ksc["q"], ksc["p"])
            c.op("act", lambda s: s.activation(ksc["sc"][0:1, 0:1], ksc["ls"][:],
                                               AF.Exp))
            c.op("dve", lambda v: v.reciprocal(ksc["sc"][0:1, 1:2],
                                               ksc["sc"][0:1, 0:1]))
            c.op("dve", lambda v: v.tensor_scalar(ksc["sc"][0:1, 2:3],
                                                  ksc["sc"][0:1, 1:2], 2.0, None,
                                                  ALU.mult))
            c.op("pe", lambda t: t.matmul(ps[0][0:64, 0:1], c_ones[0:1, 0:64],
                                          ksc["sc"][0:1, 2:3], start=True,
                                          stop=True))
            c.op("dve", lambda v: v.tensor_copy(ksc["rs"][:, 0:1], ps[0][0:64, 0:1]))
            c.op("dve", lambda v: v.tensor_scalar(ksc["t0"][:], ksc["omr"][:],
                                                  1.0, None, ALU.add))
            c.op("dve", lambda v: v.tensor_tensor(ksc["t1"][:], ksc["t0"][:],
                                                  ksc["t0"][:], ALU.mult))
            c.op("dve", lambda v: v.tensor_tensor(ksc["t2"][:], ksc["omi"][:],
                                                  ksc["omi"][:], ALU.mult))
            c.op("dve", lambda v: v.tensor_add(ksc["t3"][:], ksc["t1"][:],
                                               ksc["t2"][:]))
            c.op("dve", lambda v: v.reciprocal(ksc["t3"][:], ksc["t3"][:]))
            c.op("dve", lambda v: v.tensor_scalar(ksc["t1"][:], ksc["omr"][:],
                                                  -1.0, 1.0, ALU.mult, ALU.add))
            c.op("dve", lambda v: v.tensor_tensor(ksc["t1"][:], ksc["t1"][:],
                                                  ksc["t0"][:], ALU.mult))
            c.op("dve", lambda v: v.tensor_sub(ksc["t1"][:], ksc["t1"][:],
                                               ksc["t2"][:]))
            c.op("dve", lambda v: v.tensor_tensor(ksc["gr"][:], ksc["t1"][:],
                                                  ksc["t3"][:], ALU.mult))
            c.op("dve", lambda v: v.tensor_scalar_mul(ksc["gr"][:], ksc["gr"][:],
                                                      ksc["rs"][:, 0:1]))
            c.op("dve", lambda v: v.tensor_tensor(ksc["gi"][:], ksc["omi"][:],
                                                  ksc["t3"][:], ALU.mult))
            c.op("dve", lambda v: v.tensor_scalar(ksc["gi"][:], ksc["gi"][:],
                                                  2.0, None, ALU.mult))
            c.op("dve", lambda v: v.tensor_scalar_mul(ksc["gi"][:], ksc["gi"][:],
                                                      ksc["rs"][:, 0:1]))
            c.op("dve", lambda v: v.tensor_tensor(ksc["cr"][:], ksc["t0"][:],
                                                  ksc["t3"][:], ALU.mult))
            c.op("dve", lambda v: v.tensor_tensor(ksc["ci"][:], ksc["omi"][:],
                                                  ksc["t3"][:], ALU.mult))
            c.op("dve", lambda v: v.tensor_scalar(ksc["cr"][:], ksc["cr"][:],
                                                  2.0, None, ALU.mult))
            c.op("dve", lambda v: v.tensor_scalar(ksc["ci"][:], ksc["ci"][:],
                                                  2.0, None, ALU.mult))
            c.op("pool", lambda g: g.dma_start(
                out=grows_d[0:1, :].rearrange("o (p f) -> (o p) f", p=64),
                in_=ksc["gr"][:]), dma=True)
            c.op("pool", lambda g: g.dma_start(
                out=grows_d[1:2, :].rearrange("o (p f) -> (o p) f", p=64),
                in_=ksc["gi"][:]), dma=True)
            sc32 = 2.0 ** -32
            for i in range(DM // 512):
                c.op("act", lambda s, i=i: s.dma_start(
                    out=ksc["gr8"],
                    in_=grows_d[0:1, i * 512:(i + 1) * 512]), dma=True)
                c.op("act", lambda s, i=i: s.dma_start(
                    out=ksc["gi8"],
                    in_=grows_d[1:2, i * 512:(i + 1) * 512]), dma=True)
                c.op("pe", lambda t: t.matmul(ps[2][0:64, :], c_ones[0:1, 0:64],
                                              ksc["gr8"], start=True,
                                              stop=True))
                c.op("pe", lambda t: t.matmul(ps[3][0:64, :], c_ones[0:1, 0:64],
                                              ksc["gi8"], start=True,
                                              stop=True))
                c.op("dve", lambda v: v.tensor_scalar(
                    ksc["ir"][:], ps[2][0:64, :], ksc["lam"][:, 0:1], sc32,
                    ALU.subtract, ALU.mult))
                c.op("dve", lambda v: v.tensor_scalar(
                    ksc["ii"][:], ps[3][0:64, :], ksc["lam"][:, 1:2], sc32,
                    ALU.subtract, ALU.mult))
                d0 = ksc["d0"][0:64, 0:512]
                d1 = ksc["d1"][0:64, 512:1024]
                c.op("dve", lambda v, d0=d0: v.tensor_tensor(d0, ksc["ir"][:],
                                                             ksc["ir"][:], ALU.mult))
                c.op("dve", lambda v, d1=d1: v.tensor_tensor(d1, ksc["ii"][:],
                                                             ksc["ii"][:], ALU.mult))
                c.op("dve", lambda v, d0=d0, d1=d1: v.tensor_add(d0, d0, d1))
                c.op("dve", lambda v, d0=d0: v.reciprocal(d0, d0))
                c.op("dve", lambda v, d0=d0: v.tensor_tensor(ksc["ir"][:],
                                                             ksc["ir"][:], d0,
                                                             ALU.mult))
                c.op("dve", lambda v, d0=d0: v.tensor_tensor(ksc["ii"][:],
                                                             ksc["ii"][:], d0,
                                                             ALU.mult))
                c.op("dve", lambda v: v.tensor_scalar(ksc["ir"][:], ksc["ir"][:],
                                                      sc32, None, ALU.mult))
                c.op("dve", lambda v: v.tensor_scalar(ksc["ii"][:], ksc["ii"][:],
                                                      -sc32, None, ALU.mult))
                c.op("pe", lambda t: t.matmul(ps[4][0:8, :], ksc["V"][:],
                                              ksc["ir"][:], start=True, stop=True))
                c.op("dve", lambda v: v.tensor_copy(ksc["cs"][:], ps[4][0:8, :]))
                c.op("pool", lambda g, i=i: g.dma_start(
                    out=crows_d[0:8, i * 512:(i + 1) * 512], in_=ksc["cs"][:]),
                    dma=True)
                c.op("pe", lambda t: t.matmul(ps[4][0:8, :], ksc["V"][:],
                                              ksc["ii"][:], start=True, stop=True))
                c.op("dve", lambda v: v.tensor_copy(ksc["cs"][:], ps[4][0:8, :]))
                c.op("pool", lambda g, i=i: g.dma_start(
                    out=crows_d[8:16, i * 512:(i + 1) * 512], in_=ksc["cs"][:]),
                    dma=True)
            for j in range(8):
                c.op("act", lambda s, j=j: s.dma_start(
                    out=ksc["kk"][0:64, j * 64:(j + 1) * 64],
                    in_=crows_d[j:j + 1, :].rearrange("o (p f) -> (o p) f", p=64)),
                    dma=True)
                c.op("act", lambda s, j=j: s.dma_start(
                    out=ksc["kb"][0:64, 512 + j * 64:512 + (j + 1) * 64],
                    in_=crows_d[8 + j:9 + j, :].rearrange("o (p f) -> (o p) f",
                                                          p=64)), dma=True)
            kk = ksc["kk"][0:64, 0:512]
            kb = ksc["kb"][0:64, 512:1024]
            dst = ksc["kcplx"][0:64, 0:512]
            for j in range(4):
                c.op("dve", lambda v, j=j: v.tensor_sub(
                    dst[:, j * 64:(j + 1) * 64], kk[:, j * 64:(j + 1) * 64],
                    kb[:, (4 + j) * 64:(5 + j) * 64]))
                c.op("dve", lambda v, j=j: v.tensor_add(
                    dst[:, (4 + j) * 64:(5 + j) * 64], kb[:, j * 64:(j + 1) * 64],
                    kk[:, (4 + j) * 64:(5 + j) * 64]))
            k00r, k01r, k10r, k11r = (dst[:, j * 64:(j + 1) * 64] for j in range(4))
            k00i, k01i, k10i, k11i = (dst[:, (4 + j) * 64:(5 + j) * 64]
                                      for j in range(4))
            t0, t1, t2, t3, t4, t5 = (ksc[f"t{j}"][:] for j in range(6))
            c.op("dve", lambda v: v.tensor_scalar(t0, k11r, 1.0, None, ALU.add))
            c.op("dve", lambda v: v.tensor_tensor(t1, t0, t0, ALU.mult))
            c.op("dve", lambda v: v.tensor_tensor(t2, k11i, k11i, ALU.mult))
            c.op("dve", lambda v: v.tensor_add(t1, t1, t2))
            c.op("dve", lambda v: v.reciprocal(t1, t1))
            c.op("dve", lambda v: v.tensor_tensor(t2, k01r, k10r, ALU.mult))
            c.op("dve", lambda v: v.tensor_tensor(t4, k01i, k10i, ALU.mult))
            c.op("dve", lambda v: v.tensor_tensor(t3, k01r, k10i, ALU.mult))
            c.op("dve", lambda v: v.tensor_tensor(t5, k01i, k10r, ALU.mult))
            c.op("dve", lambda v: v.tensor_sub(t2, t2, t4))
            c.op("dve", lambda v: v.tensor_add(t3, t3, t5))
            c.op("dve", lambda v: v.tensor_tensor(t4, t2, t0, ALU.mult))
            c.op("dve", lambda v: v.tensor_tensor(t5, t3, k11i, ALU.mult))
            c.op("dve", lambda v: v.tensor_add(t4, t4, t5))
            c.op("dve", lambda v: v.tensor_tensor(t5, t3, t0, ALU.mult))
            c.op("dve", lambda v: v.tensor_tensor(t0, t2, k11i, ALU.mult))
            c.op("dve", lambda v: v.tensor_sub(t5, t5, t0))
            c.op("dve", lambda v: v.tensor_tensor(t4, t4, t1, ALU.mult))
            c.op("dve", lambda v: v.tensor_tensor(t5, t5, t1, ALU.mult))
            c.op("dve", lambda v: v.tensor_sub(t0, k00r, t4))
            c.op("dve", lambda v: v.tensor_sub(t1, k00i, t5))
            c.op("dve", lambda v: v.tensor_tensor(t2, ksc["cr"][:], t0, ALU.mult))
            c.op("dve", lambda v: v.tensor_tensor(t4, ksc["ci"][:], t1, ALU.mult))
            c.op("dve", lambda v: v.tensor_tensor(t3, ksc["cr"][:], t1, ALU.mult))
            c.op("dve", lambda v: v.tensor_tensor(t5, ksc["ci"][:], t0, ALU.mult))
            c.op("dve", lambda v: v.tensor_sub(t2, t2, t4))
            c.op("dve", lambda v: v.tensor_add(t3, t3, t5))

            def s1r_mm(t):
                t.matmul(ps[0][0:64, 0:64], ksc["t2"][:], ksc["fr"][:],
                         start=True, stop=False)
                return t.matmul(ps[0][0:64, 0:64], ksc["t3"][:], ksc["fni"][:],
                                start=False, stop=True)

            def s1i_mm(t):
                t.matmul(ps[1][0:64, 0:64], ksc["t2"][:], ksc["fi"][:],
                         start=True, stop=False)
                return t.matmul(ps[1][0:64, 0:64], ksc["t3"][:], ksc["fr"][:],
                                start=False, stop=True)
            c.op("pe", s1r_mm)
            c.op("pe", s1i_mm)
            c.op("dve", lambda v: v.tensor_copy(ksc["s1r"][:], ps[0][0:64, 0:64]))
            c.op("dve", lambda v: v.tensor_copy(ksc["s1i"][:], ps[1][0:64, 0:64]))
            c.op("dve", lambda v: v.tensor_tensor(ksc["br"][:], ksc["s1r"][:],
                                                  ksc["twr"][:], ALU.mult))
            c.op("dve", lambda v: v.tensor_tensor(ksc["t0"][:], ksc["s1i"][:],
                                                  ksc["twi"][:], ALU.mult))
            c.op("dve", lambda v: v.tensor_tensor(ksc["bi"][:], ksc["s1r"][:],
                                                  ksc["twi"][:], ALU.mult))
            c.op("dve", lambda v: v.tensor_tensor(ksc["t1"][:], ksc["s1i"][:],
                                                  ksc["twr"][:], ALU.mult))
            c.op("dve", lambda v: v.tensor_sub(ksc["br"][:], ksc["br"][:],
                                               ksc["t0"][:]))
            c.op("dve", lambda v: v.tensor_add(ksc["bi"][:], ksc["bi"][:],
                                               ksc["t1"][:]))

            def s2_mm(t):
                t.matmul(ps[2][0:64, 0:64], ksc["fr"][:], ksc["br"][:],
                         start=True, stop=False)
                return t.matmul(ps[2][0:64, 0:64], ksc["fni"][:], ksc["bi"][:],
                                start=False, stop=True)
            c.op("pe", s2_mm)
            c.op("act", lambda s: s.activation(ksc["Ksb"][:], ps[2][0:64, 0:64],
                                               AF.Copy, scale=1.0 / 4096.0))
            c.op("dve", lambda v: v.tensor_add(ksc["Ksb"][0:1, 0:1],
                                               ksc["Ksb"][0:1, 0:1], ksc["dD"][:]))
            if "K" in tap_t:
                c.op("pool", lambda g: g.dma_start(out=tap_t["K"],
                                                   in_=ksc["Ksb"][:]), dma=True)
            c.op("dve", lambda v: v.tensor_copy(ksc["Kbf"][:], ksc["Ksb"][:]))
            c.op("dve", lambda v: v.memset(ksc["z256"][:], 0.0))
            c.op("pool", lambda g: g.dma_start(out=Zbuf[0:1, 0:128],
                                               in_=ksc["z256"][0:1, 0:128]), dma=True)
            c.op("pool", lambda g: g.dma_start(out=Zbuf[0:1, 4224:4352],
                                               in_=ksc["z256"][0:1, 128:256]), dma=True)
            c.op("pool", lambda g: g.dma_start(
                out=Zbuf[0:1, 128:4224].rearrange("o (p f) -> (o p) f", p=64),
                in_=ksc["Kbf"][:]), dma=True)
            c.op("act", lambda s: s.dma_start(
                out=hbig[:], in_=AP(Zbuf.tensor, 0, [[1, 128], [1, 4224]])),
                dma=True)
            c.op("dve", lambda v: v.memset(toe[:, 0:3 * 128], 0.0))
            for tt in range(32):
                c.op("pe", lambda t, tt=tt: t.matmul(
                    ps[tt % 2][0:128, 0:128], c_revb[:],
                    hbig[:, 128 * tt + 1:128 * tt + 129], start=True, stop=True))
                c.op("dve", lambda v, tt=tt: v.tensor_copy(
                    toe[:, (3 + tt) * 128:(4 + tt) * 128],
                    ps[tt % 2][0:128, 0:128]))
            return c.dep()

        def emit_outproj(nm, src_w, row0, src_dep, resid_src, out_dst, with_b3,
                         extra_pump=0):
            npan = DM // 256
            ntiles = nGT
            pe_base = X.val("mpe")
            mdv0 = X.val("mdv")
            qst0 = X.val("qst")
            pan_v = {}
            rs_v = {}
            b3v = {}

            def slice_free(grp):
                return mdv0 + 2 * grp + 1

            def slice_done(grp):
                return mdv0 + 2 * (grp + 1)

            def store_done(grp):
                return qst0 + 16 * (grp + 1)

            @block.sync
            def _(sy):
                for pi in range(npan):
                    free = None
                    if pi >= 2:
                        free = ("mpe", pe_base + (pi - 1) * ntiles)
                    pan_v[pi] = load_panel_half(sy, nm, pi * 256, pi % 2, free)
                    for lt in range(ntiles):
                        k = pi * ntiles + lt
                        ring = k % 2
                        if k >= 2:
                            sy.wait_ge(X.handle("mdv"), slice_done(k - 2))
                        elif mdv0 > 0:
                            sy.wait_ge(X.handle("mdv"), mdv0)
                        gi = row0 // 128 + lt
                        X.inc(f"qr{ring}", sy.dma_start(
                            out=rsl[ring][:, 0:256],
                            in_=resid_src[gi * 128:(gi + 1) * 128,
                                          pi * 256:(pi + 1) * 256]), 16)
                        rs_v[k] = X.val(f"qr{ring}")
            if extra_pump:
                p0b_pump(extra_pump)

            @block.tensor
            def _(t):
                t.wait_ge(X.handle(src_dep[0]), src_dep[1])
                t.wait_ge(X.handle("mdv"), mdv0)
                for pi in range(npan):
                    t.wait_ge(X.handle(pan_v[pi][0]), pan_v[pi][1])
                    for lt in range(ntiles):
                        grp = pi * ntiles + lt
                        bank = grp % 5
                        if grp >= 5:
                            t.wait_ge(X.handle("mdv"), slice_free(grp - 5))
                        mm = None
                        for k in range(NKC):
                            mm = t.matmul(
                                ps[bank][:, 0:256],
                                x4[:, k * src_w + lt * 128:k * src_w + lt * 128 + 128],
                                wh_slice(pi % 2, k),
                                start=(k == 0),
                                stop=(k == NKC - 1 and not with_b3))
                        if with_b3:
                            mm = t.matmul(
                                ps[bank][:, 0:256], c_onesb[0:1, 0:128],
                                b3bf[0:1, pi * 256:(pi + 1) * 256],
                                start=False, stop=True)
                        X.inc("mpe", mm, 1)

            @block.vector
            def _(v):
                for pi in range(npan):
                    for lt in range(ntiles):
                        grp = pi * ntiles + lt
                        bank = grp % 5
                        ring = grp % 2
                        v.wait_ge(X.handle("mpe"), pe_base + grp + 1)
                        v.wait_ge(X.handle(f"qr{ring}"), rs_v[grp])
                        if grp >= 2:
                            v.wait_ge(X.handle("qst"), store_done(grp - 2))
                        elif qst0 > 0:
                            v.wait_ge(X.handle("qst"), qst0)
                        sl = ep1[:, (grp % 2) * 256:(grp % 2) * 256 + 256]
                        X.inc("mdv", v.tensor_add(sl, ps[bank][:, 0:256],
                                                  rsl[ring][:, 0:256]), 1)
                        X.inc("mdv", v.engine_nop(), 1)

            @block.gpsimd
            def _(g):
                for pi in range(npan):
                    for lt in range(ntiles):
                        grp = pi * ntiles + lt
                        gi = row0 // 128 + lt
                        g.wait_ge(X.handle("mdv"), slice_done(grp))
                        g.wait_ge(X.handle("qst"), X.val("qst"))
                        sl = ep1[:, (grp % 2) * 256:(grp % 2) * 256 + 256]
                        X.inc("qst", g.dma_start(
                            out=out_dst[gi * 128:(gi + 1) * 128,
                                        pi * 256:(pi + 1) * 256], in_=sl), 16)
            return ("qst", X.val("qst"))

        def emit_gemm1(hfm_dep):
            npair = DM // 128
            pe_base = X.val("mpe")
            mac0 = X.val("mac")
            mdv0 = X.val("mdv")
            pan_v = {}

            @block.sync
            def _(sy):
                for pi in range(npair):
                    for side in range(2):
                        q = (2 * pi + side) % 4
                        free = None
                        if pi >= 2:
                            free = ("mpe", pe_base + (pi - 1))
                        m0 = pi * 128 + side * (DFF // 2)
                        pan_v[(pi, side)] = load_panel_quarter(sy, "w12", m0, q,
                                                               free)

            @block.tensor
            def _(t):
                t.wait_ge(X.handle(hfm_dep[0]), hfm_dep[1])
                t.wait_ge(X.handle("mac"), mac0)
                t.wait_ge(X.handle("mdv"), mdv0)
                for pi in range(npair):
                    qa = (2 * pi) % 4
                    qb = (2 * pi + 1) % 4
                    t.wait_ge(X.handle(pan_v[(pi, 0)][0]), pan_v[(pi, 0)][1])
                    t.wait_ge(X.handle(pan_v[(pi, 1)][0]), pan_v[(pi, 1)][1])
                    bankA = (2 * pi) % 6
                    bankB = (2 * pi + 1) % 6
                    if pi >= 3:
                        t.wait_ge(X.handle("mac"), mac0 + pi - 2)
                        t.wait_ge(X.handle("mdv"), mdv0 + pi - 2)
                    for k in range(NKC):
                        t.matmul(ps[bankA][:], wqt_slice(qa, k),
                                 hfm[:, k * 512:(k + 1) * 512],
                                 start=(k == 0), stop=(k == NKC - 1))
                    mm = None
                    for k in range(NKC):
                        mm = t.matmul(ps[bankB][:], wqt_slice(qb, k),
                                      hfm[:, k * 512:(k + 1) * 512],
                                      start=(k == 0), stop=(k == NKC - 1))
                    X.inc("mpe", mm, 1)

            @block.scalar
            def _(s):
                for pi in range(npair):
                    bankA = (2 * pi) % 6
                    s.wait_ge(X.handle("mpe"), pe_base + pi + 1)
                    if pi >= 2:
                        s.wait_ge(X.handle("mdv"), mdv0 + pi - 1)
                    X.inc("mac", s.activation(
                        sil[:, (pi % 2) * 512:(pi % 2) * 512 + 512],
                        ps[bankA][:], AF.Silu, bias=b12_sb[:, pi:pi + 1]), 1)

            @block.vector
            def _(v):
                for pi in range(npair):
                    bankB = (2 * pi + 1) % 6
                    v.wait_ge(X.handle("mac"), mac0 + pi + 1)
                    X.inc("mdv", v.scalar_tensor_tensor(
                        x4[:, pi * 512:(pi + 1) * 512], ps[bankB][:],
                        b12_sb[:, 32 + pi:33 + pi],
                        sil[:, (pi % 2) * 512:(pi % 2) * 512 + 512],
                        ALU.add, ALU.mult), 1)
            return ("mdv", X.val("mdv"))

        def emit_p1():
            kv = {}
            with ExitStack() as p1es:
                def p1sb(name, shape, dt_):
                    return p1es.enter_context(nc.sbuf_tensor(name, shape, dt_))
                kv["toe"] = p1sb("p1_toe", [128, 35 * 128], BF16)
                kv["hbig"] = p1sb("p1_hbig", [128, 4224], BF16)
                ksc = {}
                for knm, kshape, kdt in [
                        ("lam", [64, 2], F32), ("p", [64, 2], F32),
                        ("q", [64, 2], F32), ("ct", [64, 2], F32),
                        ("vc", [64, 128], F32), ("bp", [64, 1], F32),
                        ("dD", [1, 1], F32), ("ls", [1, 1], F32),
                        ("omr", [64, 64], F32), ("omi", [64, 64], F32),
                        ("fr", [64, 64], F32), ("fi", [64, 64], F32),
                        ("fni", [64, 64], F32), ("twr", [64, 64], F32),
                        ("twi", [64, 64], F32), ("Bv", [64, 2], F32),
                        ("V", [64, 8], F32), ("sc", [1, 4], F32),
                        ("rs", [64, 1], F32),
                        ("t0", [64, 64], F32), ("t1", [64, 64], F32),
                        ("t2", [64, 64], F32), ("t3", [64, 64], F32),
                        ("t4", [64, 64], F32), ("t5", [64, 64], F32),
                        ("gr", [64, 64], F32), ("gi", [64, 64], F32),
                        ("cr", [64, 64], F32), ("ci", [64, 64], F32),
                        ("ir", [64, 512], F32), ("ii", [64, 512], F32),
                        ("cs", [8, 512], F32),
                        ("s1r", [64, 64], F32), ("s1i", [64, 64], F32),
                        ("br", [64, 64], F32), ("bi", [64, 64], F32),
                        ("Ksb", [64, 64], F32), ("Kbf", [64, 64], BF16),
                        ("z256", [1, 256], BF16)]:
                    ksc[knm] = p1sb("k_" + knm, kshape, kdt)
                ksc["gr8"] = ep1[0:1, 0:512]
                ksc["gi8"] = ep1[0:1, 512:1024]
                ksc["kcplx"] = gsl_sb
                ksc["d0"] = sil
                ksc["d1"] = sil
                ksc["kk"] = ep1
                ksc["kb"] = ep1
                kv["ksc"] = ksc
                toe_dep = emit_kcomp(kv)
                h1_dep = emit_ln_sweep(X_in, h_scr_a, prep_v,
                                       taph="h1" if "h1" in tap_t else None)
                p0b_pump(24)
                prev_pe = None
                for sl_i in range(nS):
                    hfm_dep = emit_hfm_load(h_scr_a, sl_i * 512, h1_dep,
                                            free_dep=prev_pe)
                    pe_base = X.val("mpe")
                    mdv0 = X.val("mdv")
                    qst0 = X.val("qst")
                    rs_v = {}

                    @block.sync
                    def _(sy, sl_i=sl_i, mdv0=mdv0):
                        for lt in range(nGT):
                            gi = sl_i * nGT + lt
                            for pn in range(8):
                                k = lt * 8 + pn
                                ring = k % 2
                                if mdv0 + k - 1 > 0:
                                    sy.wait_ge(X.handle("mdv"), mdv0 + k - 1)
                                X.inc(f"qr{ring}", sy.dma_start(
                                    out=rsl[ring][:],
                                    in_=X_in[gi * 128:(gi + 1) * 128,
                                             pn * 512:(pn + 1) * 512]), 16)
                                rs_v[k] = X.val(f"qr{ring}")

                    @block.tensor
                    def _(t, sl_i=sl_i, hfm_dep=hfm_dep, mdv0=mdv0):
                        t.wait_ge(X.handle(hfm_dep[0]), hfm_dep[1])
                        t.wait_ge(X.handle(toe_dep[0]), toe_dep[1])
                        t.wait_ge(X.handle("mdv"), mdv0)
                        for lt in range(nGT):
                            for pn in range(8):
                                grp = lt * 8 + pn
                                bank = grp % 5
                                if grp >= 5:
                                    t.wait_ge(X.handle("mdv"), mdv0 + grp - 4)
                                L0 = pn * 4
                                jmax = min(L0 + 4, NKC)
                                mm = None
                                for J in range(jmax):
                                    soff = (L0 - J + 3) * 128
                                    mm = t.matmul(
                                        ps[bank][:],
                                        hfm[:, J * 512 + lt * 128:
                                            J * 512 + (lt + 1) * 128],
                                        kv["toe"][:, soff:soff + 512],
                                        start=(J == 0), stop=(J == jmax - 1))
                                X.inc("mpe", mm, 1)

                    @block.vector
                    def _(v, sl_i=sl_i, mdv0=mdv0, pe_base=pe_base, qst0=qst0):
                        for lt in range(nGT):
                            for pn in range(8):
                                grp = lt * 8 + pn
                                bank = grp % 5
                                ring = grp % 2
                                v.wait_ge(X.handle("mpe"), pe_base + grp + 1)
                                v.wait_ge(X.handle(f"qr{ring}"), rs_v[grp])
                                if qst0 + 16 * (grp - 1) > 0:
                                    v.wait_ge(X.handle("qst"),
                                              qst0 + 16 * (grp - 1))
                                sl = ep1[:, (grp % 2) * 512:(grp % 2) * 512 + 512]
                                X.inc("mdv", v.tensor_add(sl, ps[bank][:],
                                                          rsl[ring][:]), 1)

                    @block.gpsimd
                    def _(g, sl_i=sl_i, mdv0=mdv0):
                        for lt in range(nGT):
                            gi = sl_i * nGT + lt
                            for pn in range(8):
                                grp = lt * 8 + pn
                                g.wait_ge(X.handle("mdv"), mdv0 + grp + 1)
                                g.wait_ge(X.handle("qst"), X.val("qst"))
                                sl = ep1[:, (grp % 2) * 512:(grp % 2) * 512 + 512]
                                X.inc("qst", g.dma_start(
                                    out=resid_a[gi * 128:(gi + 1) * 128,
                                                pn * 512:(pn + 1) * 512],
                                    in_=sl), 16)
                    prev_pe = ("mpe", X.val("mpe"))
                    p0b_pump(8)
                x1_dep = ("qst", X.val("qst"))
                if "x1" in tap_t:
                    cx = Chain(X, x1_dep)
                    cx.op("pool", lambda g: g.dma_start(out=tap_t["x1"],
                                                        in_=resid_a), dma=True)
            return x1_dep

        def emit_attnb(abt, h3_dep):
            kvring = abt["kvring"]
            cols = abt["cols"]
            gcols, dtcols, edcols, cdcols = (cols["g"], cols["dt"], cols["ed"],
                                             cols["cd"])
            ctm2, stm2, dtg = abt["ctm2"], abt["stm2"], abt["dtg"]
            prev_pe = None
            dt_deps = []
            kv_spill = None
            for sl_i in range(nS):
                row0 = sl_i * 512
                prev_rope = ("mpo", X.val("mpo")) if sl_i else None
                prev_spill = ("qg", X.val("qg")) if sl_i else None
                hfm_dep = emit_hfm_load(h_scr_a, row0, h3_dep, free_dep=prev_pe)
                pe_base = X.val("mpe")
                mac0 = X.val("mac")
                pan_v = {}
                kq_done = {}

                @block.sync
                def _(sy, sl_i=sl_i, pe_base=pe_base):
                    for j in range(8):
                        nm2 = "wk" if j < 4 else "wv"
                        m0 = (j % 4) * 256
                        slot = j % 2
                        free = None
                        if sl_i == 0:
                            if j >= 2:
                                free = ("mpe", pe_base + (j - 1) * 4)
                        else:
                            free = ("mpe", pe_base + max(0, (j - 1) * 4))
                        pan_v[j] = load_panel_half(sy, nm2, m0, slot, free)

                @block.tensor
                def _(t, sl_i=sl_i, hfm_dep=hfm_dep, mac0=mac0):
                    t.wait_ge(X.handle(hfm_dep[0]), hfm_dep[1])
                    t.wait_ge(X.handle(prep_v[0]), prep_v[1])
                    t.wait_ge(X.handle("mac"), mac0)
                    gidx = 0
                    for j in range(8):
                        slot = j % 2
                        t.wait_ge(X.handle(pan_v[j][0]), pan_v[j][1])
                        for lt in range(nGT):
                            bank = gidx % 6
                            if gidx >= 6:
                                t.wait_ge(X.handle("mac"), mac0 + gidx - 5)
                            mm = None
                            for k in range(NKC):
                                mm = t.matmul(
                                    ps[bank][:, 0:256],
                                    hfm[:, k * 512 + lt * 128:
                                        k * 512 + lt * 128 + 128],
                                    wh_slice(slot, k),
                                    start=(k == 0), stop=(k == NKC - 1))
                            X.inc("mpe", mm, 1)
                            gidx += 1
                    bank = gidx % 6
                    if gidx >= 6:
                        t.wait_ge(X.handle("mac"), mac0 + gidx - 5)
                    mm = None
                    for k in range(NKC):
                        mm = t.matmul(ps[bank][0:NH, :],
                                      winT[:, k * NH:(k + 1) * NH],
                                      hfm[:, k * 512:(k + 1) * 512],
                                      start=(k == 0), stop=(k == NKC - 1))
                    X.inc("mpe", mm, 1)
                dt_bank = 32 % 6

                @block.scalar
                def _(s, sl_i=sl_i, pe_base=pe_base, prev_rope=prev_rope,
                      prev_spill=prev_spill):
                    if prev_rope:
                        s.wait_ge(X.handle(prev_rope[0]), prev_rope[1])
                    if prev_spill:
                        s.wait_ge(X.handle(prev_spill[0]), prev_spill[1])
                    gidx = 0
                    for j in range(8):
                        dstt = "k" if j < 4 else "v"
                        mp = j % 4
                        for lt in range(nGT):
                            bank = gidx % 6
                            s.wait_ge(X.handle("mpe"), pe_base + gidx + 1)
                            X.inc("mac", s.copy(
                                kvring[dstt][:, lt * 1024 + mp * 256:
                                             lt * 1024 + (mp + 1) * 256],
                                ps[bank][:, 0:256]), 1)
                            gidx += 1

                @block.gpsimd
                def _(g, sl_i=sl_i, mac0=mac0):
                    for lt in range(nGT):
                        gi = sl_i * nGT + lt
                        g.wait_ge(X.handle("mac"), mac0 + 3 * 4 + lt + 1)
                        g.wait_ge(X.handle("qg"), X.val("qg"))
                        X.inc("qg", g.dma_start(
                            out=ctm2[:],
                            in_=ins["c_costm"][:, gi * 128:(gi + 1) * 128]), 16)
                        X.inc("qg", g.dma_start(
                            out=stm2[:],
                            in_=ins["c_sintm"][:, gi * 128:(gi + 1) * 128]), 16)
                        g.wait_ge(X.handle("qg"), X.val("qg"))
                        for kvh in range(NKV):
                            base = lt * 1024 + kvh * 128
                            klo = kvring["k"][:, base:base + 64]
                            khi = kvring["k"][:, base + 64:base + 128]
                            X.inc("mpo", g.tensor_tensor(
                                gsl_sb[:, 0:128], kvring["k"][:, base:base + 128],
                                ctm2[:], ALU.mult), 1)
                            X.inc("mpo", g.tensor_tensor(
                                gsl_sb[:, 128:192], khi, stm2[:, 0:64],
                                ALU.mult), 1)
                            X.inc("mpo", g.tensor_tensor(
                                gsl_sb[:, 192:256], klo, stm2[:, 64:128],
                                ALU.mult), 1)
                            g.wait_ge(X.handle("mpo"), X.val("mpo"))
                            X.inc("mpo", g.tensor_sub(
                                klo, gsl_sb[:, 0:64], gsl_sb[:, 128:192]), 1)
                            X.inc("mpo", g.tensor_add(
                                khi, gsl_sb[:, 64:128], gsl_sb[:, 192:256]), 1)
                            g.wait_ge(X.handle("mpo"), X.val("mpo"))
                        g.wait_ge(X.handle("qg"), X.val("qg"))
                        X.inc("qg", g.dma_start(
                            out=k_scr[gi * 128:(gi + 1) * 128, :],
                            in_=kvring["k"][:, lt * 1024:(lt + 1) * 1024]), 16)
                    for lt in range(nGT):
                        gi = sl_i * nGT + lt
                        g.wait_ge(X.handle("mac"), mac0 + 7 * 4 + lt + 1)
                        g.wait_ge(X.handle("qg"), X.val("qg"))
                        X.inc("qg", g.dma_start(
                            out=v_scr[gi * 128:(gi + 1) * 128, :],
                            in_=kvring["v"][:, lt * 1024:(lt + 1) * 1024]), 16)
                kv_spill = ("qg", X.val("qg"))

                cdt = Chain(X, ("mpe", X.val("mpe")),
                            dt_deps[-1] if dt_deps else None)
                cdt.op("act", lambda s: s.activation(
                    dtg["a"], ps[dt_bank][0:NH, :], AF.Abs,
                    bias=dtb_sb[:]))
                cdt.op("act", lambda s: s.activation(
                    dtg["a"], dtg["a"], AF.Exp, scale=-1.0))
                cdt.op("act", lambda s: s.activation(
                    dtg["a"], dtg["a"], AF.Ln,
                    bias=cbias[0:NH, 1:2]))
                cdt.op("act", lambda s: s.activation(
                    dtg["b"], ps[dt_bank][0:NH, :], AF.Relu,
                    bias=dtb_sb[:]))
                cdt.op("dve", lambda v: v.tensor_add(
                    dtg["b"], dtg["b"], dtg["a"]))
                cdt.op("pool", lambda g, row0=row0: g.dma_start(
                    out=dt_scr[:, row0:row0 + 512], in_=dtg["b"]),
                    dma=True)
                cdt.op("dve", lambda v: v.tensor_scalar(
                    dtg["a"], dtg["b"], expa_sb[:], -1.0,
                    ALU.mult, ALU.mult))
                for cc4 in range(4):
                    cdt.op("dve", lambda v, cc4=cc4: v.tensor_tensor_scan(
                        dtg["b"][0:NH, cc4 * 128:(cc4 + 1) * 128],
                        dtg["ones"][:],
                        dtg["a"][0:NH, cc4 * 128:(cc4 + 1) * 128],
                        0.0, ALU.mult, ALU.add))
                cdt.op("pool", lambda g, row0=row0: g.dma_start(
                    out=g_scr[:, row0:row0 + 512], in_=dtg["b"]),
                    dma=True)
                dt_deps.append(cdt.dep())
                prev_pe = ("mpe", X.val("mpe"))
                p0b_pump(10)
            gdt_dep = dt_deps[-1]
            for tn, src in (("G", g_scr), ("dt", dt_scr), ("kT", k_scr),
                            ("vT", v_scr)):
                if tn in tap_t:
                    cg = Chain(X, gdt_dep, kv_spill)
                    cg.op("pool", lambda g, tn=tn, src=src: g.dma_start(
                        out=tap_t[tn], in_=src), dma=True)

            col_dep = None
            for h in range(NH):
                chh = Chain(X, gdt_dep, col_dep)
                chh.op("act", lambda s, h=h: s.dma_start(
                    out=g16[:nT, :], in_=g_scr[h:h + 1, :]
                    .rearrange("o (n j) -> (o n) j", j=128)), dma=True)
                chh.op("pe", lambda t: t.transpose(
                    ps[5][0:128, 0:nT], g16[0:nT, :], c_id32[0:nT, 0:nT]))
                chh.op("dve", lambda v, h=h: v.tensor_copy(
                    gcols[:, h * 16:h * 16 + nT], ps[5][0:128, 0:nT]))
                chh.op("pe", lambda t: t.transpose(
                    ps[5][0:1, 256:256 + nT], g16[0:nT, 127:128],
                    c_id32[0:nT, 0:nT]))
                chh.op("dve", lambda v: v.tensor_copy(
                    grow[0:1, 0:nT], ps[5][0:1, 256:256 + nT]))
                chh.op("act", lambda s: s.activation(
                    grow[0:1, 32:32 + nT], grow[0:1, 0:nT], AF.Exp))
                chh.op("pe", lambda t: t.matmul(
                    ps[5][0:128, 256:256 + nT], c_ones[:], grow[0:1, 32:32 + nT],
                    start=True, stop=True))
                chh.op("dve", lambda v, h=h: v.tensor_copy(
                    cdcols[:, h * 16:h * 16 + nT], ps[5][0:128, 256:256 + nT]))
                chh.op("pe", lambda t: t.matmul(
                    ps[5][0:128, 256:256 + nT], c_ones[:], grow[0:1, 0:nT],
                    start=True, stop=True))
                chh.op("dve", lambda v, h=h: v.tensor_sub(
                    edcols[:, h * 16:h * 16 + nT], ps[5][0:128, 256:256 + nT],
                    gcols[:, h * 16:h * 16 + nT]))
                chh.op("act", lambda s, h=h: s.activation(
                    edcols[:, h * 16:h * 16 + nT],
                    edcols[:, h * 16:h * 16 + nT], AF.Exp))
                chh.op("act", lambda s, h=h: s.dma_start(
                    out=g16[:nT, :], in_=dt_scr[h:h + 1, :]
                    .rearrange("o (n j) -> (o n) j", j=128)), dma=True)
                chh.op("pe", lambda t: t.transpose(
                    ps[5][0:128, 0:nT], g16[0:nT, :], c_id32[0:nT, 0:nT]))
                chh.op("dve", lambda v, h=h: v.tensor_copy(
                    dtcols[:, h * 16:h * 16 + nT], ps[5][0:128, 0:nT]))
                chh.op("dve", lambda v, h=h: v.tensor_tensor(
                    edcols[:, h * 16:h * 16 + nT],
                    edcols[:, h * 16:h * 16 + nT],
                    dtcols[:, h * 16:h * 16 + nT], ALU.mult))
                col_dep = chh.dep()

            cch = Chain(X, col_dep, kv_spill)
            cch.op("dve", lambda v: v.memset(S_all[:], 0.0))
            for n in range(nT):
                ring = n % 2
                cch.op("act", lambda s, n=n, ring=ring: s.dma_start(
                    out=kvring["k"][:, ring * 1024:ring * 1024 + 1024],
                    in_=k_scr[n * 128:(n + 1) * 128, :]), dma=True)
                cch.op("act", lambda s, n=n, ring=ring: s.dma_start(
                    out=kvring["v"][:, ring * 1024:ring * 1024 + 1024],
                    in_=v_scr[n * 128:(n + 1) * 128, :]), dma=True)
                for h in range(NH):
                    kvh = h // 4
                    idx = h * 16 + n
                    ksl = kvring["k"][:, ring * 1024 + kvh * 128:
                                      ring * 1024 + kvh * 128 + 128]
                    vsl = kvring["v"][:, ring * 1024 + kvh * 128:
                                      ring * 1024 + kvh * 128 + 128]
                    cch.op("dve", lambda v, ksl=ksl, idx=idx: v.tensor_scalar_mul(
                        kdd_sb[:], ksl, edcols[:, idx:idx + 1]))
                    cch.op("pe", lambda t, vsl=vsl, h=h: t.matmul(
                        ps[h % 2][0:128, 0:128], kdd_sb[:], vsl,
                        start=True, stop=True))
                    cch.op("dve", lambda v, h=h, idx=idx: v.scalar_tensor_tensor(
                        S_all[:, h * 128:(h + 1) * 128],
                        S_all[:, h * 128:(h + 1) * 128],
                        cdcols[:, idx:idx + 1], ps[h % 2][0:128, 0:128],
                        ALU.mult, ALU.add))
            chain_dep = cch.dep()
            if "S" in tap_t:
                cs2 = Chain(X, chain_dep)
                cs2.op("pool", lambda g: g.dma_start(out=tap_t["S"],
                                                     in_=S_all[:]), dma=True)

            @block.gpsimd
            def _(g):
                g.wait_ge(X.handle(chain_dep[0]), chain_dep[1])
                g.wait_ge(X.handle(consts_v[0]), consts_v[1])
                g.wait_ge(X.handle("qg"), X.val("qg"))
                par = g.value_load(c_magic[0:1, 1:2])
                off = g.compute_val(par * (128 * NH * 128))
                pslot = AP(pay.tensor, off, [[NH * 128, 128], [1, NH * 128]])
                gv = X.inc("qg", g.dma_start(out=pslot, in_=S_all[:]), 16)
                g.wait_ge(X.handle("qg"), gv)
                foff = g.compute_val(par * 4)
                fslot = AP(flg.tensor, foff, [[4, 1], [1, 4]])
                gv = X.inc("qg", g.dma_start(out=fslot, in_=c_magic[:]), 16)
                g.wait_ge(X.handle("qg"), gv)
                f0 = AP(flg.tensor, 0, [[4, 1], [1, 1]])
                with g.register("creg") as creg, g.register("cnt") as cnt, \
                        g.register("itr") as itr:
                    g.reg_mov(cnt, 0)
                    g.reg_mov(itr, 0)

                    def poll():
                        g.reg_add(cnt, cnt, 16)
                        g.reg_add(itr, itr, 1)
                        g.dma_start(out=sflag[0:1, 0:1], in_=f0).then_inc(
                            X.handle("qpl"), 16)
                        g.wait_ge(X.handle("qpl"), cnt)
                        vv = g.value_load(sflag[0:1, 0:1])
                        mg = g.value_load(c_magic[0:1, 0:1])
                        g.reg_alu(creg, vv, mg, ALU.not_equal)
                        with g.If_cmp(itr, MAXPOLL, "IS_GE"):
                            g.reg_mov(creg, 0)
                    poll()
                    with g.While(creg):
                        poll()
                p0 = AP(pay.tensor, 0, [[NH * 128, 128], [1, NH * 128]])
                gv = X.inc("qg", g.dma_start(out=S_all[:], in_=p0), 16)
                g.wait_ge(X.handle("qg"), gv)
                X.inc("mpo", g.tensor_scalar_mul(S_all[:], S_all[:],
                                                 c_mask[:, 0:1]), 1)
                g.wait_ge(X.handle("mpo"), X.val("mpo"))
                X.inc("mpo", g.tensor_copy(Sb_host[:].bitcast(BF16)[:, 0:DM],
                                           S_all[:]), 1)
            return ("mpo", X.val("mpo"))

        def emit_attnb_groups(abt, sinit_dep, h3_dep):
            kvring = abt["kvring"]
            cols = abt["cols"]
            gcols, dtcols, edcols, cdcols = (cols["g"], cols["dt"], cols["ed"],
                                             cols["cd"])
            cosq, sinq = abt["cosq"], abt["sinq"]
            decr_l = [dec_sb, eg_sb]
            egxr_l = [ytile, ysq_sb]
            ynr_l = abt["yn"]
            sddr_l = [sdd_sb, abt["sdd1"]]
            kddr_l = [kdd_sb, abt["kdd1"]]
            qexr_l = [qex_sb, abt["qex1"]]
            colb_l = abt["colb"]
            gnwb = abt["gnwb"]
            nTt = T // 128
            cgn = Chain(X, sinit_dep)
            cgn.op("pe", lambda t: t.transpose(ps[5][0:1, 0:128], gnw_sb[:],
                                               c_id32[:]))
            cgn.op("dve", lambda v: v.tensor_copy(grow[0:1, 0:128],
                                                  ps[5][0:1, 0:128]))
            cgn.op("pe", lambda t: t.matmul(ps[5][0:128, 128:256], c_ones[:],
                                            grow[0:1, 0:128],
                                            start=True, stop=True))
            cgn.op("dve", lambda v: v.tensor_copy(gnwb[:],
                                                  ps[5][0:128, 128:256]))
            cgn.op("dve", lambda v: v.memset(colb_l[0][:], 0.0))
            cgn.op("dve", lambda v: v.memset(colb_l[1][:], 0.0))
            gnwb_dep = cgn.dep()
            sd = {k: {} for k in
                  ("kload", "vload", "kfT", "kfc", "colb", "M1", "M2",
                   "M3a", "M3b", "M4", "Tyn", "decstt", "sdd", "qex", "ynstt",
                   "qslc", "Sstt", "egx", "dexp", "rexp", "sbc", "kdd", "ysq",
                   "M2e", "M4e")}
            out_dep = None
            for grp in range(nS):
                row0 = grp * 512
                hfm_dep = emit_hfm_load(h_scr_a, row0, h3_dep,
                                        free_dep=("mpe", X.val("mpe")))

                mdv_guard = X.val("mdv")
                mac_guard = X.val("mac")
                mpo_guard = X.val("mpo")

                @block.sync
                def _(sy, row0=row0, mdv_guard=mdv_guard, mac_guard=mac_guard,
                      mpo_guard=mpo_guard):
                    sy.wait_ge(X.handle("mdv"), mdv_guard)
                    sy.wait_ge(X.handle("mac"), mac_guard)
                    sy.wait_ge(X.handle("mpo"), mpo_guard)
                    X.inc("qrt", sy.dma_start(
                        out=cosq[:], in_=ins["c_cosfm"][:, row0:row0 + 512]), 16)
                    X.inc("qrt", sy.dma_start(
                        out=sinq[:], in_=ins["c_sinfm"][:, row0:row0 + 512]), 16)
                rope_tab = ("qrt", X.val("qrt"))

                pe_base = X.val("mpe")
                mac0 = X.val("mac")
                qst0g = X.val("qst")
                pan_v = {}

                @block.sync
                def _(sy, grp=grp, pe_base=pe_base):
                    for j in range(2 * (DM // 256)):
                        pi, wi = j // 2, j % 2
                        nm = "wq" if wi == 0 else "wg"
                        hq = j % 2
                        free = None
                        if j >= 2:
                            free = ("mpe", pe_base + (j - 1) * 2)
                        pan_v[j] = load_panel_half(sy, nm, pi * 256, hq, free)

                mdv_proj = X.val("mdv")

                @block.tensor
                def _(t, grp=grp, hfm_dep=hfm_dep, mac0=mac0,
                      mdv_proj=mdv_proj):
                    t.wait_ge(X.handle(hfm_dep[0]), hfm_dep[1])
                    t.wait_ge(X.handle("mac"), mac0)
                    t.wait_ge(X.handle("mdv"), mdv_proj)
                    gidx = 0
                    for j in range(2 * (DM // 256)):
                        pi, wi = j // 2, j % 2
                        hq = j % 2
                        t.wait_ge(X.handle(pan_v[j][0]), pan_v[j][1])
                        for mc2 in range(2):
                            mh = pi * 2 + mc2
                            bank = gidx % 6
                            if gidx >= 6:
                                t.wait_ge(X.handle("mac"), mac0 + gidx - 5)
                            mm = None
                            for k in range(NKC):
                                mm = t.matmul(
                                    ps[bank][:],
                                    wh_slice(hq, k, mc2 * 128, 128),
                                    hfm[:, k * 512:(k + 1) * 512],
                                    start=(k == 0), stop=(k == NKC - 1))
                            X.inc("mpe", mm, 1)
                            gidx += 1

                gate_st = {}

                @block.scalar
                def _(s, grp=grp, pe_base=pe_base):
                    gidx = 0
                    for j in range(2 * (DM // 256)):
                        pi, wi = j // 2, j % 2
                        for mc2 in range(2):
                            mh = pi * 2 + mc2
                            bank = gidx % 6
                            s.wait_ge(X.handle("mpe"), pe_base + gidx + 1)
                            if wi == 0:
                                X.inc("mac", s.copy(
                                    x4[:, mh * 512:(mh + 1) * 512],
                                    ps[bank][:]), 1)
                            else:
                                if gidx >= 4:
                                    s.wait_ge(X.handle("qst"),
                                              gate_st.get(gidx - 4, 0))
                                X.inc("mac", s.copy(
                                    kvring["g2"][:, (gidx % 2) * 512:
                                                 (gidx % 2) * 512 + 512],
                                    ps[bank][:]), 1)
                            gidx += 1

                @block.gpsimd
                def _(g, grp=grp, mac0=mac0):
                    gidx = 0
                    for j in range(2 * (DM // 256)):
                        pi, wi = j // 2, j % 2
                        for mc2 in range(2):
                            mh = pi * 2 + mc2
                            if wi == 1:
                                g.wait_ge(X.handle("mac"), mac0 + gidx + 1)
                                g.wait_ge(X.handle("qst"), X.val("qst"))
                                X.inc("qst", g.dma_start(
                                    out=gate_scr[mh * 128:(mh + 1) * 128, :],
                                    in_=kvring["g2"][:, (gidx % 2) * 512:
                                                     (gidx % 2) * 512 + 512]), 16)
                                gate_st[gidx] = X.val("qst")
                            gidx += 1
                gate_done = ("qst", X.val("qst"))

                pe0r = X.val("mpe")
                mdv0r = X.val("mdv")

                @block.tensor
                def _(t, grp=grp, mac0=mac0, mdv0r=mdv0r):
                    t.wait_ge(X.handle("mac"), mac0 + 64)
                    t.wait_ge(X.handle("mdv"), mdv0r)
                    for mh in range(NH):
                        _gq = (mh // 2) * 4 + (mh % 2)
                        if mh >= 2:
                            t.wait_ge(X.handle("mdv"), mdv0r + 3 * (mh - 1))
                        mm = t.matmul(ps[mh % 2][0:128, 0:512], c_protb[:],
                                      x4[:, mh * 512:(mh + 1) * 512],
                                      start=True, stop=True)
                        X.inc("mpe", mm, 1)

                @block.vector
                def _(v, grp=grp, pe0r=pe0r, qst0g=qst0g):
                    v.wait_ge(X.handle(rope_tab[0]), rope_tab[1])
                    v.wait_ge(X.handle("qst"), qst0g)
                    for mh in range(NH):
                        v.wait_ge(X.handle("mpe"), pe0r + mh + 1)
                        X.inc("mdv", v.tensor_tensor(
                            ep1[:, 0:512], x4[:, mh * 512:(mh + 1) * 512],
                            cosq[:], ALU.mult), 1)
                        X.inc("mdv", v.tensor_tensor(
                            ep1[:, 512:1024], ps[mh % 2][0:128, 0:512],
                            sinq[:], ALU.mult), 1)
                        X.inc("mdv", v.tensor_add(
                            x4[:, mh * 512:(mh + 1) * 512],
                            ep1[:, 0:512], ep1[:, 512:1024]), 1)
                if "q" in tap_t and grp == 0:
                    cq = Chain(X, ("mdv", X.val("mdv")))
                    cq.op("pool", lambda g: g.dma_start(
                        out=tap_t["q"], in_=x4[:, 0:NKC * 512]), dma=True)

                P = Plan(X, serial=bool(globals().get("SCAN_SERIAL")))
                rope_end = ("mdv", X.val("mdv"))
                mac_ent = ("mac", X.val("mac"))
                g0t = grp * nGT * NH

                def eloads(gx2):
                    if gx2 >= nTt:
                        return
                    r2 = gx2 % 2
                    war = [sd["M4e"].get(gx2 - 2),
                           sinit_dep if gx2 < 2 else None]
                    sd["kload"][gx2] = P.dma(
                        "pool", lambda g, gx2=gx2, r2=r2: g.dma_start(
                            out=kvring["k"][:, r2 * 1024:r2 * 1024 + 1024],
                            in_=k_scr[gx2 * 128:(gx2 + 1) * 128, :]), war)
                    sd["vload"][gx2] = P.dma(
                        "pool", lambda g, gx2=gx2, r2=r2: g.dma_start(
                            out=kvring["v"][:, r2 * 1024:r2 * 1024 + 1024],
                            in_=v_scr[gx2 * 128:(gx2 + 1) * 128, :]), war)
                if grp == 0:
                    eloads(0)
                    eloads(1)
                for nl in range(nGT):
                    gx = grp * nGT + nl
                    ring = gx % 2
                    kfb = kvring["kf"] if ring == 0 else kvring["g2"]
                    for kvh in range(NKV):
                        s_ = gx * 8 + kvh
                        poff = ((s_ // 2) % 8) * 128
                        sd["kfT"][(gx, kvh)] = P.add(
                            "pe", lambda t, ring=ring, kvh=kvh, poff=poff:
                            t.transpose(
                                pbf[kvh % 2][0:128, poff:poff + 128],
                                kvring["k"][:, ring * 1024 + kvh * 128:
                                            ring * 1024 + kvh * 128 + 128],
                                c_idb[:]),
                            [sd["kload"][gx],
                             sd["kfc"].get(divmod(s_ - 2, 8) if s_ >= 2
                                           else (-1, 0))])
                        sd["kfc"][(gx, kvh)] = P.add(
                            "act", lambda s, kvh=kvh, poff=poff, kfb=kfb:
                            s.copy(
                                kfb[:, kvh * 128:(kvh + 1) * 128],
                                pbf[kvh % 2][0:128, poff:poff + 128]),
                            [sd["kfT"][(gx, kvh)], sd["M2e"].get(gx - 2),
                             gate_done if ring == 1 else None])
                    for h in range(NH):
                        t_ = gx * NH + h
                        p2 = t_ % 2
                        p4 = t_ % 4
                        kvh = h // 4
                        cidx = h * 16 + gx
                        ent = ([rope_end, mac_ent, gnwb_dep]
                               if t_ - g0t < 4 else [])
                        qsl = x4[:, h * 512 + nl * 128:
                                 h * 512 + nl * 128 + 128]
                        ksl = kvring["k"][:, ring * 1024 + kvh * 128:
                                          ring * 1024 + kvh * 128 + 128]
                        vsl = kvring["v"][:, ring * 1024 + kvh * 128:
                                          ring * 1024 + kvh * 128 + 128]
                        kfs = kfb[:, kvh * 128:(kvh + 1) * 128]
                        ssl = S_all[:, h * 128:(h + 1) * 128]
                        sbl = Sb(h)
                        bA = ps[0] if p2 == 0 else ps[2]
                        bB = ps[1] if p2 == 0 else ps[3]
                        ps0 = bA[:, 0:128]
                        ps2 = bA[:, 128:256]
                        ps3 = bB[:, 0:128]
                        ps1 = bB[:, 128:256]
                        ps5q = (ps[4] if p2 == 0 else ps[5])[:, 0:128]
                        ssc = small[:, p4:p4 + 1]
                        rsc = small[:, 4 + p4:5 + p4]
                        dec = decr_l[p2]
                        egx = egxr_l[p2]
                        sdd = sddr_l[p2]
                        kdd = kddr_l[p2]
                        qex = qexr_l[p2]
                        ynr = ynr_l[p2]
                        colb = colb_l[p2]
                        sd["colb"][t_] = P.add(
                            "dve", lambda v, colb=colb, cidx=cidx:
                            v.tensor_scalar(colb[:], colb[:], 0.0,
                                            gcols[:, cidx:cidx + 1],
                                            ALU.mult, ALU.add),
                            [sd["M1"].get(t_ - 2), gnwb_dep])
                        sd["M1"][t_] = P.add(
                            "pe", lambda t, ps3=ps3, colb=colb: t.transpose(
                                ps3, colb[:], c_id32[:]),
                            [sd["colb"][t_], sd["ynstt"].get(t_ - 2)] + ent)
                        sd["egx"][t_] = P.add(
                            "act", lambda s, egx=egx, ps3=ps3: s.activation(
                                egx[:], ps3, AF.Exp),
                            [sd["M1"][t_], sd["qex"].get(t_ - 2)])
                        sd["decstt"][t_] = P.add(
                            "dve", lambda v, dec=dec, ps3=ps3, cidx=cidx:
                            v.scalar_tensor_tensor(
                                dec[:], ps3, gcols[:, cidx:cidx + 1],
                                c_maskneg[:], ALU.subtract, ALU.add),
                            [sd["egx"][t_]])
                        sd["dexp"][t_] = P.add(
                            "act", lambda s, dec=dec: s.activation(
                                dec[:], dec[:], AF.Exp),
                            [sd["decstt"][t_]])
                        sd["qex"][t_] = P.add(
                            "dve", lambda v, qex=qex, qsl=qsl, egx=egx:
                            v.tensor_tensor(qex[:], qsl, egx[:], ALU.mult),
                            [sd["egx"][t_], sd["M3b"].get(t_ - 2)])
                        sd["M2"][t_] = P.add(
                            "pe", lambda t, ps0=ps0, kfs=kfs, qsl=qsl:
                            t.matmul(ps0, kfs, qsl, start=True, stop=True),
                            [sd["kfc"][(gx, kvh)], sd["Sstt"].get(t_ - 2)]
                            + ent)
                        sd["sdd"][t_] = P.add(
                            "dve", lambda v, sdd=sdd, ps0=ps0, cidx=cidx,
                            dec=dec: v.scalar_tensor_tensor(
                                sdd[:], ps0, dtcols[:, cidx:cidx + 1],
                                dec[:], ALU.mult, ALU.mult),
                            [sd["M2"][t_], sd["dexp"][t_]])
                        sd["M3a"][t_] = P.add(
                            "pe", lambda t, ps1=ps1, sdd=sdd, vsl=vsl:
                            t.matmul(ps1, sdd[:], vsl, start=True, stop=False),
                            [sd["sdd"][t_], sd["vload"][gx], sd["egx"][t_]]
                            + ent)
                        sd["M3b"][t_] = P.add(
                            "pe", lambda t, ps1=ps1, qex=qex, sbl=sbl:
                            t.matmul(ps1, qex[:], sbl, start=False, stop=True),
                            [sd["qex"][t_],
                             sd["sbc"].get(t_ - 32,
                                           sinit_dep if t_ < 32 else None)])
                        sd["kdd"][t_] = P.add(
                            "pool", lambda g, kdd=kdd, ksl=ksl, cidx=cidx:
                            g.tensor_scalar_mul(kdd[:], ksl,
                                                edcols[:, cidx:cidx + 1]),
                            [sd["kload"][gx], sd["M4"].get(t_ - 2)])
                        sd["M4"][t_] = P.add(
                            "pe", lambda t, ps2=ps2, kdd=kdd, vsl=vsl:
                            t.matmul(ps2, kdd[:], vsl, start=True, stop=True),
                            [sd["kdd"][t_]] + ent)
                        sd["Sstt"][t_] = P.add(
                            "dve", lambda v, ssl=ssl, cidx=cidx, ps2=ps2:
                            v.scalar_tensor_tensor(
                                ssl, ssl, cdcols[:, cidx:cidx + 1], ps2,
                                ALU.mult, ALU.add),
                            [sd["M4"][t_],
                             sd["sbc"].get(t_ - 32,
                                           sinit_dep if t_ < 32 else None)])
                        ysqd = P.add(
                            "act", lambda s, ps1=ps1, ps5q=ps5q, ssc=ssc:
                            s.activation(ps5q, ps1, AF.Square, accum_out=ssc),
                            [sd["M3b"][t_], sd["qslc"].get(t_ - 2)])
                        sd["ysq"][t_] = ysqd
                        lnd = P.add(
                            "act", lambda s, ssc=ssc, rsc=rsc: s.activation(
                                rsc, ssc, AF.Ln, scale=1.0 / DH,
                                bias=cbias[:, 2:3]),
                            [ysqd, sd["ynstt"].get(t_ - 4)])
                        sd["rexp"][t_] = P.add(
                            "act", lambda s, rsc=rsc: s.activation(
                                rsc, rsc, AF.Exp, scale=-0.5),
                            [lnd])
                        sd["sbc"][t_] = P.add(
                            "act", lambda s, sbl=sbl, ssl=ssl:
                            s.copy(sbl, ssl),
                            [sd["Sstt"][t_], sd["M3b"][t_]])
                        sd["ynstt"][t_] = P.add(
                            "dve", lambda v, ynr=ynr, ps1=ps1, rsc=rsc:
                            v.scalar_tensor_tensor(
                                ynr[:], ps1, rsc, gnwb[:], ALU.mult, ALU.mult),
                            [sd["rexp"][t_], sd["Tyn"].get(t_ - 2)])
                        sd["Tyn"][t_] = P.add(
                            "pe", lambda t, ps5q=ps5q, ynr=ynr: t.transpose(
                                ps5q, ynr[:], c_id32[:]),
                            [sd["ynstt"][t_], sd["ysq"][t_]])
                        sd["qslc"][t_] = P.add(
                            "dve", lambda v, qsl=qsl, ps5q=ps5q:
                            v.tensor_copy(qsl, ps5q),
                            [sd["Tyn"][t_]])
                    sd["M2e"][gx] = sd["M2"][gx * NH + NH - 1]
                    sd["M4e"][gx] = sd["M4"][gx * NH + NH - 1]
                    eloads(gx + 2)
                P.emit()
                scan_end = ("mdv", P.n["dve"])
                gp = Chain(X, scan_end, gate_done)
                for cblk in range(NKC):
                    rr = cblk % 2
                    gp.op("act", lambda s, cblk=cblk, rr=rr: s.dma_start(
                        out=kvring["g2"][:, rr * 512:rr * 512 + 512],
                        in_=gate_scr[cblk * 128:(cblk + 1) * 128, :]), dma=True)
                    gp.op("act", lambda s, rr=rr: s.activation(
                        gsl_sb[:], kvring["g2"][:, rr * 512:rr * 512 + 512],
                        AF.Silu))
                    gp.op("dve", lambda v, cblk=cblk: v.tensor_tensor(
                        x4[:, cblk * 512:(cblk + 1) * 512],
                        x4[:, cblk * 512:(cblk + 1) * 512], gsl_sb[:], ALU.mult))
                if "y" in tap_t and grp == 0:
                    cy = Chain(X, gp.dep())
                    cy.op("pool", lambda g: g.dma_start(
                        out=tap_t["y"], in_=x4[:, 0:NKC * 512]), dma=True)
                out_dep = emit_outproj("wo", 512, row0, gp.dep(),
                                       resid_b, resid_a, False)
            return out_dep

        def mark(nm2):
            PHASE_MARKS.append((nm2, nc.n_instructions()))

        PHASE_MARKS.clear()
        for nm, _M in WSPEC:
            p0b_enqueue(nm)
        p0b_mark_finals()
        p0b_pump(40)
        mark("P1")
        x1_dep = emit_p1()
        p0b_drain("w12")
        fin_src = None
        ab_dep = None
        if upto == "p1":
            fin_src = resid_a
        else:
            mark("h2")
            h2_dep = emit_ln_sweep(resid_a, h_scr_b, x1_dep,
                                   taph="h2" if "h2" in tap_t else None)
            p0b_drain("w3")
            op_dep = None
            mark("FFN1")
            for sl_i in range(nS):
                hfm_dep = emit_hfm_load(h_scr_b, sl_i * 512, h2_dep,
                                        free_dep=None if sl_i == 0 else
                                        ("mpe", X.val("mpe")))
                g1 = emit_gemm1(hfm_dep)
                p0b_pump(14)
                op_dep = emit_outproj("w3", 512, sl_i * 512, g1,
                                      resid_a, resid_b, True, extra_pump=8)
            if upto == "p2":
                fin_src = resid_b
            else:
                mark("h3")
                h3_dep = emit_ln_sweep(resid_b, h_scr_a, op_dep,
                                       tapx="x2" if "x2" in tap_t else None,
                                       taph="h3" if "h3" in tap_t else None)
                p0b_drain("wk")
                p0b_drain("wv")
                with ExitStack() as abes:
                    def absb(name, shape, dt_):
                        return abes.enter_context(
                            nc.sbuf_tensor(name, shape, dt_))
                    abt = {}
                    abt["kvring"] = {"k": absb("ab_k", [128, 4096], BF16),
                                     "v": absb("ab_v", [128, 4096], BF16),
                                     "kf": absb("ab_kf", [128, 1024], BF16),
                                     "g2": absb("ab_g2", [128, 1024], BF16)}
                    abt["cols"] = {"g": absb("ab_gc", [128, NH * 16], F32),
                                   "dt": absb("ab_dc", [128, NH * 16], F32),
                                   "ed": absb("ab_ec", [128, NH * 16], F32),
                                   "cd": absb("ab_cc", [128, NH * 16], F32)}
                    abt["ctm2"] = absb("ab_ctm", [128, 128], F32)
                    abt["stm2"] = absb("ab_stm", [128, 128], F32)
                    abt["cosq"] = gsl_sb
                    abt["sinq"] = lnsq
                    abt["dtg"] = {"a": sil[0:NH, 0:512],
                                  "b": sil[0:NH, 512:1024],
                                  "ones": absb("ab_dt1", [NH, 128], F32)}
                    silb = sil[:].bitcast(BF16)
                    abt["yn"] = [sil[:, 0:128], sil[:, 128:256]]
                    abt["gnwb"] = sil[:, 256:384]
                    abt["colb"] = [sil[:, 384:512], sil[:, 512:640]]
                    abt["sdd1"] = silb[:, 1280:1408]
                    abt["kdd1"] = silb[:, 1408:1536]
                    abt["qex1"] = silb[:, 1536:1664]
                    cinit = Chain(X, consts_v)
                    cinit.op("dve", lambda v: v.memset(abt["dtg"]["ones"][:],
                                                       1.0))
                    mark("ABp1")
                    sinit_dep = emit_attnb(abt, h3_dep)
                    p0b_drain("wq")
                    p0b_drain("wg")
                    p0b_drain("wo")
                    if upto == "p4a":
                        fin_src = resid_b
                    else:
                        mark("ABgrp")
                        ab_dep = emit_attnb_groups(abt, sinit_dep, h3_dep)
                        if upto == "p4":
                            fin_src = resid_a
                if fin_src is None:
                    mark("h4")
                    h4_dep = emit_ln_sweep(resid_a, h_scr_b, ab_dep,
                                           tapx="x3" if "x3" in tap_t else None)
                    mark("FFN2")
                    for sl_i in range(nS):
                        hfm_dep = emit_hfm_load(h_scr_b, sl_i * 512, h4_dep,
                                                free_dep=None if sl_i == 0 else
                                                ("mpe", X.val("mpe")))
                        g1 = emit_gemm1(hfm_dep)
                        emit_outproj("w3", 512, sl_i * 512, g1,
                                     resid_a, Y_out, True)

        mark("tail")
        p0b_pump(len(p0b_jobs))

        @block.gpsimd
        def _(g):
            if fin_src is not None:
                g.wait_ge(X.handle("qst"), X.val("qst"))
                g.wait_ge(X.handle("mdv"), X.val("mdv"))
                g.wait_ge(X.handle("mac"), X.val("mac"))
                for i in range(nT):
                    g.wait_ge(X.handle("qg"), X.val("qg"))
                    X.inc("qg", g.dma_start(
                        out=Y_out[i * 128:(i + 1) * 128, :],
                        in_=fin_src[i * 128:(i + 1) * 128, :]), 16)
            g.wait_ge(X.handle("qg"), X.val("qg"))
            g.wait_ge(X.handle("qst"), X.val("qst"))
            g.wait_ge(X.handle("qtp"), X.val("qtp"))

        @block.sync
        def _(sy):
            sy.wait_ge(X.handle("qs"), X.val("qs"))
    return nc



def _host_consts(T, half, rng):
    c = {}
    c["c_ones"] = np.ones((1, 128), np.float32)
    c["c_onescol"] = np.ones((128, 1), np.float32)
    c["c_id32"] = np.eye(128, dtype=np.float32)
    c["c_rev"] = np.eye(128, dtype=np.float32)[::-1].copy()
    i_idx = np.arange(128)
    c["c_maskneg"] = np.where(i_idx[None, :] >= i_idx[:, None], 0.0,
                              -1e30).astype(np.float32)
    prot = np.zeros((128, 128), np.float32)
    for dp in range(64):
        prot[dp, dp + 64] = -1.0
    for dp in range(64, 128):
        prot[dp, dp - 64] = 1.0
    c["c_prot"] = prot.T.copy()
    k = np.arange(4096).reshape(64, 64)
    c["c_omr"] = np.cos(2 * np.pi * k / 4096).astype(np.float32)
    c["c_omi"] = np.sin(2 * np.pi * k / 4096).astype(np.float32)
    kn = np.arange(64)
    F = np.exp(2j * np.pi * kn[:, None] * kn[None, :] / 64)
    c["c_fr"] = F.real.astype(np.float32)
    c["c_fi"] = F.imag.astype(np.float32)
    c["c_fni"] = (-F.imag).astype(np.float32)
    TW = np.exp(2j * np.pi * kn[:, None] * kn[None, :] / 4096)
    c["c_twr"] = TW.real.astype(np.float32)
    c["c_twi"] = TW.imag.astype(np.float32)
    pos0 = half * T
    invf = (10000.0 ** (-np.arange(0, 128, 2) / 128.0))
    pos = pos0 + np.arange(T)
    ang = pos[:, None] * invf[None, :]
    cos_t = np.concatenate([np.cos(ang), np.cos(ang)], 1).astype(np.float32)
    sin_t = np.concatenate([np.sin(ang), np.sin(ang)], 1).astype(np.float32)
    c["c_cosfm"] = cos_t.T.copy()
    c["c_sinfm"] = sin_t.T.copy()
    nT = T // 128
    ctm = np.zeros((128, T), np.float32)
    stm = np.zeros((128, T), np.float32)
    for gi in range(nT):
        ctm[:, gi * 128:(gi + 1) * 128] = cos_t[gi * 128:(gi + 1) * 128, :]
        stm[:, gi * 128:(gi + 1) * 128] = sin_t[gi * 128:(gi + 1) * 128, :]
    c["c_costm"] = ctm
    c["c_sintm"] = stm
    c["c_mask"] = np.full((128, 1), 1.0 if half == 1 else 0.0, np.float32)
    cb = np.zeros((128, 4), np.float32)
    cb[:, 0] = EPS_LN
    cb[:, 1] = 1.0
    cb[:, 2] = EPS_RMS
    c["c_bias"] = cb
    magic = int(rng.integers(1, 2 ** 31 - 1))
    c["c_magic"] = np.array([[magic, half, 0, 0]], np.int32)
    return c


def build_inmaps(inputs, T):
    rng = np.random.default_rng(abs(int(
        np.float64(np.asarray(inputs["L0_attn_a_D"],
                              np.float32).reshape(-1)[0]) * 1e9)) % (2 ** 31)
        + 7919)
    u = np.ascontiguousarray(np.asarray(inputs["L0_attn_a_u"], np.float32))
    maps = []
    for core in range(8):
        b, half = core // 2, core % 2
        m = {"x": np.ascontiguousarray(u[b, half * T:(half + 1) * T, :])}
        m["a_lam"] = np.ascontiguousarray(inputs["L0_attn_a_Lambda_ri"],
                                          np.float32)
        m["a_p"] = np.ascontiguousarray(inputs["L0_attn_a_p_ri"], np.float32)
        m["a_q"] = np.ascontiguousarray(inputs["L0_attn_a_q_ri"], np.float32)
        m["a_vc"] = np.ascontiguousarray(
            inputs["L0_attn_a_Vc_ri"], np.float32).reshape(64, 128)
        m["a_ct"] = np.ascontiguousarray(
            inputs["L0_attn_a_Ct_ri"], np.float32).reshape(64, 2)
        m["a_bp"] = np.ascontiguousarray(inputs["L0_attn_a_Bp"], np.float32)
        m["a_d"] = np.ascontiguousarray(inputs["L0_attn_a_D"],
                                        np.float32).reshape(1, 1)
        m["a_ls"] = np.ascontiguousarray(inputs["L0_attn_a_log_step"],
                                         np.float32).reshape(1, 1)
        m["wq"] = np.ascontiguousarray(inputs["L0_attn_b_Wq"], np.float32)
        m["wk"] = np.ascontiguousarray(inputs["L0_attn_b_Wk"], np.float32)
        m["wv"] = np.ascontiguousarray(inputs["L0_attn_b_Wv"], np.float32)
        m["wo"] = np.ascontiguousarray(inputs["L0_attn_b_Wo"], np.float32)
        m["wg"] = np.ascontiguousarray(inputs["L0_attn_b_Wg"], np.float32)
        m["win"] = np.ascontiguousarray(inputs["L0_attn_b_Win"], np.float32)
        m["dtb"] = np.ascontiguousarray(inputs["L0_attn_b_dt_bias"],
                                        np.float32).reshape(NH, 1)
        m["alb"] = np.ascontiguousarray(inputs["L0_attn_b_A_log_bias"],
                                        np.float32).reshape(NH, 1)
        m["gnw"] = np.ascontiguousarray(inputs["L0_attn_b_gnorm_w"],
                                        np.float32).reshape(DH, 1)
        m["w12"] = np.ascontiguousarray(inputs["L0_ffn_w12"], np.float32)
        m["b12"] = np.ascontiguousarray(
            inputs["L0_ffn_b12"], np.float32).reshape(64, 128).T.copy()
        m["w3"] = np.ascontiguousarray(inputs["L0_ffn_w3"], np.float32)
        m["b3"] = np.ascontiguousarray(inputs["L0_ffn_b3"],
                                       np.float32).reshape(1, DM)
        m.update(_host_consts(T, half, rng))
        maps.append(m)
    return maps


def kernel(**inputs):
    T = 2048
    nc = build(T=T, upto="all")
    maps = build_inmaps(inputs, T)
    res = run_bass_kernel_spmd(nc, maps, core_ids=list(range(8))).results
    out = np.empty((4, 2 * T, DM), np.float32)
    for core in range(8):
        b, half = core // 2, core % 2
        out[b, half * T:(half + 1) * T, :] = res[core]["y"]
    return out
